# version 06
# baseline (speedup 1.0000x reference)
import contextlib
import numpy as np
import concourse.bass as bass
import concourse.mybir as mybir
from concourse.bass_utils import run_bass_kernel_spmd

F32 = mybir.dt.float32
BF16 = mybir.dt.bfloat16
I32 = mybir.dt.int32
AF = mybir.ActivationFunctionType
OP = mybir.AluOpType
AX = mybir.AxisListType

D = 1024
T = 4096
NCORES = 8


ENGINES = ("sp", "pe", "act", "dve", "pool")
_ENG_ATTR = {"sp": "sync", "pe": "tensor", "act": "scalar", "dve": "vector",
             "pool": "gpsimd"}


class _Op:
    __slots__ = ("idx", "eng", "fn", "reads", "writes", "dma", "deps", "tag",
                 "marked", "cnt_after")

    def __init__(self, idx, eng, fn, reads, writes, dma, tag):
        self.idx = idx
        self.eng = eng
        self.fn = fn
        self.reads = reads
        self.writes = writes
        self.dma = dma
        self.deps = set()
        self.tag = tag
        self.marked = dma
        self.cnt_after = 0


class Prog:

    def __init__(self, nc):
        self.nc = nc
        self.ops = []
        self.state = {}

    def _match(self, name, key):
        st = self.state.get(name)
        if st is None:
            return []
        if key is None:
            return list(st.values())
        out = []
        if key in st:
            out.append(st[key])
        if None in st:
            out.append(st[None])
        return out

    def _record_read(self, name, key, idx):
        st = self.state.setdefault(name, {})
        ent = st.get(key)
        if ent is None:
            ent = st[key] = [None, set()]
        ent[1].add(idx)

    def _record_write(self, name, key, idx):
        st = self.state.setdefault(name, {})
        if key is None:
            st.clear()
        st[key] = [idx, set()]

    def emit(self, eng, fn, reads=(), writes=(), dma=False, tag=""):
        idx = len(self.ops)
        op = _Op(idx, eng, fn, list(reads), list(writes), dma, tag)
        deps = op.deps
        for r in op.reads:
            n, k = r if isinstance(r, tuple) else (r, None)
            for ent in self._match(n, k):
                if ent[0] is not None:
                    deps.add(ent[0])
        for w in op.writes:
            n, k = w if isinstance(w, tuple) else (w, None)
            for ent in self._match(n, k):
                if ent[0] is not None:
                    deps.add(ent[0])
                deps.update(ent[1])
        deps.discard(idx)
        for r in op.reads:
            n, k = r if isinstance(r, tuple) else (r, None)
            self._record_read(n, k, idx)
        for w in op.writes:
            n, k = w if isinstance(w, tuple) else (w, None)
            self._record_write(n, k, idx)
        self.ops.append(op)
        return idx

    def barrier(self):
        idx = len(self.ops)
        op = _Op(idx, "__barrier__", None, [], [], False, "barrier")
        self.ops.append(op)
        return idx

    def lower(self):
        nc = self.nc
        ops = self.ops
        for op in ops:
            if op.eng == "__barrier__":
                continue
            for d in op.deps:
                dop = ops[d]
                if dop.eng == op.eng == "pe":
                    continue
                dop.marked = True
        barrier_info = {}
        last_per_eng = {e: None for e in ENGINES}
        for op in ops:
            if op.eng == "__barrier__":
                barrier_info[op.idx] = dict(last_per_eng)
                for e in ENGINES:
                    if last_per_eng[e] is not None:
                        ops[last_per_eng[e]].marked = True
            else:
                last_per_eng[op.eng] = op.idx
        comp_cnt = {e: 0 for e in ENGINES}
        dma_cnt = {e: 0 for e in ENGINES}
        for op in ops:
            if op.eng == "__barrier__":
                continue
            if op.dma:
                dma_cnt[op.eng] += 16
                op.cnt_after = dma_cnt[op.eng]
            elif op.marked:
                comp_cnt[op.eng] += 1
                op.cnt_after = comp_cnt[op.eng]
        self._final_dma_counts = dict(dma_cnt)
        self._final_comp_counts = dict(comp_cnt)

        self._stack = contextlib.ExitStack()
        sems = {}
        for e in ENGINES:
            sems[(e, "c")] = self._stack.enter_context(nc.semaphore(f"sem_{e}_c"))
            sems[(e, "d")] = self._stack.enter_context(nc.semaphore(f"sem_{e}_d"))
        self.sems = sems

        op_clock = [None] * len(ops)
        eng_clock = {e: {} for e in ENGINES}

        def merge(dst, src):
            for k, v in src.items():
                if dst.get(k, 0) < v:
                    dst[k] = v

        streams = {e: [] for e in ENGINES}
        for op in ops:
            if op.eng == "__barrier__":
                info = barrier_info[op.idx]
                for e in ENGINES:
                    clk = eng_clock[e]
                    for e2 in ENGINES:
                        if e2 == e or info[e2] is None:
                            continue
                        dop = ops[info[e2]]
                        sk = (e2, "d" if dop.dma else "c")
                        v = dop.cnt_after
                        if clk.get(sk, 0) < v:
                            streams[e].append(("wait", sk, v))
                            clk[sk] = v
                            if op_clock[dop.idx]:
                                merge(clk, op_clock[dop.idx])
                continue
            e = op.eng
            clk = eng_clock[e]
            needed = {}
            for d in sorted(op.deps):
                dop = ops[d]
                if dop.eng == e and dop.eng == "pe" and e == "pe" and not dop.dma:
                    continue
                sk = (dop.eng, "d" if dop.dma else "c")
                v = dop.cnt_after
                if v == 0:
                    continue
                if needed.get(sk, 0) < v:
                    needed[sk] = v
                    needed[(sk, "src")] = dop.idx
            for sk in list(needed):
                if isinstance(sk, tuple) and len(sk) == 2 and sk[1] in ("c", "d"):
                    v = needed[sk]
                    if clk.get(sk, 0) >= v:
                        continue
                    streams[e].append(("wait", sk, v))
                    clk[sk] = v
                    src = needed.get((sk, "src"))
                    if src is not None and op_clock[src]:
                        merge(clk, op_clock[src])
            streams[e].append(("op", op))
            if op.marked:
                snap = dict(clk)
                snap[(e, "d" if op.dma else "c")] = op.cnt_after
                op_clock[op.idx] = snap

        self._streams = streams

    def build_block(self):
        nc = self.nc
        sems = self.sems
        streams = self._streams
        block = self._stack.enter_context(nc.Block())

        def run_stream(eng_name):
            stream = streams[eng_name]

            def body(engobj):
                for item in stream:
                    if item[0] == "wait":
                        _, sk, v = item
                        engobj.wait_ge(sems[sk], v)
                    else:
                        op = item[1]
                        inst = op.fn(engobj)
                        if op.dma:
                            inst.then_inc(sems[(op.eng, "d")], 16)
                        elif op.marked:
                            inst.then_inc(sems[(op.eng, "c")], 1)
            return body

        for e in ENGINES:
            if not streams[e]:
                continue
            getattr(block, _ENG_ATTR[e])(run_stream(e))

    def finish(self):
        self.lower()
        self.build_block()
        self._stack.close()



EPS = 1e-5


class B:

    def __init__(self, nc):
        self.nc = nc
        self.p = Prog(nc)
        self.stack = contextlib.ExitStack()
        self.scopes = []
        self.dram = {}
        self._uid = 0
        self.pb = [self.stack.enter_context(nc.psum_tensor(f"pb{i}", [128, 512], F32))
                   for i in range(7)]
        self.pb_res = [f"pb{i}" for i in range(7)]
        self.pbt = self.stack.enter_context(nc.psum_tensor("pbt", [128, 1024], BF16))
        self.pbt_res = "pbt"
        self._pb_rr = 0
        self.input_names = set()
        self.const_buf = self.stack.enter_context(
            nc.sbuf_tensor("constbuf", [128, 16], F32))
        self._consts = {}

    def uid(self):
        self._uid += 1
        return self._uid

    def dram_in(self, name, shape, dtype=F32):
        if name not in self.dram:
            self.dram[name] = self.nc.dram_tensor(name, shape, dtype,
                                                  kind="ExternalInput").ap()
            self.input_names.add(name)
        return self.dram[name]

    def dram_out(self, name, shape, dtype=F32):
        if name not in self.dram:
            self.dram[name] = self.nc.dram_tensor(name, shape, dtype,
                                                  kind="ExternalOutput").ap()
        return self.dram[name]

    def dram_tmp(self, name, shape, dtype=F32):
        if name not in self.dram:
            self.dram[name] = self.nc.dram_tensor(name, shape, dtype).ap()
        return self.dram[name]

    def sb(self, name, shape, dtype):
        st = self.scopes[-1] if self.scopes else self.stack
        name = f"{name}_{self.uid()}"
        h = st.enter_context(self.nc.sbuf_tensor(name, shape, dtype))
        return h

    @contextlib.contextmanager
    def scope(self):
        st = contextlib.ExitStack()
        self.scopes.append(st)
        try:
            yield st
        finally:
            self.scopes.pop()
            st.close()
            self.p.barrier()

    def constf(self, v, p=128):
        v = float(v)
        if v not in self._consts:
            i = len(self._consts)
            assert i < 16
            self._consts[v] = i
            cb = self.const_buf
            self.p.emit("pool", lambda e: e.memset(cb[:, i:i + 1], v),
                        writes=[("c_const", i)], tag="const")
        i = self._consts[v]
        return self.const_buf[0:p, i:i + 1]

    def bank(self):
        i = self._pb_rr
        self._pb_rr = (self._pb_rr + 1) % 4
        return self.pb[i], self.pb_res[i]

    def rn(self, h):
        return h.name

    def dma(self, eng, dst, src, reads=(), writes=(), tag="", slow=False):
        if slow:
            self.p.emit(eng,
                        lambda e: e.dma_start(dst, src,
                                              allow_slow_non_contiguous=True),
                        reads=reads, writes=writes, dma=True, tag=tag)
        else:
            self.p.emit(eng, lambda e: e.dma_start(dst, src),
                        reads=reads, writes=writes, dma=True, tag=tag)

    def mm_group(self, psum_ap, pairs, psum_res, extra_reads=(), tag="mm",
                 accumulate=False):
        n = len(pairs)
        for i, (lt, rh, rds) in enumerate(pairs):
            st = (i == 0) and not accumulate
            sp = (i == n - 1)
            self.p.emit("pe",
                        lambda e, lt=lt, rh=rh, st=st, sp=sp:
                        e.matmul(psum_ap, lt, rh, start=st, stop=sp),
                        reads=list(rds) + list(extra_reads),
                        writes=[psum_res], tag=tag)


def bf16_transpose(e, nc, out, in_, ident):
    with nc.allow_low_precision(reason="bit-exact bf16 transpose via PE"):
        return e.transpose(out, in_, ident)


def lp(nc, fn):
    with nc.allow_low_precision(reason="bf16 operand rounding (honest bf16)"):
        return fn()


def ap3(handle_or_ap, pstep_count, mid, inner, offset=0):
    t = handle_or_ap
    return bass.AP(t, offset, [list(pstep_count), list(mid), list(inner)])



def _bf(x):
    import ml_dtypes
    return np.ascontiguousarray(np.asarray(x, dtype=np.float32).astype(ml_dtypes.bfloat16))


def _t(x):
    return np.ascontiguousarray(np.asarray(x).T)


def _tile_major(v):
    v = np.asarray(v)
    return np.ascontiguousarray(v.reshape(-1, 128).T)


def prep_shared(inputs):
    import ml_dtypes
    g = {}
    g["c_eye"] = np.eye(128, dtype=ml_dtypes.bfloat16)
    g["c_ones"] = np.ones((128, 1), dtype=ml_dtypes.bfloat16)

    g["L0f_wg"] = np.ascontiguousarray(np.asarray(inputs["L0_ffn_w_gate"],
                                                  dtype=np.float32))
    g["L0f_wu"] = np.ascontiguousarray(np.asarray(inputs["L0_ffn_w_up"],
                                                  dtype=np.float32))
    g["L0f_wd"] = np.ascontiguousarray(np.asarray(inputs["L0_ffn_w_down"],
                                                  dtype=np.float32))

    g["L1a_WqT"] = _t(inputs["L1_attn_Wq"])
    g["L1a_WkT"] = _t(inputs["L1_attn_Wk"])
    g["L1a_WvT"] = _t(inputs["L1_attn_Wv"])
    g["L1a_WoT"] = _t(inputs["L1_attn_Wo"])

    wq3 = np.asarray(inputs["L3_attn_Wq"]).transpose(1, 0, 2).reshape(1024, 1024)
    wk3 = np.asarray(inputs["L3_attn_Wk"]).transpose(1, 0, 2).reshape(1024, 1024)
    wv3 = np.asarray(inputs["L3_attn_Wv"]).transpose(1, 0, 2).reshape(1024, 1024)
    g["L3a_WqT"] = np.ascontiguousarray(wq3)
    g["L3a_WkT"] = np.ascontiguousarray(wk3)
    g["L3a_WvT"] = np.ascontiguousarray(wv3)
    g["L3a_WoT"] = _t(inputs["L3_attn_Wo"])
    g["L3a_bias"] = np.concatenate(
        [_tile_major(np.asarray(inputs["L3_attn_bq"]).reshape(1024)),
         _tile_major(np.asarray(inputs["L3_attn_bk"]).reshape(1024)),
         _tile_major(np.asarray(inputs["L3_attn_bv"]).reshape(1024)),
         _tile_major(np.asarray(inputs["L3_attn_bo"]))], axis=1)
    mk4 = np.zeros((128, 4 * 512), np.float32)
    for dv in range(4):
        for i in range(128):
            mk4[i, dv * 512:(dv + 1) * 512] = (np.arange(512) >= (dv * 128 + i))
    g["c_l3mask"] = _bf(mk4)

    wg_all = np.stack([_t(inputs["L1_ffn_Wg_s"])] +
                      [_t(np.asarray(inputs["L1_ffn_Wg"])[e]) for e in range(8)])
    wu_all = np.stack([_t(inputs["L1_ffn_Wu_s"])] +
                      [_t(np.asarray(inputs["L1_ffn_Wu"])[e]) for e in range(8)])
    wd_all = np.stack([_t(inputs["L1_ffn_Wd_s"])] +
                      [_t(np.asarray(inputs["L1_ffn_Wd"])[e]) for e in range(8)])
    g["L1f_WgT"] = np.ascontiguousarray(wg_all.reshape(9 * 1024, 2048))
    g["L1f_WuT"] = np.ascontiguousarray(wu_all.reshape(9 * 1024, 2048))
    g["L1f_WdT"] = np.ascontiguousarray(wd_all.reshape(9 * 2048, 1024))
    g["L1f_WrT"] = _t(inputs["L1_ffn_Wr"])
    g["L1f_rb"] = np.ascontiguousarray(np.asarray(inputs["L1_ffn_rb"],
                                                  dtype=np.float32).reshape(8, 1))
    eind = np.zeros((8, 8 * 128), np.float32)
    for e in range(8):
        eind[e, e * 128:(e + 1) * 128] = 1.0
    g["c_einds"] = _bf(eind)
    g["c_eyef"] = np.eye(128, dtype=np.float32)

    g["L3f_w1T"] = _t(inputs["L3_ffn_w1_w"])
    g["L3f_w2T"] = _t(inputs["L3_ffn_w2_w"])
    g["L3f_h1T"] = _t(inputs["L3_ffn_h1_w"])
    g["L3f_h2T"] = _t(inputs["L3_ffn_h2_w"])
    g["L3f_w3T"] = _t(inputs["L3_ffn_w3_w"])
    g["L3f_sc"] = np.concatenate([_tile_major(inputs["L3_ffn_w1_s"]),
                                  _tile_major(inputs["L3_ffn_w2_s"]),
                                  _tile_major(inputs["L3_ffn_h1_s"]),
                                  _tile_major(inputs["L3_ffn_h2_s"])], axis=1)
    g["L3f_s3"] = _tile_major(inputs["L3_ffn_w3_s"])

    g["L0a_WaT"] = _t(inputs["L0_attn_Wa_w"])
    g["L0a_Wab"] = _tile_major(inputs["L0_attn_Wa_b"])
    g["L0a_BT"] = _t(inputs["L0_attn_B_w"])
    g["L0a_CT"] = _t(inputs["L0_attn_C_w"])
    g["L0a_DT"] = _t(inputs["L0_attn_D_w"])

    g["L2a_WgT"] = _t(inputs["L2_attn_Wg_w"])
    g["L2a_WrT"] = _t(inputs["L2_attn_Wr_w"])
    g["L2a_WkT"] = _t(inputs["L2_attn_Wk_w"])
    g["L2a_WvT"] = _t(inputs["L2_attn_Wv_w"])
    g["L2a_WoT"] = _t(inputs["L2_attn_Wo_w"])
    vec_list = [inputs["L2_attn_Wg_b"], inputs["L2_attn_Wr_b"],
                inputs["L2_attn_Wk_b"], inputs["L2_attn_Wv_b"],
                inputs["L2_attn_Wo_b"], inputs["L2_attn_mg"],
                inputs["L2_attn_mr"], inputs["L2_attn_mk"],
                inputs["L2_attn_mv"],
                np.asarray(inputs["L2_attn_w"]).reshape(1024),
                np.asarray(inputs["L2_attn_u"]).reshape(1024)]
    g["L2a_vecs"] = np.concatenate([_tile_major(v) for v in vec_list], axis=1)
    feats = np.arange(1024)
    heads = feats // 64
    ind = np.zeros((1024, 16), np.float32)
    ind[feats, heads] = 1.0
    indh = np.zeros((128, 128), np.float32)
    for i in range(8):
        indh[:, i * 16:(i + 1) * 16] = ind[i * 128:(i + 1) * 128, :]
    g["c_ind"] = _bf(indh)
    indT = np.zeros((16, 1024), np.float32)
    for i in range(8):
        indT[:, i * 128:(i + 1) * 128] = ind[i * 128:(i + 1) * 128, :].T
    g["c_indT"] = _bf(indT)

    g["L2f_WrT"] = _t(inputs["L2_ffn_W_r"])
    g["L2f_WkT"] = _t(inputs["L2_ffn_W_k"])
    g["L2f_WvT"] = _t(inputs["L2_ffn_W_v"])
    g["L2f_mur"] = _tile_major(inputs["L2_ffn_mu_r"])
    g["L2f_muk"] = _tile_major(inputs["L2_ffn_mu_k"])
    return g



def emit_ln(b, rx, rx_res, xn_ap, xn_res, W, scr, ones, rx_fw=None,
            rx_stride=None, rx_off=0):
    p = b.p
    if rx_fw is None:
        rx_fw = 8 * W
    if rx_stride is None:
        rx_stride = W
    rx3 = ap3(rx, (rx_fw, 128), (rx_stride, 8), (1, W), offset=rx_off)
    rxbf, rsq, rows, rows_bf = scr["rxbf"], scr["rsq"], scr["rows"], scr["rows_bf"]
    xc = scr["ln_xc"]
    pS1, pS2, pAB = scr["pS1"], scr["pS2"], scr["pAB"]
    onesrow = scr["onesrow"]
    rxbf_r, rsq_r = b.rn(rxbf), b.rn(rsq)
    xc_r = b.rn(xc)
    rows_r, rowsbf_r = b.rn(rows), b.rn(rows_bf)
    pS1_r, pS2_r, pAB_r = scr["pS1_r"], scr["pS2_r"], scr["pAB_r"]

    def R(i):
        return rows[0:1, i * W:(i + 1) * W]

    def RB(i):
        return rows_bf[0:1, i * W:(i + 1) * W]

    rxbf3 = ap3(rxbf, (8 * W, 128), (W, 8), (1, W))
    rsq3 = ap3(rsq, (8 * W, 128), (W, 8), (1, W))
    xc3 = ap3(xc, (8 * W, 128), (W, 8), (1, W))
    p.emit("act", lambda e: e.copy(rxbf3, rx3), reads=[rx_res],
           writes=[rxbf_r], tag="ln_cp")
    pairs1 = [(ones[:, :], bass.AP(rxbf, k * W, [[8 * W, 128], [1, W]]),
               [rxbf_r]) for k in range(8)]
    b.mm_group(pS1[0:1, 0:W], pairs1, pS1_r, extra_reads=["c_ones_s"], tag="ln_s1")
    p.emit("dve", lambda e: e.tensor_scalar(RB(0), pS1[0:1, 0:W],
                                            1.0 / D, None, OP.mult),
           reads=[pS1_r], writes=[(rowsbf_r, 0)], tag="ln_m0")
    p.emit("pe", lambda e: e.matmul(pAB[:, 0:W], onesrow[0:1, :], RB(0),
                                    start=True, stop=True),
           reads=[(rowsbf_r, 0), "c_onesrow"], writes=[pAB_r], tag="ln_mb")
    mb3 = ap3(pAB, (512, 128), (0, 8), (1, W))
    p.emit("dve", lambda e: e.tensor_tensor(xc3, rx3, mb3, OP.subtract),
           reads=[rx_res, pAB_r], writes=[xc_r], tag="ln_ctr")
    p.emit("act", lambda e: e.copy(rxbf3, xc3), reads=[xc_r],
           writes=[rxbf_r], tag="ln_cp2")
    p.emit("act", lambda e: e.activation(rsq3, xc3, AF.Square),
           reads=[xc_r], writes=[rsq_r], tag="ln_sq")
    pairs1b = [(ones[:, :], bass.AP(rxbf, k * W, [[8 * W, 128], [1, W]]),
                [rxbf_r]) for k in range(8)]
    pairs2 = [(ones[:, :], bass.AP(rsq, k * W, [[8 * W, 128], [1, W]]),
               [rsq_r]) for k in range(8)]
    b.mm_group(pS1[0:1, 0:W], pairs1b, pS1_r, extra_reads=["c_ones_s"],
               tag="ln_s1b")
    b.mm_group(pS2[0:1, 0:W], pairs2, pS2_r, extra_reads=["c_ones_s"],
               tag="ln_s2")
    p.emit("dve", lambda e: e.tensor_scalar(R(0), pS1[0:1, 0:W],
                                            1.0 / D, None, OP.mult),
           reads=[pS1_r], writes=[(rows_r, 0)], tag="ln_dm")
    p.emit("dve", lambda e: e.tensor_scalar(R(1), pS2[0:1, 0:W],
                                            1.0 / D, None, OP.mult),
           reads=[pS2_r], writes=[(rows_r, 1)], tag="ln_ex2")
    p.emit("dve", lambda e: e.tensor_tensor(R(2), R(0), R(0), OP.mult),
           reads=[(rows_r, 0)], writes=[(rows_r, 2)], tag="ln_msq")
    p.emit("dve", lambda e: e.tensor_tensor(R(1), R(1), R(2), OP.subtract),
           reads=[(rows_r, 1), (rows_r, 2)], writes=[(rows_r, 1)], tag="ln_v")
    p.emit("dve", lambda e: e.tensor_scalar(R(1), R(1), 0.0, None, OP.max),
           reads=[(rows_r, 1)], writes=[(rows_r, 1)], tag="ln_v0")
    epsap = b.constf(EPS, p=1)
    p.emit("act", lambda e: e.activation(R(3), R(1), AF.Sqrt, bias=epsap),
           reads=[(rows_r, 1), ("c_const", b._consts[EPS])],
           writes=[(rows_r, 3)], tag="ln_sqrt")
    p.emit("dve", lambda e: e.reciprocal(R(2), R(3)),
           reads=[(rows_r, 3)], writes=[(rows_r, 2)], tag="ln_rstd")
    p.emit("dve", lambda e: e.tensor_copy(RB(0), R(2)),
           reads=[(rows_r, 2)], writes=[(rowsbf_r, 0)], tag="ln_bf0")
    p.emit("dve", lambda e: e.scalar_tensor_tensor(RB(1), R(0), -1.0, R(2),
                                                   OP.mult, OP.mult),
           reads=[(rows_r, 0), (rows_r, 2)], writes=[(rowsbf_r, 1)], tag="ln_bf1")
    p.emit("pe", lambda e: e.matmul(pAB[:, 0:W], onesrow[0:1, :], RB(0),
                                    start=True, stop=True),
           reads=[(rowsbf_r, 0), "c_onesrow"], writes=[pAB_r], tag="ln_bA")
    p.emit("pe", lambda e: e.matmul(pAB[:, W:2 * W], onesrow[0:1, :], RB(1),
                                    start=True, stop=True),
           reads=[(rowsbf_r, 1), "c_onesrow"], writes=[pAB_r], tag="ln_bB")
    if "ln_tmp32" in scr:
        tmp = scr["ln_tmp32"]
        src3 = xc3
        src_r = xc_r
    else:
        tmp = scr["ln_tmp"]
        src3 = ap3(rxbf, (8 * W, 128), (W, 8), (1, W))
        src_r = rxbf_r
    tmp_r = b.rn(tmp)
    Aap = ap3(pAB, (512, 128), (0, 8), (1, W))
    Bap = ap3(pAB, (512, 128), (0, 8), (1, W), offset=W)
    tmp3 = ap3(tmp, (8 * W, 128), (W, 8), (1, W))
    p.emit("dve", lambda e: e.tensor_tensor(tmp3, src3, Aap, OP.mult),
           reads=[src_r, pAB_r], writes=[tmp_r], tag="ln_ap1")
    p.emit("dve", lambda e: e.tensor_tensor(xn_ap, tmp3, Bap, OP.add),
           reads=[tmp_r, pAB_r], writes=[xn_res], tag="ln_ap2")



def resid_chunk_ap(dram_ap, c, W):
    return bass.AP(dram_ap.tensor, c * W, [[T, 128], [128 * T, 8], [1, W]])


def load_w2mb(b, dram_name, buf, res, kdim=8, mdim=1024, eng="pool"):
    src = bass.AP(b.dram[dram_name].tensor, 0,
                  [[mdim, 128], [128 * mdim, kdim], [1, mdim]])
    b.dma(eng, buf[:, :], src, writes=[res], tag=f"w_{dram_name}")


def emit_mix3(b, cur, sh, d3, delta_res, mu_bc, mu_res, out3, out_res, deps):
    p = b.p
    p.emit("dve", lambda e, d3=d3, cur=cur, sh=sh:
           e.tensor_tensor(d3, cur, sh, OP.subtract),
           reads=list(deps), writes=[delta_res], tag="mix_d")
    p.emit("dve", lambda e, out3=out3, d3=d3, mu_bc=mu_bc:
           e.tensor_tensor(out3, d3, mu_bc, OP.mult),
           reads=[delta_res, mu_res], writes=[out_res], tag="mix_m")
    p.emit("dve", lambda e, out3=out3, sh=sh:
           e.tensor_tensor(out3, out3, sh, OP.add),
           reads=[out_res] + list(deps), writes=[out_res], tag="mix_a")


def proj_group(b, w_sb, w_res, rhs_sb, rhs_res, m, W, nk=8, rhs_stride=None,
               wm=1024, psum=None, rhs_off=0, rhs_fw=None):
    if rhs_stride is None:
        rhs_stride = W
    if rhs_fw is None:
        rhs_fw = nk * rhs_stride
    if psum is None:
        pbk, pbk_r = b.bank()
    else:
        pbk, pbk_r = psum
    pairs = [(bass.AP(w_sb, k * wm + m * 128, [[nk * wm, 128], [1, 128]]),
              bass.AP(rhs_sb, k * rhs_stride + rhs_off, [[rhs_fw, 128], [1, W]]),
              [w_res, rhs_res]) for k in range(nk)]
    b.mm_group(pbk[:, 0:W], pairs, pbk_r, tag="proj")
    return pbk, pbk_r


def expand_cols(b, src, src_off, src_w, src_res, dst, dst_res, n, W):
    b.p.emit("dve", lambda e: e.tensor_copy(
        ap3(dst, (n * W, 128), (W, n), (1, W)),
        ap3(src, (src_w, 128), (1, n), (0, W), offset=src_off)),
        reads=[src_res], writes=[dst_res], tag="expand")


def make_ln_scr(b, W, ones=None, onesrow=None, xc_buf=None):
    if onesrow is None:
        onesrow = b.sb("onesrow", [1, 128], BF16)
        b.p.emit("pool", lambda e: e.memset(onesrow[:, :], 1.0),
                 writes=["c_onesrow"])
    if ones is None:
        ones = b.sb("ones", [128, 1], BF16)
        b.p.emit("pool", lambda e: e.memset(ones[:, :], 1.0),
                 writes=["c_ones_s"])
    return {
        "rxbf": b.sb("rxbf", [128, 8 * W], BF16),
        "rsq": b.sb("rsq", [128, 8 * W], BF16),
        "ln_xc": xc_buf if xc_buf is not None else b.sb("ln_xc", [128, 8 * W], F32),
        "rows": b.sb("rows", [1, 4 * W], F32),
        "rows_bf": b.sb("rows_bf", [1, 2 * W], BF16),
        "ln_tmp": b.sb("ln_tmp", [128, 8 * W], BF16),
        "pS1": b.pb[4], "pS2": b.pb[5], "pAB": b.pb[6],
        "pS1_r": b.pb_res[4], "pS2_r": b.pb_res[5], "pAB_r": b.pb_res[6],
        "onesrow": onesrow, "ones": ones,
    }



def emit_l2attn(b, resid_in, rin_res, resid_out, rout_res):
    p = b.p
    W = 256
    NCH = T // W
    xns = b.dram_tmp("xn_scr", [D, T], BF16)
    with b.scope():
        wg = b.sb("l2a_wg", [128, 8 * 1024], BF16)
        wrr = b.sb("l2a_wr", [128, 8 * 1024], BF16)
        wk = b.sb("l2a_wk", [128, 8 * 1024], BF16)
        wv = b.sb("l2a_wv", [128, 8 * 1024], BF16)
        wo = b.sb("l2a_wo", [128, 8 * 1024], BF16)
        wbufs = {"g": (wg, b.rn(wg)), "r": (wrr, b.rn(wrr)), "k": (wk, b.rn(wk)),
                 "v": (wv, b.rn(wv)), "o": (wo, b.rn(wo))}
        for nm, key in (("L2a_WgT", "g"), ("L2a_WrT", "r"), ("L2a_WkT", "k"),
                        ("L2a_WvT", "v"), ("L2a_WoT", "o")):
            load_w2mb(b, nm, wbufs[key][0], wbufs[key][1])
        vecs = b.sb("l2a_vecs", [128, 8 * 11], F32)
        b.dma("sp", vecs[:, :], b.dram["L2a_vecs"], writes=["l2a_vecs"])
        def V(i):
            return vecs[:, i * 8:(i + 1) * 8]
        BG, BR, BK, BV, BO, MG, MR, MK, MV, WW, UU = range(11)
        dec = b.sb("l2a_dec", [128, 8], F32)
        p.emit("act", lambda e: e.activation(dec[:, :], V(WW), AF.Exp),
               reads=["l2a_vecs"], writes=["l2a_dec"])
        p.emit("act", lambda e: e.activation(dec[:, :], dec[:, :], AF.Exp,
                                             scale=-1.0),
               reads=["l2a_dec"], writes=["l2a_dec"])
        ind = b.sb("l2a_ind", [128, 8 * 16], BF16)
        b.dma("sp", ind[:, :], b.dram["c_ind"], writes=["c_ind_s"])
        indT = b.sb("l2a_indT", [16, 8 * 128], BF16)
        b.dma("sp", indT[:, :], b.dram["c_indT"], writes=["c_indT_s"])

        scr = make_ln_scr(b, W)
        ones = scr["ones"]

        rx = [b.sb("rx", [128, 8 * W], F32) for _ in range(2)]
        xnc = [b.sb("xnc", [128, 8 * W], BF16) for _ in range(2)]
        kvs = b.dram_tmp("kv_scr", [D, T], BF16)

        for c in range(NCH):
            rxc = rx[c % 2]
            xc = xnc[c % 2]
            rxc_r, xc_r = b.rn(rxc), b.rn(xc)
            b.dma("sp", rxc[:, :], resid_chunk_ap(resid_in, c, W),
                  reads=[(rin_res, c)], writes=[rxc_r], tag=f"l2a_rx{c}")
            xn_ap = ap3(xc, (8 * W, 128), (W, 8), (1, W))
            emit_ln(b, rxc, rxc_r, xn_ap, xc_r, W, scr, ones)
            b.dma("sp", resid_chunk_ap(xns, c, W), xc[:, :],
                  reads=[xc_r], writes=[("xn_scr", c)], tag=f"l2a_xn{c}")

        xb = [b.sb("xb", [128, 8 * (W + 1)], BF16) for _ in range(2)]
        delta = b.sb("delta", [128, 8 * W], BF16)
        mix_t = b.sb("mix_t", [128, 8 * W], BF16)
        mix_t2 = b.sb("mix_t2", [128, 8 * W], BF16)
        kk = b.sb("kk", [128, 8 * W], BF16)
        vv = b.sb("vv", [128, 8 * W], BF16)
        kvc = b.sb("kvc", [128, 8 * W], BF16)
        mu_bc = b.sb("mu_bc", [128, 8 * W], BF16)
        mu_bc2 = b.sb("mu_bc2", [128, 8 * W], BF16)
        expand_cols(b, vecs, MK * 8, 88, "l2a_vecs", mu_bc, "mu_bck", 8, W)
        expand_cols(b, vecs, MV * 8, 88, "l2a_vecs", mu_bc2, "mu_bcv", 8, W)
        for c in range(NCH):
            xc = xb[c % 2]
            xc_r = b.rn(xc)
            if c == 0:
                src0 = bass.AP(xns.tensor, 0, [[T, 128], [128 * T, 8], [1, W]])
                b.dma("sp", ap3(xc, (8 * (W + 1), 128), (W + 1, 8), (1, W), offset=1),
                      src0, reads=[("xn_scr", 0)], writes=[(xc_r, "m")],
                      tag="l2a_xb0")
                srcw = bass.AP(xns.tensor, T - 1, [[T, 128], [128 * T, 8], [1, 1]])
                b.dma("sp", ap3(xc, (8 * (W + 1), 128), (W + 1, 8), (1, 1)),
                      srcw, reads=[("xn_scr", NCH - 1)], writes=[(xc_r, "b")],
                      tag="l2a_xbw", slow=True)
            else:
                src = bass.AP(xns.tensor, c * W - 1,
                              [[T, 128], [128 * T, 8], [1, W + 1]])
                b.dma("sp", xc[:, :], src,
                      reads=[("xn_scr", c), ("xn_scr", c - 1)],
                      writes=[(xc_r, "m"), (xc_r, "b")], tag=f"l2a_xb{c}")
            cur = ap3(xc, (8 * (W + 1), 128), (W + 1, 8), (1, W), offset=1)
            sh = ap3(xc, (8 * (W + 1), 128), (W + 1, 8), (1, W), offset=0)
            d3 = ap3(delta, (8 * W, 128), (W, 8), (1, W))
            t3 = ap3(mix_t, (8 * W, 128), (W, 8), (1, W))
            t3b = ap3(mix_t2, (8 * W, 128), (W, 8), (1, W))
            m3 = ap3(mu_bc, (8 * W, 128), (W, 8), (1, W))
            m3b = ap3(mu_bc2, (8 * W, 128), (W, 8), (1, W))
            emit_mix3(b, cur, sh, d3, b.rn(delta), m3, "mu_bck", t3, b.rn(mix_t),
                      [(xc_r, "m"), (xc_r, "b")])
            p.emit("dve", lambda e, t3b=t3b, d3=d3, m3b=m3b:
                   e.tensor_tensor(t3b, d3, m3b, OP.mult),
                   reads=[b.rn(delta), "mu_bcv"], writes=[b.rn(mix_t2)])
            p.emit("dve", lambda e, t3b=t3b, sh=sh:
                   e.tensor_tensor(t3b, t3b, sh, OP.add),
                   reads=[b.rn(mix_t2), (xc_r, "m"), (xc_r, "b")],
                   writes=[b.rn(mix_t2)])
            for m in range(8):
                pk, pk_r = proj_group(b, wk, wbufs["k"][1], mix_t, b.rn(mix_t), m, W)
                p.emit("act", lambda e, pk=pk, m=m: e.activation(
                    kk[:, m * W:(m + 1) * W], pk[:, 0:W], AF.Identity,
                    bias=vecs[:, BK * 8 + m:BK * 8 + m + 1]),
                    reads=[pk_r, "l2a_vecs"], writes=[(b.rn(kk), m)], tag="l2a_bk")
            for m in range(8):
                pv, pv_r = proj_group(b, wv, wbufs["v"][1], mix_t2, b.rn(mix_t2), m, W)
                p.emit("act", lambda e, pv=pv, m=m: e.activation(
                    vv[:, m * W:(m + 1) * W], pv[:, 0:W], AF.Identity,
                    bias=vecs[:, BV * 8 + m:BV * 8 + m + 1]),
                    reads=[pv_r, "l2a_vecs"], writes=[(b.rn(vv), m)], tag="l2a_bv")
            p.emit("dve", lambda e: e.tensor_tensor(
                kvc[:, :], kk[:, :], vv[:, :], OP.mult),
                reads=[b.rn(kk), b.rn(vv)], writes=[b.rn(kvc)],
                tag="l2a_kv")
            b.dma("sp", resid_chunk_ap(kvs, c, W), kvc[:, :],
                  reads=[b.rn(kvc)], writes=[("kv_scr", ("kv", c))],
                  tag=f"l2a_kvo{c}")

        accs = b.sb("accs", [128, T], BF16)
        kvt = [b.sb("kvt", [128, T], BF16)]
        for i in range(8):
            kvti = kvt[0]
            kvti_r = b.rn(kvti)
            tile_ap = bass.AP(kvs.tensor, i * 128 * T, [[T, 128], [1, T]])
            b.dma("sp", kvti[:, :], tile_ap,
                  reads=["kv_scr"],
                  writes=[kvti_r], tag=f"l2a_kvl{i}")
            dec0 = bass.AP(dec, i, [[8, 128], [0, T]])
            p.emit("dve", lambda e, kvti=kvti, dec0=dec0:
                   e.tensor_tensor_scan(accs[:, :], dec0, kvti[:, :], 0.0,
                                        OP.mult, OP.add),
                   reads=[kvti_r, "l2a_dec"], writes=["l2a_accs"], tag="l2a_scan")
            p.emit("dve", lambda e, kvti=kvti, i=i: e.scalar_tensor_tensor(
                kvti[:, :], kvti[:, :], vecs[:, UU * 8 + i:UU * 8 + i + 1],
                accs[:, :], OP.mult, OP.add),
                reads=["l2a_accs", kvti_r, "l2a_vecs"], writes=[kvti_r],
                tag="l2a_wkv")
            b.dma("sp", tile_ap, kvti[:, :],
                  reads=[kvti_r], writes=[("kv_scr", ("wkv", i))],
                  tag=f"l2a_wkvo{i}")

        rr = kk
        sg = vv
        rw = b.sb("rw", [128, 8 * W], BF16)
        rwsq = delta
        ot = b.sb("ot", [128, 8 * W], BF16)
        grows = b.sb("grows", [16, 4 * W], F32)
        grows_bf = b.sb("growsbf", [16, 2 * W], BF16)
        expand_cols(b, vecs, MG * 8, 88, "l2a_vecs", mu_bc, "mu_bck", 8, W)
        expand_cols(b, vecs, MR * 8, 88, "l2a_vecs", mu_bc2, "mu_bcv", 8, W)
        eps_c = b.constf(EPS, p=16)
        for c in range(NCH):
            xc = xb[c % 2]
            xc_r = b.rn(xc)
            rxc = rx[c % 2]
            ryc = rxc
            rxc_r = b.rn(rxc)
            ryc_r = rxc_r
            if c == 0:
                src0 = bass.AP(xns.tensor, 0, [[T, 128], [128 * T, 8], [1, W]])
                b.dma("sp", ap3(xc, (8 * (W + 1), 128), (W + 1, 8), (1, W), offset=1),
                      src0, reads=[("xn_scr", 0)], writes=[(xc_r, "m")], tag="l2a_2b0")
                srcw = bass.AP(xns.tensor, T - 1, [[T, 128], [128 * T, 8], [1, 1]])
                b.dma("sp", ap3(xc, (8 * (W + 1), 128), (W + 1, 8), (1, 1)),
                      srcw, reads=[("xn_scr", NCH - 1)], writes=[(xc_r, "b")],
                      tag="l2a_2bw", slow=True)
            else:
                src = bass.AP(xns.tensor, c * W - 1,
                              [[T, 128], [128 * T, 8], [1, W + 1]])
                b.dma("sp", xc[:, :], src,
                      reads=[("xn_scr", c), ("xn_scr", c - 1)],
                      writes=[(xc_r, "m"), (xc_r, "b")], tag=f"l2a_2b{c}")
            b.dma("sp", rxc[:, :], resid_chunk_ap(resid_in, c, W),
                  reads=[(rin_res, c)], writes=[rxc_r], tag=f"l2a_2rx{c}")
            cur = ap3(xc, (8 * (W + 1), 128), (W + 1, 8), (1, W), offset=1)
            sh = ap3(xc, (8 * (W + 1), 128), (W + 1, 8), (1, W), offset=0)
            d3 = ap3(delta, (8 * W, 128), (W, 8), (1, W))
            t3 = ap3(mix_t, (8 * W, 128), (W, 8), (1, W))
            t3b = ap3(mix_t2, (8 * W, 128), (W, 8), (1, W))
            m3 = ap3(mu_bc, (8 * W, 128), (W, 8), (1, W))
            m3b = ap3(mu_bc2, (8 * W, 128), (W, 8), (1, W))
            emit_mix3(b, cur, sh, d3, b.rn(delta), m3, "mu_bck", t3, b.rn(mix_t),
                      [(xc_r, "m"), (xc_r, "b")])
            p.emit("dve", lambda e, t3b=t3b, d3=d3, m3b=m3b:
                   e.tensor_tensor(t3b, d3, m3b, OP.mult),
                   reads=[b.rn(delta), "mu_bcv"], writes=[b.rn(mix_t2)])
            p.emit("dve", lambda e, t3b=t3b, sh=sh:
                   e.tensor_tensor(t3b, t3b, sh, OP.add),
                   reads=[b.rn(mix_t2), (xc_r, "m"), (xc_r, "b")],
                   writes=[b.rn(mix_t2)])
            for m in range(8):
                pg, pg_r = proj_group(b, wg, wbufs["g"][1], mix_t, b.rn(mix_t), m, W)
                p.emit("act", lambda e, pg=pg, m=m: e.activation(
                    sg[:, m * W:(m + 1) * W], pg[:, 0:W], AF.Sigmoid,
                    bias=vecs[:, BG * 8 + m:BG * 8 + m + 1]),
                    reads=[pg_r, "l2a_vecs"], writes=[(b.rn(sg), m)], tag="l2a_sg")
            for m in range(8):
                pr, pr_r = proj_group(b, wrr, wbufs["r"][1], mix_t2, b.rn(mix_t2), m, W)
                p.emit("act", lambda e, pr=pr, m=m: e.activation(
                    rr[:, m * W:(m + 1) * W], pr[:, 0:W], AF.Identity,
                    bias=vecs[:, BR * 8 + m:BR * 8 + m + 1]),
                    reads=[pr_r, "l2a_vecs"], writes=[(b.rn(rr), m)], tag="l2a_br")
            b.dma("sp", kvc[:, :], resid_chunk_ap(kvs, c, W),
                  reads=["kv_scr"], writes=[b.rn(kvc)], tag=f"l2a_wkvl{c}")
            p.emit("dve", lambda e: e.tensor_tensor(
                rw[:, :], rr[:, :], kvc[:, :], OP.mult),
                reads=[b.rn(rr), b.rn(kvc)], writes=[b.rn(rw)], tag="l2a_rw")
            p.emit("act", lambda e: e.activation(rwsq[:, :], rw[:, :], AF.Square),
                   reads=[b.rn(rw)], writes=[b.rn(rwsq)], tag="l2a_rwsq")
            pG1, pG1_r = b.pb[4], b.pb_res[4]
            pG2, pG2_r = b.pb[5], b.pb_res[5]
            prs1 = [(ind[:, i * 16:(i + 1) * 16],
                     rw[:, i * W:(i + 1) * W], ["c_ind_s", b.rn(rw)])
                    for i in range(8)]
            prs2 = [(ind[:, i * 16:(i + 1) * 16],
                     rwsq[:, i * W:(i + 1) * W], ["c_ind_s", b.rn(rwsq)])
                    for i in range(8)]
            b.mm_group(pG1[0:16, 0:W], prs1, pG1_r, tag="l2a_gs1")
            b.mm_group(pG2[0:16, 0:W], prs2, pG2_r, tag="l2a_gs2")
            gr = b.rn(grows)
            grb = b.rn(grows_bf)
            def GR(i):
                return grows[0:16, i * W:(i + 1) * W]
            p.emit("dve", lambda e: e.tensor_scalar(GR(0), pG1[0:16, 0:W],
                                                    1.0 / 64, None, OP.mult),
                   reads=[pG1_r], writes=[(gr, 0)], tag="l2a_gm")
            p.emit("dve", lambda e: e.tensor_scalar(GR(1), pG2[0:16, 0:W],
                                                    1.0 / 64, None, OP.mult),
                   reads=[pG2_r], writes=[(gr, 1)], tag="l2a_gx2")
            p.emit("dve", lambda e: e.tensor_tensor(GR(2), GR(0), GR(0), OP.mult),
                   reads=[(gr, 0)], writes=[(gr, 2)])
            p.emit("dve", lambda e: e.tensor_tensor(GR(1), GR(1), GR(2), OP.subtract),
                   reads=[(gr, 1), (gr, 2)], writes=[(gr, 1)])
            p.emit("act", lambda e: e.activation(GR(3), GR(1), AF.Sqrt, bias=eps_c),
                   reads=[(gr, 1), ("c_const", b._consts[EPS])], writes=[(gr, 3)])
            p.emit("dve", lambda e: e.reciprocal(GR(2), GR(3)),
                   reads=[(gr, 3)], writes=[(gr, 2)])
            p.emit("dve", lambda e: e.tensor_copy(grows_bf[0:16, 0:W], GR(2)),
                   reads=[(gr, 2)], writes=[(grb, 0)])
            p.emit("dve", lambda e: e.scalar_tensor_tensor(
                grows_bf[0:16, W:2 * W], GR(0), -1.0, GR(2), OP.mult, OP.mult),
                reads=[(gr, 0), (gr, 2)], writes=[(grb, 1)])
            for i in range(8):
                pAB2, pAB2_r = b.bank()
                p.emit("pe", lambda e, pAB2=pAB2, i=i: e.matmul(
                    pAB2[:, 0:W], indT[0:16, i * 128:(i + 1) * 128],
                    grows_bf[0:16, 0:W], start=True, stop=True),
                    reads=[(grb, 0), "c_indT_s"], writes=[pAB2_r], tag="l2a_bA")
                p.emit("pe", lambda e, pAB2=pAB2, i=i: e.matmul(
                    pAB2[:, W:2 * W], indT[0:16, i * 128:(i + 1) * 128],
                    grows_bf[0:16, W:2 * W], start=True, stop=True),
                    reads=[(grb, 1), "c_indT_s"], writes=[pAB2_r], tag="l2a_bB")
                p.emit("dve", lambda e, pAB2=pAB2, i=i: e.tensor_tensor(
                    mix_t[:, i * W:(i + 1) * W], rw[:, i * W:(i + 1) * W],
                    pAB2[:, 0:W], OP.mult),
                    reads=[b.rn(rw), pAB2_r], writes=[(b.rn(mix_t), i)], tag="l2a_ln1")
                p.emit("dve", lambda e, pAB2=pAB2, i=i: e.tensor_tensor(
                    mix_t[:, i * W:(i + 1) * W], mix_t[:, i * W:(i + 1) * W],
                    pAB2[:, W:2 * W], OP.add),
                    reads=[(b.rn(mix_t), i), pAB2_r], writes=[(b.rn(mix_t), i)],
                    tag="l2a_ln2")
                p.emit("dve", lambda e, i=i: e.tensor_tensor(
                    ot[:, i * W:(i + 1) * W], sg[:, i * W:(i + 1) * W],
                    mix_t[:, i * W:(i + 1) * W], OP.mult),
                    reads=[(b.rn(mix_t), i), b.rn(sg)], writes=[(b.rn(ot), i)],
                    tag="l2a_ot")
            for m in range(8):
                po, po_r = proj_group(b, wo, wbufs["o"][1], ot, b.rn(ot), m, W)
                p.emit("dve", lambda e, po=po, m=m, ryc=ryc, rxc=rxc:
                       e.scalar_tensor_tensor(
                           ryc[:, m * W:(m + 1) * W], po[:, 0:W],
                           vecs[:, BO * 8 + m:BO * 8 + m + 1],
                           rxc[:, m * W:(m + 1) * W], OP.add, OP.add),
                       reads=[po_r, rxc_r, "l2a_vecs"], writes=[(ryc_r, ("y", m))],
                       tag="l2a_add")
            b.dma("sp", resid_chunk_ap(resid_out, c, W), ryc[:, :],
                  reads=[ryc_r], writes=[(rout_res, c)], tag=f"l2a_out{c}")



def emit_l0attn(b, resid_in, rin_res, resid_out, rout_res):
    p = b.p
    W = 256
    NCH = T // W
    xns = b.dram_tmp("xn_scr", [D, T], BF16)
    with b.scope():
        wa = b.sb("l0a_wa", [128, 8 * 256], BF16)
        wb = b.sb("l0a_wb", [128, 8 * 256], BF16)
        wc = b.sb("l0a_wc", [128, 2 * 1024], BF16)
        wd = b.sb("l0a_wd", [128, 8 * 1024], BF16)
        load_w2mb(b, "L0a_WaT", wa, "l0a_wa", kdim=8, mdim=256)
        load_w2mb(b, "L0a_BT", wb, "l0a_wb", kdim=8, mdim=256)
        load_w2mb(b, "L0a_CT", wc, "l0a_wc", kdim=2, mdim=1024)
        load_w2mb(b, "L0a_DT", wd, "l0a_wd", kdim=8, mdim=1024)
        wab = b.sb("l0a_wab", [128, 2], F32)
        b.dma("sp", wab[:, :], b.dram["L0a_Wab"], writes=["l0a_wab"])

        scr = make_ln_scr(b, W)
        ones = scr["ones"]
        rx = [b.sb("rx", [128, 8 * W], F32) for _ in range(2)]
        xnc = [b.sb("xnc", [128, 8 * W], BF16) for _ in range(2)]
        asb = b.sb("asb", [128, 2 * T], F32)
        bsb = b.sb("bsb", [128, 2 * T], F32)
        hbf = b.sb("hbf", [128, 2 * T], BF16)

        for c in range(NCH):
            rxc = rx[c % 2]
            xc = xnc[c % 2]
            rxc_r, xc_r = b.rn(rxc), b.rn(xc)
            b.dma("sp", rxc[:, :], resid_chunk_ap(resid_in, c, W),
                  reads=[(rin_res, c)], writes=[rxc_r], tag=f"l0a_rx{c}")
            xn_ap = ap3(xc, (8 * W, 128), (W, 8), (1, W))
            emit_ln(b, rxc, rxc_r, xn_ap, xc_r, W, scr, ones)
            b.dma("sp", resid_chunk_ap(xns, c, W), xc[:, :],
                  reads=[xc_r], writes=[("xn_scr", c)], tag=f"l0a_xn{c}")
            for m in range(2):
                pa, pa_r = proj_group(b, wa, "l0a_wa", xc, xc_r, m, W, wm=256)
                p.emit("act", lambda e, pa=pa, m=m, c=c: e.activation(
                    asb[:, m * T + c * W:m * T + (c + 1) * W], pa[:, 0:W],
                    AF.Sigmoid, bias=wab[:, m:m + 1]),
                    reads=[pa_r, "l0a_wab"], writes=[("l0a_a", (m, c))],
                    tag="l0a_sig")
            for m in range(2):
                pb_, pb_r = proj_group(b, wb, "l0a_wb", xc, xc_r, m, W, wm=256)
                p.emit("act", lambda e, pb_=pb_, m=m, c=c: e.copy(
                    bsb[:, m * T + c * W:m * T + (c + 1) * W], pb_[:, 0:W]),
                    reads=[pb_r], writes=[("l0a_b", (m, c))], tag="l0a_b")

        for j in range(2):
            p.emit("dve", lambda e, j=j: e.tensor_tensor_scan(
                hbf[:, j * T:(j + 1) * T], asb[:, j * T:(j + 1) * T],
                bsb[:, j * T:(j + 1) * T], 0.0, OP.mult, OP.add),
                reads=["l0a_a", "l0a_b"], writes=[("l0a_h", j)], tag="l0a_scan")

        xb2 = [b.sb("xb2", [128, 8 * W], BF16) for _ in range(2)]
        for c in range(NCH):
            rxc = rx[c % 2]
            xc = xb2[c % 2]
            rxc_r, xc_r = b.rn(rxc), b.rn(xc)
            b.dma("sp", rxc[:, :], resid_chunk_ap(resid_in, c, W),
                  reads=[(rin_res, c)], writes=[rxc_r], tag=f"l0a_2rx{c}")
            b.dma("sp", xc[:, :], resid_chunk_ap(xns, c, W),
                  reads=[("xn_scr", c)], writes=[xc_r], tag=f"l0a_2xn{c}")
            for m in range(8):
                pbk, pbk_r = b.bank()
                pairs = [(bass.AP(wc, k * 1024 + m * 128, [[2 * 1024, 128], [1, 128]]),
                          bass.AP(hbf, k * T + c * W, [[2 * T, 128], [1, W]]),
                          ["l0a_wc", ("l0a_h", k)]) for k in range(2)]
                pairs += [(bass.AP(wd, k * 1024 + m * 128, [[8 * 1024, 128], [1, 128]]),
                           bass.AP(xc, k * W, [[8 * W, 128], [1, W]]),
                           ["l0a_wd", xc_r]) for k in range(8)]
                b.mm_group(pbk[:, 0:W], pairs, pbk_r, tag="l0a_y")
                p.emit("dve", lambda e, pbk=pbk, m=m, rxc=rxc: e.tensor_tensor(
                    rxc[:, m * W:(m + 1) * W], rxc[:, m * W:(m + 1) * W],
                    pbk[:, 0:W], OP.add),
                    reads=[pbk_r, rxc_r], writes=[(rxc_r, ("y", m))],
                    tag="l0a_add")
            b.dma("sp", resid_chunk_ap(resid_out, c, W), rxc[:, :],
                  reads=[rxc_r], writes=[(rout_res, c)], tag=f"l0a_out{c}")



def emit_l3ffn(b, resid_in, rin_res, resid_out, rout_res, first):
    p = b.p
    W = 512
    NCH = T // W
    H4 = 4096
    caches = {
        "w1": b.dram_tmp("l3c_w1", [D, H4], BF16),
        "w2": b.dram_tmp("l3c_w2", [D, H4], BF16),
        "h1": b.dram_tmp("l3c_h1", [H4, H4], BF16),
        "h2": b.dram_tmp("l3c_h2", [H4, H4], BF16),
        "w3": b.dram_tmp("l3c_w3", [H4, D], BF16),
    }
    srcs = {"w1": "L3f_w1T", "w2": "L3f_w2T", "h1": "L3f_h1T",
            "h2": "L3f_h2T", "w3": "L3f_w3T"}
    kd = {"w1": 8, "w2": 8, "h1": 32, "h2": 32, "w3": 32}
    md = {"w1": H4, "w2": H4, "h1": H4, "h2": H4, "w3": D}

    def wblock_ap(tensor_ap, which, m0, mn):
        mdim = md[which]
        return bass.AP(tensor_ap.tensor, m0 * 128,
                       [[mdim, 128], [128 * mdim, kd[which]], [1, mn * 128]])

    with b.scope():
        sc = b.sb("l3f_sc", [128, 32 * 4 + 8 * 2], F32)
        b.dma("sp", sc[:, 0:128], b.dram["L3f_sc"], writes=[("l3f_sc", 0)])
        b.dma("sp", sc[:, 128:136], b.dram["L3f_s3"], writes=[("l3f_sc", 1)])
        p.emit("dve", lambda e: e.tensor_scalar(sc[:, 136:144], sc[:, 128:136],
                                                0.5, None, OP.mult),
               reads=[("l3f_sc", 1)], writes=[("l3f_sc", 2)], tag="l3f_s3h")

        def S1(m):
            return sc[:, m:m + 1]

        def S2(m):
            return sc[:, 32 + m:32 + m + 1]

        def SH1(m):
            return sc[:, 64 + m:64 + m + 1]

        def SH2(m):
            return sc[:, 96 + m:96 + m + 1]

        def S3H(m):
            return sc[:, 136 + m:136 + m + 1]

        scr = make_ln_scr(b, 256)
        ones = scr["ones"]
        rx = b.sb("rx", [128, 8 * W], F32)
        rx_r = b.rn(rx)
        xn = b.sb("xn", [128, 8 * W], BF16)
        xn_r = b.rn(xn)
        hA = b.sb("hA", [128, 32 * W], BF16)
        hB = b.sb("hB", [128, 32 * W], BF16)
        hA_r, hB_r = b.rn(hA), b.rn(hB)
        t1 = b.sb("t1", [128, W], BF16)
        t1_r = b.rn(t1)
        w1b = b.sb("w1b", [128, 8 * 512], BF16)
        w2b = b.sb("w2b", [128, 8 * 512], BF16)
        hst = [b.sb("hst", [128, 32 * 128], BF16) for _ in range(2)]

        def stream_w(which, m0, mn, buf, buf_res, c):
            if first and c == 0:
                src = wblock_ap(b.dram[srcs[which]], which, m0, mn)
                b.dma("pool", buf[:, 0:kd[which] * mn * 128], src,
                      writes=[buf_res], tag=f"l3fw_{which}{m0}")
                dstc = wblock_ap(caches[which], which, m0, mn)
                b.dma("sp", dstc, buf[:, 0:kd[which] * mn * 128],
                      reads=[buf_res], writes=[(f"l3c_{which}", m0)],
                      tag=f"l3fc_{which}{m0}")
            else:
                src = wblock_ap(caches[which], which, m0, mn)
                b.dma("sp", buf[:, 0:kd[which] * mn * 128], src,
                      reads=[(f"l3c_{which}", m0)],
                      writes=[buf_res], tag=f"l3fr_{which}{m0}")

        for c in range(NCH):
            b.dma("sp", rx[:, :], resid_chunk_ap(resid_in, c, W),
                  reads=[(rin_res, c)], writes=[rx_r], tag=f"l3f_rx{c}")
            for s in range(2):
                xn_ap = ap3(xn, (8 * W, 128), (W, 8), (1, 256), offset=s * 256)
                emit_ln(b, rx, rx_r, xn_ap, (xn_r, s), 256, scr, ones,
                        rx_fw=8 * W, rx_stride=W, rx_off=s * 256)
            for mb in range(8):
                stream_w("w1", mb * 4, 4, w1b, (b.rn(w1b), "w"), c)
                stream_w("w2", mb * 4, 4, w2b, (b.rn(w2b), "w"), c)
                for mi in range(4):
                    m = mb * 4 + mi
                    pg, pg_r = b.bank()
                    pairs = [(bass.AP(w1b, k * 512 + mi * 128,
                                      [[8 * 512, 128], [1, 128]]),
                              bass.AP(xn, k * W, [[8 * W, 128], [1, W]]),
                              [(b.rn(w1b), "w"), xn_r])
                             for k in range(8)]
                    b.mm_group(pg[:, 0:W], pairs, pg_r, tag="l3f_g")
                    p.emit("act", lambda e, pg=pg, m=m: e.activation(
                        t1[:, :], pg[:, 0:W], AF.Silu, scale=S1(m)),
                        reads=[pg_r, ("l3f_sc", 0)], writes=[t1_r], tag="l3f_silu")
                    pu, pu_r = b.bank()
                    pairs = [(bass.AP(w2b, k * 512 + mi * 128,
                                      [[8 * 512, 128], [1, 128]]),
                              bass.AP(xn, k * W, [[8 * W, 128], [1, W]]),
                              [(b.rn(w2b), "w"), xn_r])
                             for k in range(8)]
                    b.mm_group(pu[:, 0:W], pairs, pu_r, tag="l3f_u")
                    p.emit("dve", lambda e, pu=pu, m=m: e.scalar_tensor_tensor(
                        hA[:, m * W:(m + 1) * W], pu[:, 0:W], S2(m), t1[:, :],
                        OP.mult, OP.mult),
                        reads=[pu_r, t1_r, ("l3f_sc", 0)], writes=[(hA_r, m)],
                        tag="l3f_hA")
            for src_buf, src_r, dst_buf, dst_r, which, SH in (
                    (hA, hA_r, hB, hB_r, "h1", SH1),
                    (hB, hB_r, hA, hA_r, "h2", SH2)):
                for m in range(32):
                    hsb = hst[m % 2]
                    stream_w(which, m, 1, hsb, (b.rn(hsb), f"{which}_{m}"), c)
                    ph, ph_r = b.bank()
                    pairs = [(bass.AP(hsb, k * 128, [[32 * 128, 128], [1, 128]]),
                              bass.AP(src_buf, k * W, [[32 * W, 128], [1, W]]),
                              [(b.rn(hsb), f"{which}_{m}"), src_r])
                             for k in range(32)]
                    b.mm_group(ph[:, 0:W], pairs, ph_r, tag=f"l3f_{which}")
                    p.emit("act", lambda e, ph=ph, m=m, SH=SH, dst_buf=dst_buf:
                           e.activation(dst_buf[:, m * W:(m + 1) * W], ph[:, 0:W],
                                        AF.Gelu, scale=SH(m)),
                           reads=[ph_r, ("l3f_sc", 0)], writes=[(dst_r, m)],
                           tag=f"l3f_gelu")
            for m in range(8):
                hsb = hst[m % 2]
                stream_w("w3", m, 1, hsb, (b.rn(hsb), f"w3_{m}"), c)
                pd, pd_r = b.bank()
                pairs = [(bass.AP(hsb, k * 128, [[32 * 128, 128], [1, 128]]),
                          bass.AP(hA, k * W, [[32 * W, 128], [1, W]]),
                          [(b.rn(hsb), f"w3_{m}"), hA_r]) for k in range(32)]
                b.mm_group(pd[:, 0:W], pairs, pd_r, tag="l3f_w3")
                p.emit("dve", lambda e, pd=pd, m=m: e.scalar_tensor_tensor(
                    rx[:, m * W:(m + 1) * W], pd[:, 0:W], S3H(m),
                    rx[:, m * W:(m + 1) * W], OP.mult, OP.add),
                    reads=[pd_r, rx_r, ("l3f_sc", 2)], writes=[(rx_r, ("y", m))],
                    tag="l3f_add")
            b.dma("sp", resid_chunk_ap(resid_out, c, W), rx[:, :],
                  reads=[rx_r], writes=[(rout_res, c)], tag=f"l3f_out{c}")



def emit_l3attn(b, resid_in, rin_res, resid_out, rout_res):
    p = b.p
    W = 512
    NCH = T // W
    qts = b.dram_tmp("qT_scr", [D, T], BF16)
    kts = b.dram_tmp("kT_scr", [D, T], BF16)
    vns = b.dram_tmp("vn_scr", [16 * T, 64], BF16)
    with b.scope():
        wo = b.sb("l3a_wo", [128, 8 * 1024], BF16)
        load_w2mb(b, "L3a_WoT", wo, "l3a_wo")
        bias = b.sb("l3a_bias", [128, 8 * 4], F32)
        b.dma("sp", bias[:, :], b.dram["L3a_bias"], writes=["l3a_bias"])
        masks = b.sb("l3a_masks", [128, 4 * W], BF16)
        b.dma("sp", masks[:, :], b.dram["c_l3mask"], writes=["c_l3mask_s"])
        onesrow = b.sb("onesrow", [1, 128], BF16)
        p.emit("pool", lambda e: e.memset(onesrow[:, :], 1.0),
               writes=["c_onesrow"])
        ones = b.sb("ones", [128, 1], BF16)
        p.emit("pool", lambda e: e.memset(ones[:, :], 1.0), writes=["c_ones_s"])
        rx = b.sb("rx", [128, 8 * W], F32)
        rx_r = b.rn(rx)
        conc = b.sb("conc", [128, 8 * T], BF16)
        conc_r = b.rn(conc)
        sweep1 = b.scope()
        sweep1.__enter__()
        wq = b.sb("l3a_wq", [128, 8 * 1024], BF16)
        wk = b.sb("l3a_wk", [128, 8 * 1024], BF16)
        wv = b.sb("l3a_wv", [128, 8 * 1024], BF16)
        for nm, buf, res in (("L3a_WqT", wq, "l3a_wq"), ("L3a_WkT", wk, "l3a_wk"),
                             ("L3a_WvT", wv, "l3a_wv")):
            load_w2mb(b, nm, buf, res)
        eyeb = b.sb("l3a_eyeb", [128, 128], BF16)
        b.dma("sp", eyeb[:, :], b.dram["c_eye"], writes=["c_eye_s"])
        scr = make_ln_scr(b, 256, ones=ones, onesrow=onesrow)
        xn = b.sb("xn", [128, 8 * W], BF16)
        xn_r = b.rn(xn)
        ev = b.sb("ev", [128, W], BF16)
        ev_r = b.rn(ev)
        vntc = b.sb("vntc", [128, 64], BF16)
        vntc_r = b.rn(vntc)

        for c in range(NCH):
            b.dma("sp", rx[:, :], resid_chunk_ap(resid_in, c, W),
                  reads=[(rin_res, c)], writes=[rx_r], tag=f"l3a_rx{c}")
            for s in range(2):
                xn_ap = ap3(xn, (8 * W, 128), (W, 8), (1, 256), offset=s * 256)
                emit_ln(b, rx, rx_r, xn_ap, (xn_r, s), 256, scr, ones,
                        rx_fw=8 * W, rx_stride=W, rx_off=s * 256)
            for which, wbuf, wres, scrt, bofs in (
                    ("q", wq, "l3a_wq", qts, 0), ("k", wk, "l3a_wk", kts, 1),
                    ("v", wv, "l3a_wv", None, 2)):
                for m in range(8):
                    pk, pk_r = proj_group(b, wbuf, wres, xn, xn_r, m, W)
                    p.emit("act", lambda e, pk=pk, m=m, bofs=bofs: e.activation(
                        ev[:, :], pk[:, 0:W], AF.Identity,
                        bias=bias[:, bofs * 8 + m:bofs * 8 + m + 1]),
                        reads=[pk_r, "l3a_bias"], writes=[ev_r], tag="l3a_ev")
                    if which != "v":
                        dst = bass.AP(scrt.tensor, m * 128 * T + c * W,
                                      [[T, 128], [1, W]])
                        b.dma("sp", dst, ev[:, :], reads=[ev_r],
                              writes=[(f"{which}T_scr", (m, c))], tag="l3a_st")
                    else:
                        for hh_i in range(2):
                            for gg in range(W // 128):
                                h_idx = 2 * m + hh_i
                                t0 = c * W + gg * 128
                                p.emit("pe", lambda e, hh_i=hh_i, gg=gg:
                                       bf16_transpose(
                                           e, b.nc, b.pbt[:, 0:64],
                                           ev[64 * hh_i:64 * (hh_i + 1),
                                              gg * 128:(gg + 1) * 128],
                                           eyeb[64 * hh_i:64 * hh_i + 64,
                                                64 * hh_i:64 * hh_i + 64]),
                                       reads=[ev_r, "c_eye_s"],
                                       writes=[b.pbt_res], tag="l3a_vT")
                                p.emit("act", lambda e: e.copy(
                                    vntc[:, :], b.pbt[:, 0:64]),
                                    reads=[b.pbt_res], writes=[vntc_r],
                                    tag="l3a_vTe")
                                dst = bass.AP(vns.tensor,
                                              (h_idx * T + t0) * 64,
                                              [[64, 128], [1, 64]])
                                b.dma("sp", dst, vntc[:, :], reads=[vntc_r],
                                      writes=[("vn_scr", (h_idx, c, gg))],
                                      tag="l3a_vno")

        sweep1.__exit__(None, None, None)
        kaug = b.sb("kaug", [65, T], BF16)
        qaug = b.sb("qaug", [65, T], BF16)
        kaug_r, qaug_r = b.rn(kaug), b.rn(qaug)
        vab = b.sb("vab", [128, 32 * 65], BF16)
        vab_r = b.rn(vab)
        sqs = b.sb("sqs", [64, T], BF16)
        sqs_r = b.rn(sqs)
        arow = b.sb("arow", [1, T], BF16)
        arow_r = b.rn(arow)
        kmx = b.sb("kmx", [1, 8], F32)
        kmx_r = b.rn(kmx)
        expt = [b.sb("expt", [128, W], BF16) for _ in range(2)]
        dreci = b.sb("dreci", [1, W], BF16)
        dreci_r = b.rn(dreci)
        ctxe = b.sb("ctxe", [64, W], BF16)
        ctxe_r = b.rn(ctxe)
        conc = b.sb("conc", [128, 8 * T], BF16)
        conc_r = b.rn(conc)
        p.emit("pool", lambda e: e.memset(kaug[64:65, :], 1.0),
               writes=[(kaug_r, "ones")], tag="l3a_k1")
        p.emit("pool", lambda e: e.memset(
            ap3(vab, (32 * 65, 128), (65, 32), (1, 1), offset=64), 1.0),
            writes=[(vab_r, "ones")], tag="l3a_v1")
        pav, pav_r = b.pb[5], b.pb_res[5]
        for h in range(16):
            b.dma("sp", kaug[0:64, :],
                  bass.AP(kts.tensor, h * 64 * T, [[T, 64], [1, T]]),
                  reads=["kT_scr"], writes=[(kaug_r, "k")], tag=f"l3a_kl{h}")
            b.dma("sp", qaug[0:64, :],
                  bass.AP(qts.tensor, h * 64 * T, [[T, 64], [1, T]]),
                  reads=["qT_scr"], writes=[(qaug_r, "q")], tag=f"l3a_ql{h}")
            b.dma("sp", ap3(vab, (32 * 65, 128), (65, 32), (1, 64)),
                  bass.AP(vns.tensor, h * T * 64, [[64, 128], [128 * 64, 32], [1, 64]]),
                  reads=["vn_scr"], writes=[(vab_r, "v")], tag=f"l3a_vl{h}")
            for aug, nm in ((kaug, "k"), (qaug, "q")):
                p.emit("act", lambda e, aug=aug, nm=nm: e.activation(
                    sqs[:, :], aug[0:64, :], AF.Square),
                    reads=[(b.rn(aug), nm)], writes=[sqs_r], tag="l3a_sq")
                for cc in range(8):
                    psq, psq_r = b.bank()
                    prs = [(ones[0:64, 0:1], sqs[0:64, cc * W:(cc + 1) * W],
                            [sqs_r, "c_ones_s"])]
                    b.mm_group(psq[0:1, 0:W], prs, psq_r, tag="l3a_qs")
                    p.emit("act", lambda e, psq=psq, cc=cc, nm=nm: e.activation(
                        arow[0:1, cc * W:(cc + 1) * W], psq[0:1, 0:W], AF.Sqrt),
                        reads=[psq_r], writes=[(arow_r, (nm, cc))], tag="l3a_sqrt")
                if nm == "k":
                    p.emit("dve", lambda e: e.reduce_max(
                        kmx[0:1, 0:1], arow[0:1, :], axis=AX.X),
                        reads=[arow_r], writes=[kmx_r], tag="l3a_kmx")
            p.emit("dve", lambda e: e.tensor_scalar(
                qaug[64:65, :], arow[0:1, :], kmx[0:1, 0:1], -1.0,
                OP.mult, OP.mult),
                reads=[arow_r, kmx_r], writes=[(qaug_r, "m")], tag="l3a_negM")
            for qc in range(NCH):
                nkb = 4 * qc + 4
                for kb in range(nkb):
                    ps, ps_r = b.bank()
                    ex = expt[kb % 2]
                    ex_r = b.rn(ex)
                    p.emit("pe", lambda e, ps=ps, kb=kb, qc=qc: e.matmul(
                        ps[:, 0:W], kaug[:, kb * 128:(kb + 1) * 128],
                        qaug[:, qc * W:(qc + 1) * W], start=True, stop=True),
                        reads=[(kaug_r, "k"), (kaug_r, "ones"),
                               (qaug_r, "q"), (qaug_r, "m")],
                        writes=[ps_r], tag="l3a_s")
                    p.emit("act", lambda e, ps=ps, ex=ex: e.activation(
                        ex[:, :], ps[:, 0:W], AF.Exp, scale=0.125),
                        reads=[ps_r], writes=[ex_r], tag="l3a_exp")
                    dv = kb - 4 * qc
                    if dv >= 0:
                        p.emit("dve", lambda e, ex=ex, dv=dv: e.tensor_tensor(
                            ex[:, :], ex[:, :],
                            masks[:, dv * W:(dv + 1) * W], OP.mult),
                            reads=[ex_r, "c_l3mask_s"], writes=[ex_r],
                            tag="l3a_msk")
                    p.emit("pe", lambda e, ex=ex, kb=kb, st=(kb == 0),
                           sp=(kb == nkb - 1): e.matmul(
                        pav[0:65, 0:W], vab[:, kb * 65:(kb + 1) * 65],
                        ex[:, :], start=st, stop=sp),
                        reads=[ex_r, (vab_r, "v"), (vab_r, "ones")],
                        writes=[pav_r], tag="l3a_av")
                p.emit("dve", lambda e: lp(b.nc, lambda: e.reciprocal(
                    dreci[0:1, :], pav[64:65, 0:W])),
                       reads=[pav_r], writes=[dreci_r], tag="l3a_rec")
                pbc, pbc_r = b.bank()
                p.emit("pe", lambda e, pbc=pbc: e.matmul(
                    pbc[0:64, 0:W], scr["onesrow"][0:1, 0:64], dreci[0:1, :],
                    start=True, stop=True),
                    reads=[dreci_r, "c_onesrow"], writes=[pbc_r], tag="l3a_rb")
                p.emit("act", lambda e: e.copy(ctxe[:, :], pav[0:64, 0:W]),
                       reads=[pav_r], writes=[ctxe_r], tag="l3a_ctx")
                co = (h // 2) * T + qc * W
                cr0 = 64 * (h % 2)
                p.emit("dve", lambda e, pbc=pbc, co=co, cr0=cr0: e.tensor_tensor(
                    conc[cr0:cr0 + 64, co:co + W], ctxe[:, :], pbc[0:64, 0:W],
                    OP.mult),
                    reads=[ctxe_r, pbc_r], writes=[(conc_r, (h, qc))],
                    tag="l3a_nrm")

        for c in range(NCH):
            b.dma("sp", rx[:, :], resid_chunk_ap(resid_in, c, W),
                  reads=[(rin_res, c)], writes=[rx_r], tag=f"l3a_2rx{c}")
            for m in range(8):
                po, po_r = proj_group(b, wo, "l3a_wo", conc, conc_r, m, W,
                                      rhs_stride=T, rhs_off=c * W)
                p.emit("dve", lambda e, po=po, m=m: e.scalar_tensor_tensor(
                    rx[:, m * W:(m + 1) * W], po[:, 0:W],
                    bias[:, 3 * 8 + m:3 * 8 + m + 1],
                    rx[:, m * W:(m + 1) * W], OP.add, OP.add),
                    reads=[po_r, rx_r, "l3a_bias"], writes=[(rx_r, ("y", m))],
                    tag="l3a_add")
            b.dma("sp", resid_chunk_ap(resid_out, c, W), rx[:, :],
                  reads=[rx_r], writes=[(rout_res, c)], tag=f"l3a_out{c}")



def emit_fp4_quant(b, x_h, x_off, x_fw, x_res, out_ap, out_res, F, q):
    p = b.p
    NB = F // 16
    x3 = ap3(x_h, (x_fw, 128), (16, NB), (1, 16), offset=x_off)
    t_h, i32_h, sgn_h, sc_h, rc_h, qq_h = (q["t"], q["i32"], q["sgn"],
                                           q["sc"], q["rc"], q["qq"])
    t_r, i_r, g_r, sc_r, rc_r, qq_r = (b.rn(t_h), b.rn(i32_h), b.rn(sgn_h),
                                       b.rn(sc_h), b.rn(rc_h), b.rn(qq_h))
    WQ, WNB = q["wide"], q["nbw"]
    xa = bass.AP(x_h, x_off, [[x_fw, 128], [1, F]])
    ta = t_h[:, 0:F]
    ia = i32_h[:, 0:F]
    ga = sgn_h[:, 0:F]
    qa = qq_h[:, 0:F]
    sca = sc_h[:, 0:NB]
    rca = rc_h[:, 0:NB]
    sc3 = ap3(sc_h, (WNB, 128), (1, NB), (0, 16))
    rc3 = ap3(rc_h, (WNB, 128), (1, NB), (0, 16))
    q3 = ap3(qq_h, (WQ, 128), (16, NB), (1, 16))
    p.emit("dve", lambda e: e.reduce_max(sca, x3, axis=AX.X,
                                         apply_absolute_value=True),
           reads=[x_res], writes=[sc_r], tag="q_amax")
    p.emit("dve", lambda e: e.tensor_scalar(sca, sca, 1.0 / 6, 1e-12,
                                            OP.mult, OP.max),
           reads=[sc_r], writes=[sc_r], tag="q_sc")
    p.emit("dve", lambda e: e.reciprocal(rca, sca),
           reads=[sc_r], writes=[rc_r], tag="q_rc")
    p.emit("dve", lambda e: e.tensor_tensor(x3, x3, rc3, OP.mult),
           reads=[x_res, rc_r], writes=[x_res], tag="q_xn")
    p.emit("act", lambda e: e.activation(ta, xa, AF.Abs),
           reads=[x_res], writes=[t_r], tag="q_abs")
    p.emit("act", lambda e: e.activation(ga, xa, AF.Sign),
           reads=[x_res], writes=[g_r], tag="q_sgn")
    p.emit("dve", lambda e: e.tensor_scalar(ia, ta, 2.0, None, OP.mult),
           reads=[t_r], writes=[i_r], tag="q_rnd")
    p.emit("dve", lambda e: e.tensor_scalar(qa, ia, 0.5, 2.0, OP.mult, OP.min),
           reads=[i_r], writes=[qq_r], tag="q_low")
    for bd in (2.5, 3.5, 5.0, 5.0):
        p.emit("dve", lambda e, bd=bd: e.scalar_tensor_tensor(
            qa, ta, bd, qa, OP.is_gt, OP.add),
            reads=[t_r, qq_r], writes=[qq_r], tag="q_hi")
    p.emit("dve", lambda e: e.tensor_tensor(qa, qa, ga, OP.mult),
           reads=[qq_r, g_r], writes=[qq_r], tag="q_sg")
    p.emit("dve", lambda e: e.tensor_tensor(out_ap, q3, sc3, OP.mult),
           reads=[qq_r, sc_r], writes=[out_res], tag="q_out")


def emit_l0ffn(b, resid_in, rin_res, resid_out, rout_res, first):
    p = b.p
    W = 256
    NCH = T // W
    I0 = 2816
    NI = 22
    cg = b.dram_tmp("l0c_g", [D, I0], BF16)
    cu = b.dram_tmp("l0c_u", [D, I0], BF16)
    cd = b.dram_tmp("l0c_d", [I0, D], BF16)
    with b.scope():
        qs = {"t": b.sb("q_t", [128, 2048], F32),
              "i32": b.sb("q_i", [128, 2048], I32),
              "sgn": b.sb("q_g", [128, 2048], BF16),
              "sc": b.sb("q_s", [128, 128], F32),
              "rc": b.sb("q_r", [128, 128], F32),
              "qq": b.sb("q_q", [128, 2048], F32),
              "wide": 2048, "nbw": 128}
        eyeb = b.sb("l0_eyeb", [128, 128], BF16)
        b.dma("sp", eyeb[:, :], b.dram["c_eye"], writes=["c_eye_s"])

        if first:
            with b.scope():
                wtile = b.sb("l0w_tile", [128, I0], F32)
                wt_r = b.rn(wtile)
                wqt = b.sb("l0w_q", [128, I0], BF16)
                wq_r = b.rn(wqt)
                stg = b.sb("l0w_stg", [128, 128], BF16)
                stg_r = b.rn(stg)
                for srcnm, cch, ccres, ntile, fw in (
                        ("L0f_wg", cg, "l0c_g", 22, 1024),
                        ("L0f_wu", cu, "l0c_u", 22, 1024),
                        ("L0f_wd", cd, "l0c_d", 8, I0)):
                    for n in range(ntile):
                        src = bass.AP(b.dram[srcnm].tensor, n * 128 * fw,
                                      [[fw, 128], [1, fw]])
                        b.dma("sp", wtile[:, 0:fw], src, writes=[wt_r],
                              tag=f"l0w_{srcnm}{n}")
                        subs = [(0, 1024)] if fw == 1024 else [(0, 1408),
                                                               (1408, 1408)]
                        for off, ln_ in subs:
                            emit_fp4_quant(
                                b, wtile, off, I0, wt_r,
                                bass.AP(wqt, off, [[I0, 128], [1, ln_]]),
                                wq_r, ln_, qs)
                        for ks in range(fw // 128):
                            p.emit("pe", lambda e, ks=ks: bf16_transpose(
                                e, b.nc, b.pbt[:, 0:128],
                                wqt[:, ks * 128:(ks + 1) * 128], eyeb[:, :]),
                                reads=[wq_r, "c_eye_s"], writes=[b.pbt_res],
                                tag="l0w_T")
                            p.emit("act", lambda e: e.copy(stg[:, :],
                                                           b.pbt[:, 0:128]),
                                   reads=[b.pbt_res], writes=[stg_r],
                                   tag="l0w_Te")
                            dst = bass.AP(cch.tensor,
                                          ks * 128 * (I0 if fw == 1024 else D)
                                          + n * 128,
                                          [[I0 if fw == 1024 else D, 128],
                                           [1, 128]])
                            b.dma("sp", dst, stg[:, :], reads=[stg_r],
                                  writes=[(ccres, (n, ks))], tag="l0w_To")

        wg = b.sb("l0_wg", [128, 8 * I0], BF16)
        wu = b.sb("l0_wu", [128, 8 * I0], BF16)
        wg_r, wu_r = b.rn(wg), b.rn(wu)
        src = bass.AP(cg.tensor, 0, [[I0, 128], [128 * I0, 8], [1, I0]])
        b.dma("sp", wg[:, :], src, reads=["l0c_g"], writes=[wg_r], tag="l0_lg")
        src = bass.AP(cu.tensor, 0, [[I0, 128], [128 * I0, 8], [1, I0]])
        b.dma("sp", wu[:, :], src, reads=["l0c_u"], writes=[wu_r], tag="l0_lu")
        wd = [b.sb("l0_wd", [128, 22 * 128], BF16) for _ in range(2)]

        scr = make_ln_scr(b, W, xc_buf=qs["t"])
        scr["ln_tmp32"] = qs["qq"]
        ones = scr["ones"]
        rx = [b.sb("rx", [128, 8 * W], F32)]
        rx.append(rx[0])
        xnf = b.sb("xnf", [128, 8 * W], F32)
        xnf_r = b.rn(xnf)
        xbt = b.sb("xbt", [128, 8 * W], F32)
        xbt_r = b.rn(xbt)
        xqb = qs["sgn"]
        xqb_r = b.rn(xqb)
        t1f = b.sb("t1f", [128, W], F32)
        t1_r = b.rn(t1f)
        hf = b.sb("hf", [128, NI * W], F32)
        hf_r = b.rn(hf)
        hqb = b.sb("hqb", [128, NI * W], BF16)
        hqb_r = b.rn(hqb)

        for c in range(NCH):
            rxc = rx[c % 2]
            rxc_r = b.rn(rxc)
            b.dma("sp", rxc[:, :], resid_chunk_ap(resid_in, c, W),
                  reads=[(rin_res, c)], writes=[rxc_r], tag=f"l0_rx{c}")
            xn_ap = ap3(xnf, (8 * W, 128), (W, 8), (1, W))
            emit_ln(b, rxc, rxc_r, xn_ap, xnf_r, W, scr, ones)
            p.emit("dve", lambda e: e.transpose(xbt[:, :], xnf[:, :]),
                   reads=[xnf_r], writes=[xbt_r], tag="l0_bt")
            emit_fp4_quant(b, xbt, 0, 8 * W, xbt_r,
                           bass.AP(xnf, 0, [[8 * W, 128], [1, 8 * W]]),
                           xnf_r, 8 * W, qs)
            p.emit("dve", lambda e: e.transpose(xbt[:, :], xnf[:, :]),
                   reads=[xnf_r], writes=[xbt_r], tag="l0_btb")
            p.emit("act", lambda e: e.copy(xqb[:, :], xbt[:, :]),
                   reads=[xbt_r], writes=[xqb_r], tag="l0_xq")
            for m in range(NI):
                pg, pg_r = proj_group(b, wg, wg_r, xqb, xqb_r, m, W, wm=I0)
                p.emit("act", lambda e, pg=pg: e.activation(
                    t1f[:, :], pg[:, 0:W], AF.Silu),
                    reads=[pg_r], writes=[t1_r], tag="l0_silu")
                pu, pu_r = proj_group(b, wu, wu_r, xqb, xqb_r, m, W, wm=I0)
                p.emit("dve", lambda e, pu=pu, m=m: e.tensor_tensor(
                    hf[:, m * W:(m + 1) * W], pu[:, 0:W], t1f[:, :], OP.mult),
                    reads=[pu_r, t1_r], writes=[(hf_r, m)], tag="l0_h")
            for sbi in range(4):
                o = sbi * 1408
                p.emit("dve", lambda e, o=o: e.transpose(
                    bass.AP(xbt, 0, [[8 * W, 128], [1, 1408]]),
                    bass.AP(hf, o, [[NI * W, 128], [1, 1408]])),
                    reads=[hf_r], writes=[xbt_r], tag="l0_hbt")
                emit_fp4_quant(b, xbt, 0, 8 * W, xbt_r,
                               bass.AP(xnf, 0, [[8 * W, 128], [1, 1408]]),
                               xnf_r, 1408, qs)
                p.emit("dve", lambda e, o=o: e.transpose(
                    bass.AP(hf, o, [[NI * W, 128], [1, 1408]]),
                    bass.AP(xnf, 0, [[8 * W, 128], [1, 1408]])),
                    reads=[xnf_r], writes=[(hf_r, ("q", sbi))], tag="l0_hbtb")
            p.emit("act", lambda e: e.copy(hqb[:, :], hf[:, :]),
                   reads=[hf_r], writes=[hqb_r], tag="l0_hq")
            for m in range(8):
                wdc = wd[m % 2]
                wdc_r = b.rn(wdc)
                src = bass.AP(cd.tensor, m * 128,
                              [[D, 128], [128 * D, 22], [1, 128]])
                b.dma("sp", wdc[:, :], src, reads=["l0c_d"], writes=[wdc_r],
                      tag=f"l0_ld{c}_{m}")
                pd, pd_r = b.bank()
                pairs = [(bass.AP(wdc, k * 128, [[22 * 128, 128], [1, 128]]),
                          bass.AP(hqb, k * W, [[NI * W, 128], [1, W]]),
                          [wdc_r, hqb_r]) for k in range(22)]
                b.mm_group(pd[:, 0:W], pairs, pd_r, tag="l0_d")
                p.emit("dve", lambda e, pd=pd, m=m, rxc=rxc:
                       e.scalar_tensor_tensor(
                           rxc[:, m * W:(m + 1) * W], pd[:, 0:W], 0.5,
                           rxc[:, m * W:(m + 1) * W], OP.mult, OP.add),
                       reads=[pd_r, rxc_r], writes=[(rxc_r, ("y", m))],
                       tag="l0_add")
            b.dma("sp", resid_chunk_ap(resid_out, c, W), rxc[:, :],
                  reads=[rxc_r], writes=[(rout_res, c)], tag=f"l0_out{c}")



def emit_l1attn(b, resid_in, rin_res, resid_out, rout_res):
    p = b.p
    W = 512
    NCH = T // W
    NW = T // 128
    qts = b.dram_tmp("qT_scr", [D, T], BF16)
    kts = b.dram_tmp("kT_scr", [D, T], BF16)
    vns = b.dram_tmp("vn_scr", [16 * T, 64], BF16)
    with b.scope():
        wo = b.sb("l1a_wo", [128, 8 * 1024], BF16)
        load_w2mb(b, "L1a_WoT", wo, "l1a_wo")
        onesrow = b.sb("onesrow", [1, 128], BF16)
        p.emit("pool", lambda e: e.memset(onesrow[:, :], 1.0),
               writes=["c_onesrow"])
        ones = b.sb("ones", [128, 1], BF16)
        p.emit("pool", lambda e: e.memset(ones[:, :], 1.0), writes=["c_ones_s"])
        rx = b.sb("rx", [128, 8 * W], F32)
        rx_r = b.rn(rx)
        conc = b.sb("conc", [128, 8 * T], BF16)
        conc_r = b.rn(conc)
        sweep1 = b.scope()
        sweep1.__enter__()
        wq = b.sb("l1a_wq", [128, 8 * 1024], BF16)
        wk = b.sb("l1a_wk", [128, 8 * 1024], BF16)
        wv = b.sb("l1a_wv", [128, 8 * 1024], BF16)
        for nm, buf, res in (("L1a_WqT", wq, "l1a_wq"), ("L1a_WkT", wk, "l1a_wk"),
                             ("L1a_WvT", wv, "l1a_wv")):
            load_w2mb(b, nm, buf, res)
        eyeb = b.sb("l1a_eyeb", [128, 128], BF16)
        b.dma("sp", eyeb[:, :], b.dram["c_eye"], writes=["c_eye_s"])
        scr = make_ln_scr(b, 256, ones=ones, onesrow=onesrow)
        xn = b.sb("xn", [128, 8 * W], BF16)
        xn_r = b.rn(xn)
        ev = b.sb("ev", [128, W], BF16)
        ev_r = b.rn(ev)
        vntc = b.sb("vntc", [128, 64], BF16)
        vntc_r = b.rn(vntc)
        for c in range(NCH):
            b.dma("sp", rx[:, :], resid_chunk_ap(resid_in, c, W),
                  reads=[(rin_res, c)], writes=[rx_r], tag=f"l1a_rx{c}")
            for s in range(2):
                xn_ap = ap3(xn, (8 * W, 128), (W, 8), (1, 256), offset=s * 256)
                emit_ln(b, rx, rx_r, xn_ap, (xn_r, s), 256, scr, ones,
                        rx_fw=8 * W, rx_stride=W, rx_off=s * 256)
            for which, wbuf, wres in (("q", wq, "l1a_wq"), ("k", wk, "l1a_wk"),
                                      ("v", wv, "l1a_wv")):
                for m in range(8):
                    pk, pk_r = proj_group(b, wbuf, wres, xn, xn_r, m, W)
                    p.emit("act", lambda e, pk=pk: e.copy(ev[:, :], pk[:, 0:W]),
                           reads=[pk_r], writes=[ev_r], tag="l1a_ev")
                    if which != "v":
                        scrt = qts if which == "q" else kts
                        dst = bass.AP(scrt.tensor, m * 128 * T + c * W,
                                      [[T, 128], [1, W]])
                        b.dma("sp", dst, ev[:, :], reads=[ev_r],
                              writes=[(f"{which}T_scr", (m, c))], tag="l1a_st")
                    else:
                        for hh_i in range(2):
                            for gg in range(W // 128):
                                h_idx = 2 * m + hh_i
                                t0 = c * W + gg * 128
                                p.emit("pe", lambda e, hh_i=hh_i, gg=gg:
                                       bf16_transpose(
                                           e, b.nc, b.pbt[:, 0:64],
                                           ev[64 * hh_i:64 * (hh_i + 1),
                                              gg * 128:(gg + 1) * 128],
                                           eyeb[64 * hh_i:64 * hh_i + 64,
                                                64 * hh_i:64 * hh_i + 64]),
                                       reads=[ev_r, "c_eye_s"],
                                       writes=[b.pbt_res], tag="l1a_vT")
                                p.emit("act", lambda e: e.copy(
                                    vntc[:, :], b.pbt[:, 0:64]),
                                    reads=[b.pbt_res], writes=[vntc_r],
                                    tag="l1a_vTe")
                                dst = bass.AP(vns.tensor,
                                              (h_idx * T + t0) * 64,
                                              [[64, 128], [1, 64]])
                                b.dma("sp", dst, vntc[:, :], reads=[vntc_r],
                                      writes=[("vn_scr", (h_idx, c, gg))],
                                      tag="l1a_vno")
        sweep1.__exit__(None, None, None)
        kaug = b.sb("kaug", [65, T], BF16)
        qaug = b.sb("qaug", [65, T], BF16)
        kaug_r, qaug_r = b.rn(kaug), b.rn(qaug)
        vab = b.sb("vab", [128, 32 * 65], BF16)
        vab_r = b.rn(vab)
        sqs = b.sb("sqs", [64, T], BF16)
        sqs_r = b.rn(sqs)
        arow = b.sb("arow", [1, T], BF16)
        arow_r = b.rn(arow)
        kmx = b.sb("kmx", [1, 8], F32)
        kmx_r = b.rn(kmx)
        expl = [b.sb("expl", [128, 256], BF16) for _ in range(2)]
        dreci = b.sb("dreci", [1, 128], BF16)
        dreci_r = b.rn(dreci)
        ctxe = b.sb("ctxe", [64, 128], BF16)
        ctxe_r = b.rn(ctxe)
        p.emit("pool", lambda e: e.memset(kaug[64:65, :], 1.0),
               writes=[(kaug_r, "ones")], tag="l1a_k1")
        p.emit("pool", lambda e: e.memset(
            ap3(vab, (32 * 65, 128), (65, 32), (1, 1), offset=64), 1.0),
            writes=[(vab_r, "ones")], tag="l1a_v1")
        pav, pav_r = b.pb[5], b.pb_res[5]
        for h in range(16):
            b.dma("sp", kaug[0:64, :],
                  bass.AP(kts.tensor, h * 64 * T, [[T, 64], [1, T]]),
                  reads=["kT_scr"], writes=[(kaug_r, "k")], tag=f"l1a_kl{h}")
            b.dma("sp", qaug[0:64, :],
                  bass.AP(qts.tensor, h * 64 * T, [[T, 64], [1, T]]),
                  reads=["qT_scr"], writes=[(qaug_r, "q")], tag=f"l1a_ql{h}")
            b.dma("sp", ap3(vab, (32 * 65, 128), (65, 32), (1, 64)),
                  bass.AP(vns.tensor, h * T * 64,
                          [[64, 128], [128 * 64, 32], [1, 64]]),
                  reads=["vn_scr"], writes=[(vab_r, "v")], tag=f"l1a_vl{h}")
            for aug, nm in ((kaug, "k"), (qaug, "q")):
                p.emit("act", lambda e, aug=aug, nm=nm: e.activation(
                    sqs[:, :], aug[0:64, :], AF.Square),
                    reads=[(b.rn(aug), nm)], writes=[sqs_r], tag="l1a_sq")
                for cc in range(8):
                    psq, psq_r = b.bank()
                    prs = [(ones[0:64, 0:1], sqs[0:64, cc * W:(cc + 1) * W],
                            [sqs_r, "c_ones_s"])]
                    b.mm_group(psq[0:1, 0:W], prs, psq_r, tag="l1a_qs")
                    p.emit("act", lambda e, psq=psq, cc=cc, nm=nm: e.activation(
                        arow[0:1, cc * W:(cc + 1) * W], psq[0:1, 0:W], AF.Sqrt),
                        reads=[psq_r], writes=[(arow_r, (nm, cc))], tag="l1a_sqrt")
                if nm == "k":
                    p.emit("dve", lambda e: e.reduce_max(
                        kmx[0:1, 0:1], arow[0:1, :], axis=AX.X),
                        reads=[arow_r], writes=[kmx_r], tag="l1a_kmx")
            p.emit("dve", lambda e: e.tensor_scalar(
                qaug[64:65, :], arow[0:1, :], kmx[0:1, 0:1], -1.0,
                OP.mult, OP.mult),
                reads=[arow_r, kmx_r], writes=[(qaug_r, "m")], tag="l1a_negM")
            for w in range(NW):
                ps, ps_r = b.bank()
                ex = expl[w % 2]
                ex_r = b.rn(ex)
                p.emit("pe", lambda e, ps=ps, w=w: e.matmul(
                    ps[:, 0:128], kaug[:, w * 128:(w + 1) * 128],
                    qaug[:, w * 128:(w + 1) * 128], start=True, stop=True),
                    reads=[(kaug_r, "k"), (kaug_r, "ones"),
                           (qaug_r, "q"), (qaug_r, "m")],
                    writes=[ps_r], tag="l1a_sl")
                p.emit("pe", lambda e, ps=ps, w=w: e.matmul(
                    ps[0:64, 128:256], kaug[:, 0:64],
                    qaug[:, w * 128:(w + 1) * 128], start=True, stop=True),
                    reads=[(kaug_r, "k"), (kaug_r, "ones"),
                           (qaug_r, "q"), (qaug_r, "m")],
                    writes=[ps_r], tag="l1a_sg")
                p.emit("act", lambda e, ps=ps, ex=ex: e.activation(
                    ex[:, 0:128], ps[:, 0:128], AF.Exp, scale=0.125),
                    reads=[ps_r], writes=[(ex_r, "l")], tag="l1a_expl")
                p.emit("act", lambda e, ps=ps, ex=ex: e.activation(
                    ex[0:64, 128:256], ps[0:64, 128:256], AF.Exp, scale=0.125),
                    reads=[ps_r], writes=[(ex_r, "g")], tag="l1a_expg")
                p.emit("pe", lambda e, ex=ex, w=w: e.matmul(
                    pav[0:65, 0:128], vab[:, w * 65:(w + 1) * 65],
                    ex[:, 0:128], start=True, stop=False),
                    reads=[(ex_r, "l"), (vab_r, "v"), (vab_r, "ones")],
                    writes=[pav_r], tag="l1a_avl")
                p.emit("pe", lambda e, ex=ex: e.matmul(
                    pav[0:65, 0:128], vab[0:64, 0:65],
                    ex[0:64, 128:256], start=False, stop=True),
                    reads=[(ex_r, "g"), (vab_r, "v"), (vab_r, "ones")],
                    writes=[pav_r], tag="l1a_avg")
                p.emit("dve", lambda e: lp(b.nc, lambda: e.reciprocal(
                    dreci[0:1, :], pav[64:65, 0:128])),
                       reads=[pav_r], writes=[dreci_r], tag="l1a_rec")
                pbc, pbc_r = b.bank()
                p.emit("pe", lambda e, pbc=pbc: e.matmul(
                    pbc[0:64, 0:128], onesrow[0:1, 0:64], dreci[0:1, :],
                    start=True, stop=True),
                    reads=[dreci_r, "c_onesrow"], writes=[pbc_r], tag="l1a_rb")
                p.emit("act", lambda e: e.copy(ctxe[:, :], pav[0:64, 0:128]),
                       reads=[pav_r], writes=[ctxe_r], tag="l1a_ctx")
                co = (h // 2) * T + w * 128
                cr0 = 64 * (h % 2)
                p.emit("dve", lambda e, pbc=pbc, co=co, cr0=cr0: e.tensor_tensor(
                    conc[cr0:cr0 + 64, co:co + 128], ctxe[:, :],
                    pbc[0:64, 0:128], OP.mult),
                    reads=[ctxe_r, pbc_r], writes=[(conc_r, (h, w))],
                    tag="l1a_nrm")
        for c in range(NCH):
            b.dma("sp", rx[:, :], resid_chunk_ap(resid_in, c, W),
                  reads=[(rin_res, c)], writes=[rx_r], tag=f"l1a_2rx{c}")
            for m in range(8):
                po, po_r = proj_group(b, wo, "l1a_wo", conc, conc_r, m, W,
                                      rhs_stride=T, rhs_off=c * W)
                p.emit("dve", lambda e, po=po, m=m: e.tensor_tensor(
                    rx[:, m * W:(m + 1) * W], rx[:, m * W:(m + 1) * W],
                    po[:, 0:W], OP.add),
                    reads=[po_r, rx_r], writes=[(rx_r, ("y", m))],
                    tag="l1a_add")
            b.dma("sp", resid_chunk_ap(resid_out, c, W), rx[:, :],
                  reads=[rx_r], writes=[(rout_res, c)], tag=f"l1a_out{c}")



def emit_l1ffn(b, resid_in, rin_res, resid_out, rout_res, first):
    p = b.p
    TC = 1024
    NCH = T // TC
    NE = 9
    I2 = 2048
    cache_g = b.dram_tmp("l1c_wg", [NE * D, I2], BF16)
    cache_u = b.dram_tmp("l1c_wu", [NE * D, I2], BF16)
    cache_d = b.dram_tmp("l1c_wd", [NE * I2, D], BF16)

    with b.scope():
        wrT = b.sb("l1_wrT", [128, 8 * 8], BF16)
        load_w2mb(b, "L1f_WrT", wrT, "l1_wrT", kdim=8, mdim=8)
        rb = b.sb("l1_rb", [8, 1], F32)
        b.dma("sp", rb[:, :], b.dram["L1f_rb"], writes=["l1_rb"])
        einds = b.sb("l1_einds", [8, 8 * 128], BF16)
        b.dma("sp", einds[:, :], b.dram["c_einds"], writes=["c_einds_s"])

        scr = make_ln_scr(b, 256)
        ones = scr["ones"]
        rx = b.sb("rx", [128, 8 * TC], F32)
        rx_r = b.rn(rx)
        xn = b.sb("xn", [128, 8 * TC], BF16)
        xn_r = b.rn(xn)
        wg = b.sb("wg", [128, 8 * I2], BF16)
        wu = b.sb("wu", [128, 8 * I2], BF16)
        wd = b.sb("wd", [128, 16 * D], BF16)
        wg_r, wu_r, wd_r = b.rn(wg), b.rn(wu), b.rn(wd)
        hh = b.sb("hh", [128, 16 * 512], BF16)
        hh_r = b.rn(hh)
        xe = b.sb("xe", [128, 8 * 512], BF16)
        xe_r = b.rn(xe)
        t1 = b.sb("t1", [128, 512], BF16)
        t1_r = b.rn(t1)
        probs = b.sb("probs", [8, TC], F32)
        pr_r = b.rn(probs)
        eyef = b.sb("l1_eyef", [128, 128], F32)
        b.dma("sp", eyef[:, :], b.dram["c_eyef"], writes=["c_eyef_s"])
        ptS = b.sb("ptS", [128, 64], F32)
        ptS_r = b.rn(ptS)
        m12 = b.sb("m12", [128, 16], F32)
        m12_r = b.rn(m12)
        msk = b.sb("msk", [128, 64], F32)
        msk_r = b.rn(msk)
        cwt = b.sb("cwt", [128, 64], F32)
        cwt_r = b.rn(cwt)
        cwbf = b.sb("cwbf", [8, TC], BF16)
        cw_r = b.rn(cwbf)

        def stream_we(which, e, c, buf, buf_res):
            cach, src, kdim, mdim = {
                "g": (cache_g, "L1f_WgT", 8, I2),
                "u": (cache_u, "L1f_WuT", 8, I2),
                "d": (cache_d, "L1f_WdT", 16, D)}[which]
            nb = kdim * 128
            if first and c == 0:
                sap = bass.AP(b.dram[src].tensor, e * nb * mdim,
                              [[mdim, 128], [128 * mdim, kdim], [1, mdim]])
                b.dma("pool", buf[:, :], sap, writes=[buf_res],
                      tag=f"l1w{which}{e}")
                dap = bass.AP(cach.tensor, e * nb * mdim,
                              [[mdim, 128], [128 * mdim, kdim], [1, mdim]])
                b.dma("sp", dap, buf[:, :], reads=[buf_res],
                      writes=[(f"l1c_{which}", e)], tag=f"l1cw{which}{e}")
            else:
                sap = bass.AP(cach.tensor, e * nb * mdim,
                              [[mdim, 128], [128 * mdim, kdim], [1, mdim]])
                b.dma("sp", buf[:, :], sap, reads=[(f"l1c_{which}", e)],
                      writes=[buf_res], tag=f"l1r{which}{e}")

        for c in range(NCH):
            b.dma("sp", rx[:, :], resid_chunk_ap(resid_in, c, TC),
                  reads=[(rin_res, c)], writes=[rx_r], tag=f"l1_rx{c}")
            for s in range(4):
                xn_ap = ap3(xn, (8 * TC, 128), (TC, 8), (1, 256), offset=s * 256)
                emit_ln(b, rx, rx_r, xn_ap, (xn_r, s), 256, scr, ones,
                        rx_fw=8 * TC, rx_stride=TC, rx_off=s * 256)
            for hf in range(2):
                plog, plog_r = b.bank()
                pairs = [(bass.AP(wrT, k * 8, [[8 * 8, 128], [1, 8]]),
                          bass.AP(xn, k * TC + hf * 512, [[8 * TC, 128], [1, 512]]),
                          ["l1_wrT", xn_r]) for k in range(8)]
                b.mm_group(plog[0:8, 0:512], pairs, plog_r, tag="l1_logit")
                p.emit("act", lambda e, plog=plog, hf=hf: e.activation(
                    probs[0:8, hf * 512:(hf + 1) * 512], plog[0:8, 0:512],
                    AF.Sigmoid, bias=rb[0:8, 0:1]),
                    reads=[plog_r, "l1_rb"], writes=[(pr_r, hf)], tag="l1_probs")
            for ggi in range(8):
                ptp, ptp_r = b.bank()
                p.emit("pe", lambda e, ptp=ptp, ggi=ggi: e.transpose(
                    ptp[:, 0:8], probs[0:8, ggi * 128:(ggi + 1) * 128],
                    eyef[0:8, 0:8]),
                    reads=[pr_r, "c_eyef_s"], writes=[ptp_r], tag="l1_ptT")
                p.emit("act", lambda e, ptp=ptp, ggi=ggi: e.copy(
                    ptS[:, ggi * 8:(ggi + 1) * 8], ptp[:, 0:8]),
                    reads=[ptp_r], writes=[(ptS_r, ggi)], tag="l1_ptE")
            pt3 = ap3(ptS, (64, 128), (8, 8), (1, 8))
            msk3 = ap3(msk, (64, 128), (8, 8), (1, 8))
            cwt3 = ap3(cwt, (64, 128), (8, 8), (1, 8))
            m1e = ap3(m12, (16, 128), (1, 8), (0, 8))
            m2e = ap3(m12, (16, 128), (1, 8), (0, 8), offset=8)
            p.emit("dve", lambda e: e.reduce_max(
                m12[:, 0:8], pt3, axis=AX.X),
                reads=[ptS_r], writes=[(m12_r, 1)], tag="l1_m1")
            p.emit("dve", lambda e: e.tensor_tensor(msk3, pt3, m1e, OP.is_ge),
                   reads=[ptS_r, (m12_r, 1)], writes=[msk_r], tag="l1_msk")
            p.emit("dve", lambda e: e.scalar_tensor_tensor(
                msk3, msk3, -1e30, pt3, OP.mult, OP.add),
                reads=[msk_r, ptS_r], writes=[msk_r], tag="l1_t2")
            p.emit("dve", lambda e: e.reduce_max(
                m12[:, 8:16], msk3, axis=AX.X),
                reads=[msk_r], writes=[(m12_r, 2)], tag="l1_m2")
            p.emit("dve", lambda e: e.tensor_tensor(msk3, pt3, m2e, OP.is_ge),
                   reads=[ptS_r, (m12_r, 2)], writes=[msk_r], tag="l1_kp")
            p.emit("dve", lambda e: e.tensor_tensor(cwt3, pt3, msk3, OP.mult),
                   reads=[ptS_r, msk_r], writes=[cwt_r], tag="l1_cwt")
            for ggi in range(8):
                pcb, pcb_r = b.bank()
                p.emit("pe", lambda e, pcb=pcb, ggi=ggi: e.transpose(
                    pcb[0:8, 0:128], cwt[:, ggi * 8:(ggi + 1) * 8], eyef[:, :]),
                    reads=[cwt_r, "c_eyef_s"], writes=[pcb_r], tag="l1_cwT")
                p.emit("act", lambda e, pcb=pcb, ggi=ggi: e.copy(
                    cwbf[0:8, ggi * 128:(ggi + 1) * 128], pcb[0:8, 0:128]),
                    reads=[pcb_r], writes=[(cw_r, ggi)], tag="l1_cwE")
            for e_i in range(NE):
                stream_we("g", e_i, c, wg, wg_r)
                stream_we("u", e_i, c, wu, wu_r)
                stream_we("d", e_i, c, wd, wd_r)
                for hf in range(2):
                    if e_i == 0:
                        up_rhs, up_rhs_r, up_stride = xn, xn_r, TC
                        up_off = hf * 512
                    else:
                        pcw, pcw_r = b.bank()
                        p.emit("pe", lambda e, pcw=pcw, e_i=e_i, hf=hf: e.matmul(
                            pcw[:, 0:512],
                            einds[0:8, (e_i - 1) * 128:e_i * 128],
                            cwbf[0:8, hf * 512:(hf + 1) * 512],
                            start=True, stop=True),
                            reads=[cw_r, "c_einds_s"], writes=[pcw_r], tag="l1_cwb")
                        src3 = ap3(xn, (8 * TC, 128), (TC, 8), (1, 512),
                                   offset=hf * 512)
                        xe3 = ap3(xe, (8 * 512, 128), (512, 8), (1, 512))
                        cw3 = ap3(pcw, (512, 128), (0, 8), (1, 512))
                        p.emit("dve", lambda e, xe3=xe3, src3=src3, cw3=cw3:
                               e.tensor_tensor(xe3, src3, cw3, OP.mult),
                               reads=[xn_r, pcw_r], writes=[xe_r], tag="l1_xe")
                        up_rhs, up_rhs_r, up_stride = xe, xe_r, 512
                        up_off = 0
                    for it in range(16):
                        pg, pg_r = b.bank()
                        pairs = [(bass.AP(wg, k * I2 + it * 128,
                                          [[8 * I2, 128], [1, 128]]),
                                  bass.AP(xn, k * TC + hf * 512,
                                          [[8 * TC, 128], [1, 512]]),
                                  [wg_r, xn_r]) for k in range(8)]
                        b.mm_group(pg[:, 0:512], pairs, pg_r, tag="l1_g")
                        p.emit("act", lambda e, pg=pg: e.activation(
                            t1[:, :], pg[:, 0:512], AF.Silu),
                            reads=[pg_r], writes=[t1_r], tag="l1_silu")
                        pu, pu_r = b.bank()
                        pairs = [(bass.AP(wu, k * I2 + it * 128,
                                          [[8 * I2, 128], [1, 128]]),
                                  bass.AP(up_rhs, k * up_stride + up_off,
                                          [[8 * up_stride, 128], [1, 512]]),
                                  [wu_r, up_rhs_r]) for k in range(8)]
                        b.mm_group(pu[:, 0:512], pairs, pu_r, tag="l1_u")
                        p.emit("dve", lambda e, pu=pu, it=it: e.tensor_tensor(
                            hh[:, it * 512:(it + 1) * 512], pu[:, 0:512],
                            t1[:, :], OP.mult),
                            reads=[pu_r, t1_r], writes=[(hh_r, it)], tag="l1_h")
                    for m in range(8):
                        pd, pd_r = b.bank()
                        pairs = [(bass.AP(wd, k * D + m * 128,
                                          [[16 * D, 128], [1, 128]]),
                                  bass.AP(hh, k * 512, [[16 * 512, 128], [1, 512]]),
                                  [wd_r, hh_r]) for k in range(16)]
                        b.mm_group(pd[:, 0:512], pairs, pd_r, tag="l1_d")
                        p.emit("dve", lambda e, pd=pd, m=m, hf=hf:
                               e.scalar_tensor_tensor(
                                   rx[:, m * TC + hf * 512:m * TC + (hf + 1) * 512],
                                   pd[:, 0:512], 0.5,
                                   rx[:, m * TC + hf * 512:m * TC + (hf + 1) * 512],
                                   OP.mult, OP.add),
                               reads=[pd_r, rx_r],
                               writes=[(rx_r, ("y", e_i, hf, m))], tag="l1_add")
            b.dma("sp", resid_chunk_ap(resid_out, c, TC), rx[:, :],
                  reads=[rx_r], writes=[(rout_res, c)], tag=f"l1_out{c}")



def emit_l2ffn(b, resid_in, rin_res, resid_out, rout_res, dbg=None):
    nc, p = b.nc, b.p
    W = 256
    NCH = T // W
    with b.scope():
        wr = b.sb("l2f_wr", [128, 8 * 1024], BF16)
        wk = b.sb("l2f_wk", [128, 8 * 1024], BF16)
        wv = b.sb("l2f_wv", [128, 8 * 1024], BF16)
        mur = b.sb("l2f_mur", [128, 8], F32)
        muk = b.sb("l2f_muk", [128, 8], F32)
        for nm, buf in (("L2f_WrT", wr), ("L2f_WkT", wk), ("L2f_WvT", wv)):
            src = bass.AP(b.dram[nm].tensor, 0,
                          [[1024, 128], [128 * 1024, 8], [1, 1024]])
            b.dma("pool", buf[:, :], src, writes=[b.rn(buf)], tag="l2f_w")
        b.dma("sp", mur[:, :], b.dram["L2f_mur"], writes=[b.rn(mur)])
        b.dma("sp", muk[:, :], b.dram["L2f_muk"], writes=[b.rn(muk)])
        wr_r, wk_r, wv_r = b.rn(wr), b.rn(wk), b.rn(wv)
        mur_r, muk_r = b.rn(mur), b.rn(muk)

        ones = b.sb("ones", [128, 1], BF16)
        p.emit("pool", lambda e: e.memset(ones[:, :], 1.0), writes=["c_ones_s"])
        onesrow = b.sb("onesrow", [1, 128], BF16)
        p.emit("pool", lambda e: e.memset(onesrow[:, :], 1.0), writes=["c_onesrow"])

        rx = [b.sb("rx", [128, 8 * W], F32) for _ in range(2)]
        ry = [b.sb("ry", [128, 8 * W], F32) for _ in range(2)]
        xnb = [b.sb("xnb", [128, 8 * (W + 1)], BF16) for _ in range(2)]
        scr = {
            "rxbf": b.sb("rxbf", [128, 8 * W], BF16),
            "rsq": b.sb("rsq", [128, 8 * W], BF16),
            "ln_xc": b.sb("ln_xc", [128, 8 * W], F32),
            "rows": b.sb("rows", [1, 4 * W], F32),
            "rows_bf": b.sb("rows_bf", [1, 2 * W], BF16),
            "ln_tmp": b.sb("ln_tmp", [128, 8 * W], BF16),
            "pS1": b.pb[4], "pS2": b.pb[5], "pAB": b.pb[6],
            "pS1_r": b.pb_res[4], "pS2_r": b.pb_res[5], "pAB_r": b.pb_res[6],
            "onesrow": onesrow,
        }
        delta = b.sb("delta", [128, 8 * W], BF16)
        xr_m = b.sb("xr_m", [128, 8 * W], BF16)
        xk_m = b.sb("xk_m", [128, 8 * W], BF16)
        rr = b.sb("rr", [128, 8 * W], BF16)
        kr = b.sb("kr", [128, 8 * W], BF16)
        ka = b.sb("ka", [128, 8 * W], BF16)
        mu_bc_r = b.sb("mu_bc_r", [128, 8 * W], F32)
        mu_bc_k = b.sb("mu_bc_k", [128, 8 * W], F32)
        p.emit("dve", lambda e: e.tensor_copy(
            ap3(mu_bc_r, (8 * W, 128), (W, 8), (1, W)),
            ap3(mur, (8, 128), (1, 8), (0, W))),
            reads=[mur_r], writes=[b.rn(mu_bc_r)])
        p.emit("dve", lambda e: e.tensor_copy(
            ap3(mu_bc_k, (8 * W, 128), (W, 8), (1, W)),
            ap3(muk, (8, 128), (1, 8), (0, W))),
            reads=[muk_r], writes=[b.rn(mu_bc_k)])

        for c in range(NCH):
            rxc = rx[c % 2]
            ryc = ry[c % 2]
            xc = xnb[c % 2]
            xp = xnb[(c + 1) % 2]
            rxc_r, ryc_r, xc_r = b.rn(rxc), b.rn(ryc), b.rn(xc)
            xp_r = b.rn(xp)
            src = bass.AP(resid_in.tensor, c * W,
                          [[T, 128], [128 * T, 8], [1, W]])
            b.dma("sp", rxc[:, :], src, reads=[(rin_res, c)],
                  writes=[rxc_r], tag=f"l2f_rx{c}")
            xn_ap = ap3(xc, (8 * (W + 1), 128), (W + 1, 8), (1, W), offset=1)
            emit_ln(b, rxc, rxc_r, xn_ap, xc_r, W, scr, ones)
            if c == 0:
                p.emit("dve", lambda e, xc=xc: e.memset(
                    ap3(xc, (8 * (W + 1), 128), (W + 1, 8), (1, 1)), 0.0),
                    writes=[(xc_r, "b")], tag="l2f_zb")
            else:
                p.emit("dve", lambda e, xc=xc, xp=xp: e.tensor_copy(
                    ap3(xc, (8 * (W + 1), 128), (W + 1, 8), (1, 1)),
                    ap3(xp, (8 * (W + 1), 128), (W + 1, 8), (1, 1), offset=W)),
                    reads=[xp_r], writes=[(xc_r, "b")], tag="l2f_cb")
            cur = ap3(xc, (8 * (W + 1), 128), (W + 1, 8), (1, W), offset=1)
            sh = ap3(xc, (8 * (W + 1), 128), (W + 1, 8), (1, W), offset=0)
            d3 = ap3(delta, (8 * W, 128), (W, 8), (1, W))
            p.emit("dve", lambda e, d3=d3, cur=cur, sh=sh:
                   e.tensor_tensor(d3, cur, sh, OP.subtract),
                   reads=[xc_r, (xc_r, "b")], writes=[b.rn(delta)], tag="l2f_d")
            t3r = ap3(xr_m, (8 * W, 128), (W, 8), (1, W))
            t3k = ap3(xk_m, (8 * W, 128), (W, 8), (1, W))
            m3r = ap3(mu_bc_r, (8 * W, 128), (W, 8), (1, W))
            m3k = ap3(mu_bc_k, (8 * W, 128), (W, 8), (1, W))
            p.emit("dve", lambda e, t3r=t3r, d3=d3, m3r=m3r:
                   e.tensor_tensor(t3r, d3, m3r, OP.mult),
                   reads=[b.rn(delta), b.rn(mu_bc_r)], writes=[b.rn(xr_m)])
            p.emit("dve", lambda e, t3r=t3r, sh=sh: e.tensor_tensor(t3r, t3r, sh, OP.add),
                   reads=[b.rn(xr_m), xc_r, (xc_r, "b")], writes=[b.rn(xr_m)])
            p.emit("dve", lambda e, t3k=t3k, d3=d3, m3k=m3k:
                   e.tensor_tensor(t3k, d3, m3k, OP.mult),
                   reads=[b.rn(delta), b.rn(mu_bc_k)], writes=[b.rn(xk_m)])
            p.emit("dve", lambda e, t3k=t3k, sh=sh: e.tensor_tensor(t3k, t3k, sh, OP.add),
                   reads=[b.rn(xk_m), xc_r, (xc_r, "b")], writes=[b.rn(xk_m)])
            for m in range(8):
                pbk, pbk_r = b.bank()
                pairs = [(bass.AP(wr, k * 1024 + m * 128, [[8 * 1024, 128], [1, 128]]),
                          bass.AP(xr_m, k * W, [[8 * W, 128], [1, W]]),
                          [wr_r, b.rn(xr_m)]) for k in range(8)]
                b.mm_group(pbk[:, 0:W], pairs, pbk_r, tag="l2f_r")
                p.emit("act", lambda e, pbk=pbk, m=m: e.activation(
                    rr[:, m * W:(m + 1) * W], pbk[:, 0:W], AF.Sigmoid),
                    reads=[pbk_r], writes=[(b.rn(rr), m)], tag="l2f_sig")
            for m in range(8):
                pbk, pbk_r = b.bank()
                pairs = [(bass.AP(wk, k * 1024 + m * 128, [[8 * 1024, 128], [1, 128]]),
                          bass.AP(xk_m, k * W, [[8 * W, 128], [1, W]]),
                          [wk_r, b.rn(xk_m)]) for k in range(8)]
                b.mm_group(pbk[:, 0:W], pairs, pbk_r, tag="l2f_k")
                p.emit("act", lambda e, pbk=pbk, m=m: e.activation(
                    kr[:, m * W:(m + 1) * W], pbk[:, 0:W], AF.Relu),
                    reads=[pbk_r], writes=[(b.rn(kr), m)], tag="l2f_relu")
            p.emit("dve", lambda e: e.tensor_tensor(ka[:, :], kr[:, :], kr[:, :],
                                                    OP.mult),
                   reads=[b.rn(kr), (b.rn(kr), 0), (b.rn(kr), 1), (b.rn(kr), 2),
                          (b.rn(kr), 3), (b.rn(kr), 4), (b.rn(kr), 5),
                          (b.rn(kr), 6), (b.rn(kr), 7)],
                   writes=[b.rn(ka)], tag="l2f_sq")
            for m in range(8):
                pbk, pbk_r = b.bank()
                pairs = [(bass.AP(wv, k * 1024 + m * 128, [[8 * 1024, 128], [1, 128]]),
                          bass.AP(ka, k * W, [[8 * W, 128], [1, W]]),
                          [wv_r, b.rn(ka)]) for k in range(8)]
                b.mm_group(pbk[:, 0:W], pairs, pbk_r, tag="l2f_v")
                tbuf = delta
                p.emit("dve", lambda e, pbk=pbk, m=m, tbuf=tbuf: e.scalar_tensor_tensor(
                    tbuf[:, m * W:(m + 1) * W], pbk[:, 0:W], 0.5,
                    rr[:, m * W:(m + 1) * W], OP.mult, OP.mult),
                    reads=[pbk_r, (b.rn(rr), m), b.rn(delta)],
                    writes=[(b.rn(delta), ("y", m))], tag="l2f_g")
                p.emit("dve", lambda e, m=m, ryc=ryc, rxc=rxc, tbuf=tbuf: e.tensor_tensor(
                    ryc[:, m * W:(m + 1) * W], rxc[:, m * W:(m + 1) * W],
                    tbuf[:, m * W:(m + 1) * W], OP.add),
                    reads=[rxc_r, (b.rn(delta), ("y", m))],
                    writes=[(ryc_r, m)], tag="l2f_add")
            if dbg is not None:
                dbg1, dbg2, dbg3 = dbg
                ddst = bass.AP(dbg1.tensor, c * W, [[T, 128], [128 * T, 8], [1, W]])
                xn_src = ap3(xc, (8 * (W + 1), 128), (W + 1, 8), (1, W), offset=1)
                b.dma("sp", ddst, xn_src, reads=[xc_r], writes=[("dbg", c)],
                      tag=f"dbg{c}")
                d2 = bass.AP(dbg2.tensor, c * W, [[T, 128], [128 * T, 8], [1, W]])
                b.dma("sp", d2, xr_m[:, :], reads=[b.rn(xr_m)],
                      writes=[("dbg2", c)], tag=f"dbg2_{c}")
                dd3 = bass.AP(dbg3.tensor, c * W, [[T, 128], [128 * T, 8], [1, W]])
                b.dma("sp", dd3, rr[:, :], reads=[b.rn(rr)],
                      writes=[("dbg3", c)], tag=f"dbg3_{c}")
            dst = bass.AP(resid_out.tensor, c * W,
                          [[T, 128], [128 * T, 8], [1, W]])
            b.dma("sp", dst, ryc[:, :],
                  reads=[ryc_r, (ryc_r, 0), (ryc_r, 1), (ryc_r, 2), (ryc_r, 3),
                         (ryc_r, 4), (ryc_r, 5), (ryc_r, 6), (ryc_r, 7)],
                  writes=[(rout_res, c)], tag=f"l2f_out{c}")



PASSES = ["l0ffn", "l0attn", "l0ffn",
          "l1ffn", "l1attn", "l1ffn",
          "l2ffn", "l2attn", "l2ffn",
          "l3ffn", "l3attn", "l3ffn"]
MODE = "multi"


def build_program(passes=PASSES):
    nc = bass.Bass("TRN2", target_bir_lowering=False)
    b = B(nc)
    xT = b.dram_in("xT", [D, T], F32)
    out = b.dram_out("out", [D, T], F32)
    if "l0ffn" in passes:
        b.dram_in("L0f_wg", [2816, 1024]); b.dram_in("L0f_wu", [2816, 1024])
        b.dram_in("L0f_wd", [1024, 2816])
        b.dram_in("c_eye", [128, 128], BF16)
    if "l1attn" in passes:
        for nm in ("L1a_WqT", "L1a_WkT", "L1a_WvT", "L1a_WoT"):
            b.dram_in(nm, [D, D])
        b.dram_in("c_eye", [128, 128], BF16)
    if "l3attn" in passes:
        for nm in ("L3a_WqT", "L3a_WkT", "L3a_WvT", "L3a_WoT"):
            b.dram_in(nm, [D, D])
        b.dram_in("L3a_bias", [128, 32])
        b.dram_in("c_eye", [128, 128], BF16)
        b.dram_in("c_l3mask", [128, 4 * 512], BF16)
    if "l1ffn" in passes:
        b.dram_in("L1f_WgT", [9 * 1024, 2048]); b.dram_in("L1f_WuT", [9 * 1024, 2048])
        b.dram_in("L1f_WdT", [9 * 2048, 1024])
        b.dram_in("L1f_WrT", [1024, 8]); b.dram_in("L1f_rb", [8, 1])
        b.dram_in("c_einds", [8, 8 * 128], BF16)
        b.dram_in("c_eyef", [128, 128])
    if "l3ffn" in passes:
        b.dram_in("L3f_w1T", [D, 4096]); b.dram_in("L3f_w2T", [D, 4096])
        b.dram_in("L3f_h1T", [4096, 4096]); b.dram_in("L3f_h2T", [4096, 4096])
        b.dram_in("L3f_w3T", [4096, D])
        b.dram_in("L3f_sc", [128, 128]); b.dram_in("L3f_s3", [128, 8])
    if "l0attn" in passes:
        b.dram_in("L0a_WaT", [D, 256]); b.dram_in("L0a_Wab", [128, 2])
        b.dram_in("L0a_BT", [D, 256]); b.dram_in("L0a_CT", [256, D])
        b.dram_in("L0a_DT", [D, D])
    if "l2attn" in passes:
        for nm in ("L2a_WgT", "L2a_WrT", "L2a_WkT", "L2a_WvT", "L2a_WoT"):
            b.dram_in(nm, [D, D])
        b.dram_in("L2a_vecs", [128, 88])
        b.dram_in("c_ind", [128, 128], BF16)
        b.dram_in("c_indT", [16, 1024], BF16)
    if "l2ffn" in passes:
        b.dram_in("L2f_WrT", [D, D]); b.dram_in("L2f_WkT", [D, D])
        b.dram_in("L2f_WvT", [D, D])
        b.dram_in("L2f_mur", [128, 8]); b.dram_in("L2f_muk", [128, 8])

    resid = b.dram_tmp("resid", [D, T], F32)
    chain = []
    for i, name in enumerate(passes):
        rin = xT if i == 0 else resid
        rin_res = "xT" if i == 0 else "resid"
        rout = out if i == len(passes) - 1 else resid
        rout_res = "out" if i == len(passes) - 1 else "resid"
        chain.append((name, rin, rin_res, rout, rout_res))
    seen = {}
    import os
    dbg = None
    if os.environ.get("K_DBG"):
        dbg = (b.dram_out("dbg", [D, T], BF16), b.dram_out("dbg2", [D, T], BF16),
               b.dram_out("dbg3", [D, T], BF16))
    for name, rin, rin_res, rout, rout_res in chain:
        if name == "l2ffn":
            emit_l2ffn(b, rin, rin_res, rout, rout_res, dbg=dbg)
        elif name == "l2attn":
            emit_l2attn(b, rin, rin_res, rout, rout_res)
        elif name == "l0attn":
            emit_l0attn(b, rin, rin_res, rout, rout_res)
        elif name == "l3ffn":
            emit_l3ffn(b, rin, rin_res, rout, rout_res,
                       first=(seen.setdefault("l3ffn", 0) == 0))
        elif name == "l1ffn":
            emit_l1ffn(b, rin, rin_res, rout, rout_res,
                       first=(seen.setdefault("l1ffn", 0) == 0))
        elif name == "l3attn":
            emit_l3attn(b, rin, rin_res, rout, rout_res)
        elif name == "l1attn":
            emit_l1attn(b, rin, rin_res, rout, rout_res)
        elif name == "l0ffn":
            emit_l0ffn(b, rin, rin_res, rout, rout_res,
                       first=(seen.setdefault("l0ffn", 0) == 0))
        if name in seen:
            seen[name] += 1
    b.p.finish()
    return nc, b


def kernel(**inputs):
    g = prep_shared(inputs)
    x = np.asarray(inputs["L0_attn_x"])
    cores = list(range(NCORES))
    if MODE == "fused":
        nc, b = build_program(PASSES)
        in_maps = []
        for core in cores:
            m = {k: v for k, v in g.items() if k in b.input_names}
            m["xT"] = np.ascontiguousarray(x[core].T)
            in_maps.append(m)
        res = run_bass_kernel_spmd(nc, in_maps, core_ids=cores)
        b.stack.close()
        outs = [np.ascontiguousarray(r["out"].T) for r in res.results]
        return np.stack(outs, axis=0)
    progs = {}
    resid = [np.ascontiguousarray(x[core].T) for core in cores]
    for name in PASSES:
        if name not in progs:
            progs[name] = build_program([name])
        nc, b = progs[name]
        in_maps = []
        for core in cores:
            m = {k: v for k, v in g.items() if k in b.input_names}
            m["xT"] = resid[core]
            in_maps.append(m)
        res = run_bass_kernel_spmd(nc, in_maps, core_ids=cores)
        resid = [np.ascontiguousarray(r["out"]) for r in res.results]
    for nc, b in progs.values():
        b.stack.close()
    return np.stack([r.T for r in resid], axis=0)


# version 12
# speedup vs baseline: 1.0580x; 1.0580x over previous
import contextlib
import numpy as np
import concourse.bass as bass
import concourse.mybir as mybir
from concourse.bass_utils import run_bass_kernel_spmd

F32 = mybir.dt.float32
BF16 = mybir.dt.bfloat16
I32 = mybir.dt.int32
AF = mybir.ActivationFunctionType
OP = mybir.AluOpType
AX = mybir.AxisListType

D = 1024
T = 4096
NCORES = 8


ENGINES = ("sp", "pe", "act", "dve", "pool")
_ENG_ATTR = {"sp": "sync", "pe": "tensor", "act": "scalar", "dve": "vector",
             "pool": "gpsimd"}


class _Op:
    __slots__ = ("idx", "eng", "fn", "reads", "writes", "dma", "deps", "tag",
                 "marked", "cnt_after")

    def __init__(self, idx, eng, fn, reads, writes, dma, tag):
        self.idx = idx
        self.eng = eng
        self.fn = fn
        self.reads = reads
        self.writes = writes
        self.dma = dma
        self.deps = set()
        self.tag = tag
        self.marked = dma
        self.cnt_after = 0


class Prog:

    def __init__(self, nc):
        self.nc = nc
        self.ops = []
        self.state = {}

    def _match(self, name, key):
        st = self.state.get(name)
        if st is None:
            return []
        if key is None:
            return list(st.values())
        out = []
        if key in st:
            out.append(st[key])
        if None in st:
            out.append(st[None])
        return out

    def _record_read(self, name, key, idx):
        st = self.state.setdefault(name, {})
        ent = st.get(key)
        if ent is None:
            ent = st[key] = [None, set()]
        ent[1].add(idx)

    def _record_write(self, name, key, idx):
        st = self.state.setdefault(name, {})
        if key is None:
            st.clear()
        st[key] = [idx, set()]

    def emit(self, eng, fn, reads=(), writes=(), dma=False, tag=""):
        idx = len(self.ops)
        op = _Op(idx, eng, fn, list(reads), list(writes), dma, tag)
        deps = op.deps
        raw = set()
        for r in op.reads:
            n, k = r if isinstance(r, tuple) else (r, None)
            for ent in self._match(n, k):
                if ent[0] is not None:
                    raw.add(ent[0])
        ww = set()
        for w in op.writes:
            n, k = w if isinstance(w, tuple) else (w, None)
            for ent in self._match(n, k):
                if ent[0] is not None:
                    ww.add(ent[0])
                ww.update(ent[1])
        for d in ww:
            dop = self.ops[d]
            if dop.eng == eng and not dop.dma and not dma:
                continue
            deps.add(d)
        deps.update(raw)
        deps.discard(idx)
        for r in op.reads:
            n, k = r if isinstance(r, tuple) else (r, None)
            self._record_read(n, k, idx)
        for w in op.writes:
            n, k = w if isinstance(w, tuple) else (w, None)
            self._record_write(n, k, idx)
        self.ops.append(op)
        return idx

    def barrier(self):
        idx = len(self.ops)
        op = _Op(idx, "__barrier__", None, [], [], False, "barrier")
        self.ops.append(op)
        return idx

    def lower(self):
        nc = self.nc
        ops = self.ops
        for op in ops:
            if op.eng == "__barrier__":
                continue
            best = {}
            for d in op.deps:
                dop = ops[d]
                if dop.eng == op.eng == "pe" and not dop.dma:
                    continue
                key = (dop.eng, dop.dma)
                if best.get(key, -1) < d:
                    best[key] = d
            op.deps = set(best.values())
            for d in op.deps:
                ops[d].marked = True
        barrier_info = {}
        last_per_eng = {e: None for e in ENGINES}
        for op in ops:
            if op.eng == "__barrier__":
                barrier_info[op.idx] = dict(last_per_eng)
                for e in ENGINES:
                    if last_per_eng[e] is not None:
                        ops[last_per_eng[e]].marked = True
            else:
                last_per_eng[op.eng] = op.idx
        comp_cnt = {e: 0 for e in ENGINES}
        dma_cnt = {e: 0 for e in ENGINES}
        for op in ops:
            if op.eng == "__barrier__":
                continue
            if op.dma:
                dma_cnt[op.eng] += 16
                op.cnt_after = dma_cnt[op.eng]
            elif op.marked:
                comp_cnt[op.eng] += 1
                op.cnt_after = comp_cnt[op.eng]
        self._final_dma_counts = dict(dma_cnt)
        self._final_comp_counts = dict(comp_cnt)

        self._stack = contextlib.ExitStack()
        sems = {}
        for e in ENGINES:
            sems[(e, "c")] = self._stack.enter_context(nc.semaphore(f"sem_{e}_c"))
            sems[(e, "d")] = self._stack.enter_context(nc.semaphore(f"sem_{e}_d"))
        self.sems = sems

        op_clock = [None] * len(ops)
        eng_clock = {e: {} for e in ENGINES}

        def merge(dst, src):
            for k, v in src.items():
                if dst.get(k, 0) < v:
                    dst[k] = v

        streams = {e: [] for e in ENGINES}
        for op in ops:
            if op.eng == "__barrier__":
                info = barrier_info[op.idx]
                for e in ENGINES:
                    clk = eng_clock[e]
                    for e2 in ENGINES:
                        if e2 == e or info[e2] is None:
                            continue
                        dop = ops[info[e2]]
                        sk = (e2, "d" if dop.dma else "c")
                        v = dop.cnt_after
                        if clk.get(sk, 0) < v:
                            streams[e].append(("wait", sk, v))
                            clk[sk] = v
                            if op_clock[dop.idx]:
                                merge(clk, op_clock[dop.idx])
                continue
            e = op.eng
            clk = eng_clock[e]
            needed = {}
            for d in sorted(op.deps):
                dop = ops[d]
                if dop.eng == e and dop.eng == "pe" and e == "pe" and not dop.dma:
                    continue
                sk = (dop.eng, "d" if dop.dma else "c")
                v = dop.cnt_after
                if v == 0:
                    continue
                if needed.get(sk, 0) < v:
                    needed[sk] = v
                    needed[(sk, "src")] = dop.idx
            for sk in list(needed):
                if isinstance(sk, tuple) and len(sk) == 2 and sk[1] in ("c", "d"):
                    v = needed[sk]
                    if clk.get(sk, 0) >= v:
                        continue
                    streams[e].append(("wait", sk, v))
                    clk[sk] = v
                    src = needed.get((sk, "src"))
                    if src is not None and op_clock[src]:
                        merge(clk, op_clock[src])
            streams[e].append(("op", op))
            if op.marked:
                snap = dict(clk)
                snap[(e, "d" if op.dma else "c")] = op.cnt_after
                op_clock[op.idx] = snap

        self._streams = streams

    def build_block(self):
        nc = self.nc
        sems = self.sems
        streams = self._streams
        block = self._stack.enter_context(nc.Block())

        def run_stream(eng_name):
            stream = streams[eng_name]

            def body(engobj):
                for item in stream:
                    if item[0] == "wait":
                        _, sk, v = item
                        engobj.wait_ge(sems[sk], v)
                    else:
                        op = item[1]
                        inst = op.fn(engobj)
                        if op.dma:
                            inst.then_inc(sems[(op.eng, "d")], 16)
                        elif op.marked:
                            inst.then_inc(sems[(op.eng, "c")], 1)
            return body

        for e in ENGINES:
            if not streams[e]:
                continue
            getattr(block, _ENG_ATTR[e])(run_stream(e))

    def finish(self):
        self.lower()
        self.build_block()
        self._stack.close()



EPS = 1e-5


class B:

    def __init__(self, nc):
        self.nc = nc
        self.p = Prog(nc)
        self.stack = contextlib.ExitStack()
        self.scopes = []
        self.dram = {}
        self._uid = 0
        self.pb = [self.stack.enter_context(nc.psum_tensor(f"pb{i}", [128, 512], F32))
                   for i in range(7)]
        self.pb_res = [f"pb{i}" for i in range(7)]
        self.pbt = self.stack.enter_context(nc.psum_tensor("pbt", [128, 1024], BF16))
        self.pbt_res = "pbt"
        self._pb_rr = 0
        self.input_names = set()
        self.const_buf = self.stack.enter_context(
            nc.sbuf_tensor("constbuf", [128, 16], F32))
        self._consts = {}
        self._ring_n = 0

    def uid(self):
        self._uid += 1
        return self._uid

    def dram_in(self, name, shape, dtype=F32):
        if name not in self.dram:
            self.dram[name] = self.nc.dram_tensor(name, shape, dtype,
                                                  kind="ExternalInput").ap()
            self.input_names.add(name)
        return self.dram[name]

    def dram_out(self, name, shape, dtype=F32):
        if name not in self.dram:
            self.dram[name] = self.nc.dram_tensor(name, shape, dtype,
                                                  kind="ExternalOutput").ap()
        return self.dram[name]

    def dram_tmp(self, name, shape, dtype=F32):
        if name not in self.dram:
            self.dram[name] = self.nc.dram_tensor(name, shape, dtype).ap()
        return self.dram[name]

    def sb(self, name, shape, dtype):
        st = self.scopes[-1] if self.scopes else self.stack
        name = f"{name}_{self.uid()}"
        h = st.enter_context(self.nc.sbuf_tensor(name, shape, dtype))
        return h

    @contextlib.contextmanager
    def scope(self):
        st = contextlib.ExitStack()
        self.scopes.append(st)
        try:
            yield st
        except BaseException:
            import traceback
            traceback.print_exc()
            raise
        finally:
            self.scopes.pop()
            try:
                st.close()
            except Exception:
                pass
            self.p.barrier()

    def ring_key(self):
        k = ("pool_ring", self._ring_n % 48)
        self._ring_n += 1
        return k

    def constf(self, v, p=128):
        v = float(v)
        if v not in self._consts:
            i = len(self._consts)
            assert i < 16
            self._consts[v] = i
            cb = self.const_buf
            self.p.emit("pool", lambda e: e.memset(cb[:, i:i + 1], v),
                        writes=[("c_const", i)], tag="const")
        i = self._consts[v]
        return self.const_buf[0:p, i:i + 1]

    def bank(self):
        i = self._pb_rr
        self._pb_rr = (self._pb_rr + 1) % 4
        return self.pb[i], self.pb_res[i]

    def rn(self, h):
        return h.name

    def dma(self, eng, dst, src, reads=(), writes=(), tag="", slow=False):
        if eng == "pool":
            writes = list(writes) + [self.ring_key()]
        if slow:
            self.p.emit(eng,
                        lambda e: e.dma_start(dst, src,
                                              allow_slow_non_contiguous=True),
                        reads=reads, writes=writes, dma=True, tag=tag)
        else:
            self.p.emit(eng, lambda e: e.dma_start(dst, src),
                        reads=reads, writes=writes, dma=True, tag=tag)

    def mm_group(self, psum_ap, pairs, psum_res, extra_reads=(), tag="mm",
                 accumulate=False):
        n = len(pairs)
        for i, (lt, rh, rds) in enumerate(pairs):
            st = (i == 0) and not accumulate
            sp = (i == n - 1)
            self.p.emit("pe",
                        lambda e, lt=lt, rh=rh, st=st, sp=sp:
                        e.matmul(psum_ap, lt, rh, start=st, stop=sp),
                        reads=list(rds) + list(extra_reads),
                        writes=[psum_res], tag=tag)


def bf16_transpose(e, nc, out, in_, ident):
    with nc.allow_low_precision(reason="bit-exact bf16 transpose via PE"):
        return e.transpose(out, in_, ident)


def lp(nc, fn):
    with nc.allow_low_precision(reason="bf16 operand rounding (honest bf16)"):
        return fn()


def ap3(handle_or_ap, pstep_count, mid, inner, offset=0):
    t = handle_or_ap
    return bass.AP(t, offset, [list(pstep_count), list(mid), list(inner)])



def _bf(x):
    import ml_dtypes
    return np.ascontiguousarray(np.asarray(x, dtype=np.float32).astype(ml_dtypes.bfloat16))


def _t(x):
    return np.ascontiguousarray(np.asarray(x).T)


def _tile_major(v):
    v = np.asarray(v)
    return np.ascontiguousarray(v.reshape(-1, 128).T)


def prep_shared(inputs):
    import ml_dtypes
    g = {}
    g["c_eye"] = np.eye(128, dtype=ml_dtypes.bfloat16)
    g["c_ones"] = np.ones((128, 1), dtype=ml_dtypes.bfloat16)

    g["L0f_wg"] = np.ascontiguousarray(np.asarray(inputs["L0_ffn_w_gate"],
                                                  dtype=np.float32))
    g["L0f_wu"] = np.ascontiguousarray(np.asarray(inputs["L0_ffn_w_up"],
                                                  dtype=np.float32))
    g["L0f_wd"] = np.ascontiguousarray(np.asarray(inputs["L0_ffn_w_down"],
                                                  dtype=np.float32))

    g["L1a_WqT"] = _t(inputs["L1_attn_Wq"])
    g["L1a_WkT"] = _t(inputs["L1_attn_Wk"])
    g["L1a_WvT"] = _t(inputs["L1_attn_Wv"])
    g["L1a_WoT"] = _t(inputs["L1_attn_Wo"])

    wq3 = np.asarray(inputs["L3_attn_Wq"]).transpose(1, 0, 2).reshape(1024, 1024)
    wk3 = np.asarray(inputs["L3_attn_Wk"]).transpose(1, 0, 2).reshape(1024, 1024)
    wv3 = np.asarray(inputs["L3_attn_Wv"]).transpose(1, 0, 2).reshape(1024, 1024)
    g["L3a_WqT"] = np.ascontiguousarray(wq3)
    g["L3a_WkT"] = np.ascontiguousarray(wk3)
    g["L3a_WvT"] = np.ascontiguousarray(wv3)
    g["L3a_WoT"] = _t(inputs["L3_attn_Wo"])
    g["L3a_bias"] = np.concatenate(
        [_tile_major(np.asarray(inputs["L3_attn_bq"]).reshape(1024)),
         _tile_major(np.asarray(inputs["L3_attn_bk"]).reshape(1024)),
         _tile_major(np.asarray(inputs["L3_attn_bv"]).reshape(1024)),
         _tile_major(np.asarray(inputs["L3_attn_bo"]))], axis=1)
    mk4 = np.zeros((128, 4 * 512), np.float32)
    for dv in range(4):
        for i in range(128):
            mk4[i, dv * 512:(dv + 1) * 512] = (np.arange(512) >= (dv * 128 + i))
    g["c_l3mask"] = _bf(mk4)

    wg_all = np.stack([_t(inputs["L1_ffn_Wg_s"])] +
                      [_t(np.asarray(inputs["L1_ffn_Wg"])[e]) for e in range(8)])
    wu_all = np.stack([_t(inputs["L1_ffn_Wu_s"])] +
                      [_t(np.asarray(inputs["L1_ffn_Wu"])[e]) for e in range(8)])
    wd_all = np.stack([_t(inputs["L1_ffn_Wd_s"])] +
                      [_t(np.asarray(inputs["L1_ffn_Wd"])[e]) for e in range(8)])
    g["L1f_WgT"] = np.ascontiguousarray(wg_all.reshape(9 * 1024, 2048))
    g["L1f_WuT"] = np.ascontiguousarray(wu_all.reshape(9 * 1024, 2048))
    g["L1f_WdT"] = np.ascontiguousarray(wd_all.reshape(9 * 2048, 1024))
    g["L1f_WrT"] = _t(inputs["L1_ffn_Wr"])
    g["L1f_rb"] = np.ascontiguousarray(np.asarray(inputs["L1_ffn_rb"],
                                                  dtype=np.float32).reshape(8, 1))
    eind = np.zeros((8, 8 * 128), np.float32)
    for e in range(8):
        eind[e, e * 128:(e + 1) * 128] = 1.0
    g["c_einds"] = _bf(eind)
    g["c_eyef"] = np.eye(128, dtype=np.float32)
    tk = (np.arange(32)[None, :] * 128 + np.arange(128)[:, None]).astype(np.uint32)
    g["c_tok128"] = np.ascontiguousarray(np.repeat(tk, 4, axis=1))
    jj = np.arange(2048)
    pad = (4096 + jj // 16).astype(np.uint32)
    g["c_padidx"] = np.ascontiguousarray(
        np.broadcast_to(np.repeat(pad, 4)[None, :], (8, 2048 * 4)))
    g["c_iotaT8"] = np.ascontiguousarray(
        np.broadcast_to(np.arange(4096).astype(np.uint32), (8, 4096)))

    g["L3f_w1T"] = _t(inputs["L3_ffn_w1_w"])
    g["L3f_w2T"] = _t(inputs["L3_ffn_w2_w"])
    g["L3f_h1T"] = _t(inputs["L3_ffn_h1_w"])
    g["L3f_h2T"] = _t(inputs["L3_ffn_h2_w"])
    g["L3f_w3T"] = _t(inputs["L3_ffn_w3_w"])
    g["L3f_sc"] = np.concatenate([_tile_major(inputs["L3_ffn_w1_s"]),
                                  _tile_major(inputs["L3_ffn_w2_s"]),
                                  _tile_major(inputs["L3_ffn_h1_s"]),
                                  _tile_major(inputs["L3_ffn_h2_s"])], axis=1)
    g["L3f_s3"] = _tile_major(inputs["L3_ffn_w3_s"])

    g["L0a_WaT"] = _t(inputs["L0_attn_Wa_w"])
    g["L0a_Wab"] = _tile_major(inputs["L0_attn_Wa_b"])
    g["L0a_BT"] = _t(inputs["L0_attn_B_w"])
    g["L0a_CT"] = _t(inputs["L0_attn_C_w"])
    g["L0a_DT"] = _t(inputs["L0_attn_D_w"])

    g["L2a_WgT"] = _t(inputs["L2_attn_Wg_w"])
    g["L2a_WrT"] = _t(inputs["L2_attn_Wr_w"])
    g["L2a_WkT"] = _t(inputs["L2_attn_Wk_w"])
    g["L2a_WvT"] = _t(inputs["L2_attn_Wv_w"])
    g["L2a_WoT"] = _t(inputs["L2_attn_Wo_w"])
    vec_list = [inputs["L2_attn_Wg_b"], inputs["L2_attn_Wr_b"],
                inputs["L2_attn_Wk_b"], inputs["L2_attn_Wv_b"],
                inputs["L2_attn_Wo_b"], inputs["L2_attn_mg"],
                inputs["L2_attn_mr"], inputs["L2_attn_mk"],
                inputs["L2_attn_mv"],
                np.asarray(inputs["L2_attn_w"]).reshape(1024),
                np.asarray(inputs["L2_attn_u"]).reshape(1024)]
    g["L2a_vecs"] = np.concatenate([_tile_major(v) for v in vec_list], axis=1)
    feats = np.arange(1024)
    heads = feats // 64
    ind = np.zeros((1024, 16), np.float32)
    ind[feats, heads] = 1.0
    indh = np.zeros((128, 128), np.float32)
    for i in range(8):
        indh[:, i * 16:(i + 1) * 16] = ind[i * 128:(i + 1) * 128, :]
    g["c_ind"] = _bf(indh)
    indT = np.zeros((16, 1024), np.float32)
    for i in range(8):
        indT[:, i * 128:(i + 1) * 128] = ind[i * 128:(i + 1) * 128, :].T
    g["c_indT"] = _bf(indT)

    g["L2f_WrT"] = _t(inputs["L2_ffn_W_r"])
    g["L2f_WkT"] = _t(inputs["L2_ffn_W_k"])
    g["L2f_WvT"] = _t(inputs["L2_ffn_W_v"])
    g["L2f_mur"] = _tile_major(inputs["L2_ffn_mu_r"])
    g["L2f_muk"] = _tile_major(inputs["L2_ffn_mu_k"])
    return g



def emit_ln(b, rx, rx_res, xn_ap, xn_res, W, scr, ones, rx_fw=None,
            rx_stride=None, rx_off=0):
    p = b.p
    if rx_fw is None:
        rx_fw = 8 * W
    if rx_stride is None:
        rx_stride = W
    rx3 = ap3(rx, (rx_fw, 128), (rx_stride, 8), (1, W), offset=rx_off)
    rxbf, rsq, rows, rows_bf = scr["rxbf"], scr["rsq"], scr["rows"], scr["rows_bf"]
    xc = scr["ln_xc"]
    pS1, pS2, pAB = scr["pS1"], scr["pS2"], scr["pAB"]
    onesrow = scr["onesrow"]
    rxbf_r, rsq_r = b.rn(rxbf), b.rn(rsq)
    xc_r = b.rn(xc)
    rows_r, rowsbf_r = b.rn(rows), b.rn(rows_bf)
    pS1_r, pS2_r, pAB_r = scr["pS1_r"], scr["pS2_r"], scr["pAB_r"]

    def R(i):
        return rows[0:1, i * W:(i + 1) * W]

    def RB(i):
        return rows_bf[0:1, i * W:(i + 1) * W]

    rxbf3 = ap3(rxbf, (8 * W, 128), (W, 8), (1, W))
    rsq3 = ap3(rsq, (8 * W, 128), (W, 8), (1, W))
    xc3 = ap3(xc, (8 * W, 128), (W, 8), (1, W))
    p.emit("act", lambda e: e.copy(rxbf3, rx3), reads=[rx_res],
           writes=[rxbf_r], tag="ln_cp")
    pairs1 = [(ones[:, :], bass.AP(rxbf, k * W, [[8 * W, 128], [1, W]]),
               [rxbf_r]) for k in range(8)]
    b.mm_group(pS1[0:1, 0:W], pairs1, pS1_r, extra_reads=["c_ones_s"], tag="ln_s1")
    p.emit("dve", lambda e: e.tensor_scalar(RB(0), pS1[0:1, 0:W],
                                            1.0 / D, None, OP.mult),
           reads=[pS1_r], writes=[(rowsbf_r, 0)], tag="ln_m0")
    p.emit("pe", lambda e: e.matmul(pAB[:, 0:W], onesrow[0:1, :], RB(0),
                                    start=True, stop=True),
           reads=[(rowsbf_r, 0), "c_onesrow"], writes=[pAB_r], tag="ln_mb")
    mb3 = ap3(pAB, (512, 128), (0, 8), (1, W))
    p.emit("dve", lambda e: e.tensor_tensor(xc3, rx3, mb3, OP.subtract),
           reads=[rx_res, pAB_r], writes=[xc_r], tag="ln_ctr")
    p.emit("act", lambda e: e.copy(rxbf3, xc3), reads=[xc_r],
           writes=[rxbf_r], tag="ln_cp2")
    p.emit("act", lambda e: e.activation(rsq3, xc3, AF.Square),
           reads=[xc_r], writes=[rsq_r], tag="ln_sq")
    pairs1b = [(ones[:, :], bass.AP(rxbf, k * W, [[8 * W, 128], [1, W]]),
                [rxbf_r]) for k in range(8)]
    pairs2 = [(ones[:, :], bass.AP(rsq, k * W, [[8 * W, 128], [1, W]]),
               [rsq_r]) for k in range(8)]
    b.mm_group(pS1[0:1, 0:W], pairs1b, pS1_r, extra_reads=["c_ones_s"],
               tag="ln_s1b")
    b.mm_group(pS2[0:1, 0:W], pairs2, pS2_r, extra_reads=["c_ones_s"],
               tag="ln_s2")
    p.emit("dve", lambda e: e.tensor_scalar(R(0), pS1[0:1, 0:W],
                                            1.0 / D, None, OP.mult),
           reads=[pS1_r], writes=[(rows_r, 0)], tag="ln_dm")
    p.emit("dve", lambda e: e.tensor_scalar(R(1), pS2[0:1, 0:W],
                                            1.0 / D, None, OP.mult),
           reads=[pS2_r], writes=[(rows_r, 1)], tag="ln_ex2")
    p.emit("dve", lambda e: e.tensor_tensor(R(2), R(0), R(0), OP.mult),
           reads=[(rows_r, 0)], writes=[(rows_r, 2)], tag="ln_msq")
    p.emit("dve", lambda e: e.tensor_tensor(R(1), R(1), R(2), OP.subtract),
           reads=[(rows_r, 1), (rows_r, 2)], writes=[(rows_r, 1)], tag="ln_v")
    p.emit("dve", lambda e: e.tensor_scalar(R(1), R(1), 0.0, None, OP.max),
           reads=[(rows_r, 1)], writes=[(rows_r, 1)], tag="ln_v0")
    epsap = b.constf(EPS, p=1)
    p.emit("act", lambda e: e.activation(R(3), R(1), AF.Sqrt, bias=epsap),
           reads=[(rows_r, 1), ("c_const", b._consts[EPS])],
           writes=[(rows_r, 3)], tag="ln_sqrt")
    p.emit("dve", lambda e: e.reciprocal(R(2), R(3)),
           reads=[(rows_r, 3)], writes=[(rows_r, 2)], tag="ln_rstd")
    p.emit("dve", lambda e: e.tensor_copy(RB(0), R(2)),
           reads=[(rows_r, 2)], writes=[(rowsbf_r, 0)], tag="ln_bf0")
    p.emit("dve", lambda e: e.scalar_tensor_tensor(RB(1), R(0), -1.0, R(2),
                                                   OP.mult, OP.mult),
           reads=[(rows_r, 0), (rows_r, 2)], writes=[(rowsbf_r, 1)], tag="ln_bf1")
    p.emit("pe", lambda e: e.matmul(pAB[:, 0:W], onesrow[0:1, :], RB(0),
                                    start=True, stop=True),
           reads=[(rowsbf_r, 0), "c_onesrow"], writes=[pAB_r], tag="ln_bA")
    p.emit("pe", lambda e: e.matmul(pAB[:, W:2 * W], onesrow[0:1, :], RB(1),
                                    start=True, stop=True),
           reads=[(rowsbf_r, 1), "c_onesrow"], writes=[pAB_r], tag="ln_bB")
    if "ln_tmp32" in scr:
        tmp = scr["ln_tmp32"]
        src3 = xc3
        src_r = xc_r
    else:
        tmp = scr["ln_tmp"]
        src3 = ap3(rxbf, (8 * W, 128), (W, 8), (1, W))
        src_r = rxbf_r
    tmp_r = b.rn(tmp)
    Aap = ap3(pAB, (512, 128), (0, 8), (1, W))
    Bap = ap3(pAB, (512, 128), (0, 8), (1, W), offset=W)
    tmp3 = ap3(tmp, (8 * W, 128), (W, 8), (1, W))
    p.emit("dve", lambda e: e.tensor_tensor(tmp3, src3, Aap, OP.mult),
           reads=[src_r, pAB_r], writes=[tmp_r], tag="ln_ap1")
    p.emit("dve", lambda e: e.tensor_tensor(xn_ap, tmp3, Bap, OP.add),
           reads=[tmp_r, pAB_r], writes=[xn_res], tag="ln_ap2")



def resid_chunk_ap(dram_ap, c, W):
    return bass.AP(dram_ap.tensor, c * W, [[T, 128], [128 * T, 8], [1, W]])


def load_w2mb(b, dram_name, buf, res, kdim=8, mdim=1024, eng="pool"):
    src = bass.AP(b.dram[dram_name].tensor, 0,
                  [[mdim, 128], [128 * mdim, kdim], [1, mdim]])
    b.dma(eng, buf[:, :], src, writes=[res], tag=f"w_{dram_name}")


def emit_mix3(b, cur, sh, d3, delta_res, mu_bc, mu_res, out3, out_res, deps):
    p = b.p
    p.emit("dve", lambda e, d3=d3, cur=cur, sh=sh:
           e.tensor_tensor(d3, cur, sh, OP.subtract),
           reads=list(deps), writes=[delta_res], tag="mix_d")
    p.emit("dve", lambda e, out3=out3, d3=d3, mu_bc=mu_bc:
           e.tensor_tensor(out3, d3, mu_bc, OP.mult),
           reads=[delta_res, mu_res], writes=[out_res], tag="mix_m")
    p.emit("dve", lambda e, out3=out3, sh=sh:
           e.tensor_tensor(out3, out3, sh, OP.add),
           reads=[out_res] + list(deps), writes=[out_res], tag="mix_a")


def proj_group(b, w_sb, w_res, rhs_sb, rhs_res, m, W, nk=8, rhs_stride=None,
               wm=1024, psum=None, rhs_off=0, rhs_fw=None):
    if rhs_stride is None:
        rhs_stride = W
    if rhs_fw is None:
        rhs_fw = nk * rhs_stride
    if psum is None:
        pbk, pbk_r = b.bank()
    else:
        pbk, pbk_r = psum
    pairs = [(bass.AP(w_sb, k * wm + m * 128, [[nk * wm, 128], [1, 128]]),
              bass.AP(rhs_sb, k * rhs_stride + rhs_off, [[rhs_fw, 128], [1, W]]),
              [w_res, rhs_res]) for k in range(nk)]
    b.mm_group(pbk[:, 0:W], pairs, pbk_r, tag="proj")
    return pbk, pbk_r


def expand_cols(b, src, src_off, src_w, src_res, dst, dst_res, n, W):
    b.p.emit("dve", lambda e: e.tensor_copy(
        ap3(dst, (n * W, 128), (W, n), (1, W)),
        ap3(src, (src_w, 128), (1, n), (0, W), offset=src_off)),
        reads=[src_res], writes=[dst_res], tag="expand")


def make_ln_scr(b, W, ones=None, onesrow=None, xc_buf=None):
    if onesrow is None:
        onesrow = b.sb("onesrow", [1, 128], BF16)
        b.p.emit("pool", lambda e: e.memset(onesrow[:, :], 1.0),
                 writes=["c_onesrow"])
    if ones is None:
        ones = b.sb("ones", [128, 1], BF16)
        b.p.emit("pool", lambda e: e.memset(ones[:, :], 1.0),
                 writes=["c_ones_s"])
    return {
        "rxbf": b.sb("rxbf", [128, 8 * W], BF16),
        "rsq": b.sb("rsq", [128, 8 * W], BF16),
        "ln_xc": xc_buf if xc_buf is not None else b.sb("ln_xc", [128, 8 * W], F32),
        "rows": b.sb("rows", [1, 4 * W], F32),
        "rows_bf": b.sb("rows_bf", [1, 2 * W], BF16),
        "ln_tmp": b.sb("ln_tmp", [128, 8 * W], BF16),
        "pS1": b.pb[4], "pS2": b.pb[5], "pAB": b.pb[6],
        "pS1_r": b.pb_res[4], "pS2_r": b.pb_res[5], "pAB_r": b.pb_res[6],
        "onesrow": onesrow, "ones": ones,
    }



def emit_l2attn(b, resid_in, rin_res, resid_out, rout_res):
    p = b.p
    W = 256
    NCH = T // W
    xns = b.dram_tmp("xn_scr", [D, T], BF16)
    with b.scope():
        wg = b.sb("l2a_wg", [128, 8 * 1024], BF16)
        wrr = b.sb("l2a_wr", [128, 8 * 1024], BF16)
        wk = b.sb("l2a_wk", [128, 8 * 1024], BF16)
        wv = b.sb("l2a_wv", [128, 8 * 1024], BF16)
        wo = b.sb("l2a_wo", [128, 8 * 1024], BF16)
        wbufs = {"g": (wg, b.rn(wg)), "r": (wrr, b.rn(wrr)), "k": (wk, b.rn(wk)),
                 "v": (wv, b.rn(wv)), "o": (wo, b.rn(wo))}
        for nm, key in (("L2a_WgT", "g"), ("L2a_WrT", "r"), ("L2a_WkT", "k"),
                        ("L2a_WvT", "v"), ("L2a_WoT", "o")):
            load_w2mb(b, nm, wbufs[key][0], wbufs[key][1])
        vecs = b.sb("l2a_vecs", [128, 8 * 11], F32)
        b.dma("sp", vecs[:, :], b.dram["L2a_vecs"], writes=["l2a_vecs"])
        def V(i):
            return vecs[:, i * 8:(i + 1) * 8]
        BG, BR, BK, BV, BO, MG, MR, MK, MV, WW, UU = range(11)
        dec = b.sb("l2a_dec", [128, 8], F32)
        p.emit("act", lambda e: e.activation(dec[:, :], V(WW), AF.Exp),
               reads=["l2a_vecs"], writes=["l2a_dec"])
        p.emit("act", lambda e: e.activation(dec[:, :], dec[:, :], AF.Exp,
                                             scale=-1.0),
               reads=["l2a_dec"], writes=["l2a_dec"])
        ind = b.sb("l2a_ind", [128, 8 * 16], BF16)
        b.dma("sp", ind[:, :], b.dram["c_ind"], writes=["c_ind_s"])
        indT = b.sb("l2a_indT", [16, 8 * 128], BF16)
        b.dma("sp", indT[:, :], b.dram["c_indT"], writes=["c_indT_s"])

        scr = make_ln_scr(b, W)
        ones = scr["ones"]

        rx = [b.sb("rx", [128, 8 * W], F32) for _ in range(2)]
        xnc = [b.sb("xnc", [128, 8 * W], BF16) for _ in range(2)]
        kvs = b.dram_tmp("kv_scr", [D, T], BF16)

        for c in range(NCH):
            rxc = rx[c % 2]
            xc = xnc[c % 2]
            rxc_r, xc_r = b.rn(rxc), b.rn(xc)
            b.dma("sp", rxc[:, :], resid_chunk_ap(resid_in, c, W),
                  reads=[(rin_res, c)], writes=[rxc_r], tag=f"l2a_rx{c}")
            xn_ap = ap3(xc, (8 * W, 128), (W, 8), (1, W))
            emit_ln(b, rxc, rxc_r, xn_ap, xc_r, W, scr, ones)
            b.dma("sp", resid_chunk_ap(xns, c, W), xc[:, :],
                  reads=[xc_r], writes=[("xn_scr", c)], tag=f"l2a_xn{c}")

        xb = [b.sb("xb", [128, 8 * (W + 1)], BF16) for _ in range(2)]
        delta = b.sb("delta", [128, 8 * W], BF16)
        mix_t = b.sb("mix_t", [128, 8 * W], BF16)
        mix_t2 = b.sb("mix_t2", [128, 8 * W], BF16)
        kk = b.sb("kk", [128, 8 * W], BF16)
        vv = b.sb("vv", [128, 8 * W], BF16)
        kvc = b.sb("kvc", [128, 8 * W], BF16)
        mu_bc = b.sb("mu_bc", [128, 8 * W], BF16)
        mu_bc2 = b.sb("mu_bc2", [128, 8 * W], BF16)
        expand_cols(b, vecs, MK * 8, 88, "l2a_vecs", mu_bc, "mu_bck", 8, W)
        expand_cols(b, vecs, MV * 8, 88, "l2a_vecs", mu_bc2, "mu_bcv", 8, W)
        for c in range(NCH):
            xc = xb[c % 2]
            xc_r = b.rn(xc)
            if c == 0:
                src0 = bass.AP(xns.tensor, 0, [[T, 128], [128 * T, 8], [1, W]])
                b.dma("sp", ap3(xc, (8 * (W + 1), 128), (W + 1, 8), (1, W), offset=1),
                      src0, reads=[("xn_scr", 0)], writes=[(xc_r, "m")],
                      tag="l2a_xb0")
                srcw = bass.AP(xns.tensor, T - 1, [[T, 128], [128 * T, 8], [1, 1]])
                b.dma("sp", ap3(xc, (8 * (W + 1), 128), (W + 1, 8), (1, 1)),
                      srcw, reads=[("xn_scr", NCH - 1)], writes=[(xc_r, "b")],
                      tag="l2a_xbw", slow=True)
            else:
                src = bass.AP(xns.tensor, c * W - 1,
                              [[T, 128], [128 * T, 8], [1, W + 1]])
                b.dma("sp", xc[:, :], src,
                      reads=[("xn_scr", c), ("xn_scr", c - 1)],
                      writes=[(xc_r, "m"), (xc_r, "b")], tag=f"l2a_xb{c}")
            cur = ap3(xc, (8 * (W + 1), 128), (W + 1, 8), (1, W), offset=1)
            sh = ap3(xc, (8 * (W + 1), 128), (W + 1, 8), (1, W), offset=0)
            d3 = ap3(delta, (8 * W, 128), (W, 8), (1, W))
            t3 = ap3(mix_t, (8 * W, 128), (W, 8), (1, W))
            t3b = ap3(mix_t2, (8 * W, 128), (W, 8), (1, W))
            m3 = ap3(mu_bc, (8 * W, 128), (W, 8), (1, W))
            m3b = ap3(mu_bc2, (8 * W, 128), (W, 8), (1, W))
            emit_mix3(b, cur, sh, d3, b.rn(delta), m3, "mu_bck", t3, b.rn(mix_t),
                      [(xc_r, "m"), (xc_r, "b")])
            p.emit("dve", lambda e, t3b=t3b, d3=d3, m3b=m3b:
                   e.tensor_tensor(t3b, d3, m3b, OP.mult),
                   reads=[b.rn(delta), "mu_bcv"], writes=[b.rn(mix_t2)])
            p.emit("dve", lambda e, t3b=t3b, sh=sh:
                   e.tensor_tensor(t3b, t3b, sh, OP.add),
                   reads=[b.rn(mix_t2), (xc_r, "m"), (xc_r, "b")],
                   writes=[b.rn(mix_t2)])
            for m in range(8):
                pk, pk_r = proj_group(b, wk, wbufs["k"][1], mix_t, b.rn(mix_t), m, W)
                p.emit("act", lambda e, pk=pk, m=m: e.activation(
                    kk[:, m * W:(m + 1) * W], pk[:, 0:W], AF.Identity,
                    bias=vecs[:, BK * 8 + m:BK * 8 + m + 1]),
                    reads=[pk_r, "l2a_vecs"], writes=[(b.rn(kk), m)], tag="l2a_bk")
            for m in range(8):
                pv, pv_r = proj_group(b, wv, wbufs["v"][1], mix_t2, b.rn(mix_t2), m, W)
                p.emit("act", lambda e, pv=pv, m=m: e.activation(
                    vv[:, m * W:(m + 1) * W], pv[:, 0:W], AF.Identity,
                    bias=vecs[:, BV * 8 + m:BV * 8 + m + 1]),
                    reads=[pv_r, "l2a_vecs"], writes=[(b.rn(vv), m)], tag="l2a_bv")
            p.emit("dve", lambda e: e.tensor_tensor(
                kvc[:, :], kk[:, :], vv[:, :], OP.mult),
                reads=[b.rn(kk), b.rn(vv)], writes=[b.rn(kvc)],
                tag="l2a_kv")
            b.dma("sp", resid_chunk_ap(kvs, c, W), kvc[:, :],
                  reads=[b.rn(kvc)], writes=[("kv_scr", ("kv", c))],
                  tag=f"l2a_kvo{c}")

        accs = b.sb("accs", [128, T], BF16)
        kvt = [b.sb("kvt", [128, T], BF16)]
        for i in range(8):
            kvti = kvt[0]
            kvti_r = b.rn(kvti)
            tile_ap = bass.AP(kvs.tensor, i * 128 * T, [[T, 128], [1, T]])
            b.dma("sp", kvti[:, :], tile_ap,
                  reads=["kv_scr"],
                  writes=[kvti_r], tag=f"l2a_kvl{i}")
            dec0 = bass.AP(dec, i, [[8, 128], [0, T]])
            p.emit("dve", lambda e, kvti=kvti, dec0=dec0:
                   e.tensor_tensor_scan(accs[:, :], dec0, kvti[:, :], 0.0,
                                        OP.mult, OP.add),
                   reads=[kvti_r, "l2a_dec"], writes=["l2a_accs"], tag="l2a_scan")
            p.emit("dve", lambda e, kvti=kvti, i=i: e.scalar_tensor_tensor(
                kvti[:, :], kvti[:, :], vecs[:, UU * 8 + i:UU * 8 + i + 1],
                accs[:, :], OP.mult, OP.add),
                reads=["l2a_accs", kvti_r, "l2a_vecs"], writes=[kvti_r],
                tag="l2a_wkv")
            b.dma("sp", tile_ap, kvti[:, :],
                  reads=[kvti_r], writes=[("kv_scr", ("wkv", i))],
                  tag=f"l2a_wkvo{i}")

        rr = kk
        sg = vv
        rw = b.sb("rw", [128, 8 * W], BF16)
        rwsq = delta
        ot = b.sb("ot", [128, 8 * W], BF16)
        grows = b.sb("grows", [16, 4 * W], F32)
        grows_bf = b.sb("growsbf", [16, 2 * W], BF16)
        expand_cols(b, vecs, MG * 8, 88, "l2a_vecs", mu_bc, "mu_bck", 8, W)
        expand_cols(b, vecs, MR * 8, 88, "l2a_vecs", mu_bc2, "mu_bcv", 8, W)
        eps_c = b.constf(EPS, p=16)
        for c in range(NCH):
            xc = xb[c % 2]
            xc_r = b.rn(xc)
            rxc = rx[c % 2]
            ryc = rxc
            rxc_r = b.rn(rxc)
            ryc_r = rxc_r
            if c == 0:
                src0 = bass.AP(xns.tensor, 0, [[T, 128], [128 * T, 8], [1, W]])
                b.dma("sp", ap3(xc, (8 * (W + 1), 128), (W + 1, 8), (1, W), offset=1),
                      src0, reads=[("xn_scr", 0)], writes=[(xc_r, "m")], tag="l2a_2b0")
                srcw = bass.AP(xns.tensor, T - 1, [[T, 128], [128 * T, 8], [1, 1]])
                b.dma("sp", ap3(xc, (8 * (W + 1), 128), (W + 1, 8), (1, 1)),
                      srcw, reads=[("xn_scr", NCH - 1)], writes=[(xc_r, "b")],
                      tag="l2a_2bw", slow=True)
            else:
                src = bass.AP(xns.tensor, c * W - 1,
                              [[T, 128], [128 * T, 8], [1, W + 1]])
                b.dma("sp", xc[:, :], src,
                      reads=[("xn_scr", c), ("xn_scr", c - 1)],
                      writes=[(xc_r, "m"), (xc_r, "b")], tag=f"l2a_2b{c}")
            b.dma("sp", rxc[:, :], resid_chunk_ap(resid_in, c, W),
                  reads=[(rin_res, c)], writes=[rxc_r], tag=f"l2a_2rx{c}")
            cur = ap3(xc, (8 * (W + 1), 128), (W + 1, 8), (1, W), offset=1)
            sh = ap3(xc, (8 * (W + 1), 128), (W + 1, 8), (1, W), offset=0)
            d3 = ap3(delta, (8 * W, 128), (W, 8), (1, W))
            t3 = ap3(mix_t, (8 * W, 128), (W, 8), (1, W))
            t3b = ap3(mix_t2, (8 * W, 128), (W, 8), (1, W))
            m3 = ap3(mu_bc, (8 * W, 128), (W, 8), (1, W))
            m3b = ap3(mu_bc2, (8 * W, 128), (W, 8), (1, W))
            emit_mix3(b, cur, sh, d3, b.rn(delta), m3, "mu_bck", t3, b.rn(mix_t),
                      [(xc_r, "m"), (xc_r, "b")])
            p.emit("dve", lambda e, t3b=t3b, d3=d3, m3b=m3b:
                   e.tensor_tensor(t3b, d3, m3b, OP.mult),
                   reads=[b.rn(delta), "mu_bcv"], writes=[b.rn(mix_t2)])
            p.emit("dve", lambda e, t3b=t3b, sh=sh:
                   e.tensor_tensor(t3b, t3b, sh, OP.add),
                   reads=[b.rn(mix_t2), (xc_r, "m"), (xc_r, "b")],
                   writes=[b.rn(mix_t2)])
            for m in range(8):
                pg, pg_r = proj_group(b, wg, wbufs["g"][1], mix_t, b.rn(mix_t), m, W)
                p.emit("act", lambda e, pg=pg, m=m: e.activation(
                    sg[:, m * W:(m + 1) * W], pg[:, 0:W], AF.Sigmoid,
                    bias=vecs[:, BG * 8 + m:BG * 8 + m + 1]),
                    reads=[pg_r, "l2a_vecs"], writes=[(b.rn(sg), m)], tag="l2a_sg")
            for m in range(8):
                pr, pr_r = proj_group(b, wrr, wbufs["r"][1], mix_t2, b.rn(mix_t2), m, W)
                p.emit("act", lambda e, pr=pr, m=m: e.activation(
                    rr[:, m * W:(m + 1) * W], pr[:, 0:W], AF.Identity,
                    bias=vecs[:, BR * 8 + m:BR * 8 + m + 1]),
                    reads=[pr_r, "l2a_vecs"], writes=[(b.rn(rr), m)], tag="l2a_br")
            b.dma("sp", kvc[:, :], resid_chunk_ap(kvs, c, W),
                  reads=["kv_scr"], writes=[b.rn(kvc)], tag=f"l2a_wkvl{c}")
            p.emit("dve", lambda e: e.tensor_tensor(
                rw[:, :], rr[:, :], kvc[:, :], OP.mult),
                reads=[b.rn(rr), b.rn(kvc)], writes=[b.rn(rw)], tag="l2a_rw")
            p.emit("act", lambda e: e.activation(rwsq[:, :], rw[:, :], AF.Square),
                   reads=[b.rn(rw)], writes=[b.rn(rwsq)], tag="l2a_rwsq")
            pG1, pG1_r = b.pb[4], b.pb_res[4]
            pG2, pG2_r = b.pb[5], b.pb_res[5]
            prs1 = [(ind[:, i * 16:(i + 1) * 16],
                     rw[:, i * W:(i + 1) * W], ["c_ind_s", b.rn(rw)])
                    for i in range(8)]
            prs2 = [(ind[:, i * 16:(i + 1) * 16],
                     rwsq[:, i * W:(i + 1) * W], ["c_ind_s", b.rn(rwsq)])
                    for i in range(8)]
            b.mm_group(pG1[0:16, 0:W], prs1, pG1_r, tag="l2a_gs1")
            b.mm_group(pG2[0:16, 0:W], prs2, pG2_r, tag="l2a_gs2")
            gr = b.rn(grows)
            grb = b.rn(grows_bf)
            def GR(i):
                return grows[0:16, i * W:(i + 1) * W]
            p.emit("dve", lambda e: e.tensor_scalar(GR(0), pG1[0:16, 0:W],
                                                    1.0 / 64, None, OP.mult),
                   reads=[pG1_r], writes=[(gr, 0)], tag="l2a_gm")
            p.emit("dve", lambda e: e.tensor_scalar(GR(1), pG2[0:16, 0:W],
                                                    1.0 / 64, None, OP.mult),
                   reads=[pG2_r], writes=[(gr, 1)], tag="l2a_gx2")
            p.emit("dve", lambda e: e.tensor_tensor(GR(2), GR(0), GR(0), OP.mult),
                   reads=[(gr, 0)], writes=[(gr, 2)])
            p.emit("dve", lambda e: e.tensor_tensor(GR(1), GR(1), GR(2), OP.subtract),
                   reads=[(gr, 1), (gr, 2)], writes=[(gr, 1)])
            p.emit("act", lambda e: e.activation(GR(3), GR(1), AF.Sqrt, bias=eps_c),
                   reads=[(gr, 1), ("c_const", b._consts[EPS])], writes=[(gr, 3)])
            p.emit("dve", lambda e: e.reciprocal(GR(2), GR(3)),
                   reads=[(gr, 3)], writes=[(gr, 2)])
            p.emit("dve", lambda e: e.tensor_copy(grows_bf[0:16, 0:W], GR(2)),
                   reads=[(gr, 2)], writes=[(grb, 0)])
            p.emit("dve", lambda e: e.scalar_tensor_tensor(
                grows_bf[0:16, W:2 * W], GR(0), -1.0, GR(2), OP.mult, OP.mult),
                reads=[(gr, 0), (gr, 2)], writes=[(grb, 1)])
            for i in range(8):
                pAB2, pAB2_r = b.bank()
                p.emit("pe", lambda e, pAB2=pAB2, i=i: e.matmul(
                    pAB2[:, 0:W], indT[0:16, i * 128:(i + 1) * 128],
                    grows_bf[0:16, 0:W], start=True, stop=True),
                    reads=[(grb, 0), "c_indT_s"], writes=[pAB2_r], tag="l2a_bA")
                p.emit("pe", lambda e, pAB2=pAB2, i=i: e.matmul(
                    pAB2[:, W:2 * W], indT[0:16, i * 128:(i + 1) * 128],
                    grows_bf[0:16, W:2 * W], start=True, stop=True),
                    reads=[(grb, 1), "c_indT_s"], writes=[pAB2_r], tag="l2a_bB")
                p.emit("dve", lambda e, pAB2=pAB2, i=i: e.tensor_tensor(
                    mix_t[:, i * W:(i + 1) * W], rw[:, i * W:(i + 1) * W],
                    pAB2[:, 0:W], OP.mult),
                    reads=[b.rn(rw), pAB2_r], writes=[(b.rn(mix_t), i)], tag="l2a_ln1")
                p.emit("dve", lambda e, pAB2=pAB2, i=i: e.tensor_tensor(
                    mix_t[:, i * W:(i + 1) * W], mix_t[:, i * W:(i + 1) * W],
                    pAB2[:, W:2 * W], OP.add),
                    reads=[(b.rn(mix_t), i), pAB2_r], writes=[(b.rn(mix_t), i)],
                    tag="l2a_ln2")
                p.emit("dve", lambda e, i=i: e.tensor_tensor(
                    ot[:, i * W:(i + 1) * W], sg[:, i * W:(i + 1) * W],
                    mix_t[:, i * W:(i + 1) * W], OP.mult),
                    reads=[(b.rn(mix_t), i), b.rn(sg)], writes=[(b.rn(ot), i)],
                    tag="l2a_ot")
            for m in range(8):
                po, po_r = proj_group(b, wo, wbufs["o"][1], ot, b.rn(ot), m, W)
                p.emit("dve", lambda e, po=po, m=m, ryc=ryc, rxc=rxc:
                       e.scalar_tensor_tensor(
                           ryc[:, m * W:(m + 1) * W], po[:, 0:W],
                           vecs[:, BO * 8 + m:BO * 8 + m + 1],
                           rxc[:, m * W:(m + 1) * W], OP.add, OP.add),
                       reads=[po_r, rxc_r, "l2a_vecs"], writes=[(ryc_r, ("y", m))],
                       tag="l2a_add")
            b.dma("sp", resid_chunk_ap(resid_out, c, W), ryc[:, :],
                  reads=[ryc_r], writes=[(rout_res, c)], tag=f"l2a_out{c}")



def emit_l0attn(b, resid_in, rin_res, resid_out, rout_res):
    p = b.p
    W = 256
    NCH = T // W
    xns = b.dram_tmp("xn_scr", [D, T], BF16)
    with b.scope():
        wa = b.sb("l0a_wa", [128, 8 * 256], BF16)
        wb = b.sb("l0a_wb", [128, 8 * 256], BF16)
        wc = b.sb("l0a_wc", [128, 2 * 1024], BF16)
        wd = b.sb("l0a_wd", [128, 8 * 1024], BF16)
        load_w2mb(b, "L0a_WaT", wa, "l0a_wa", kdim=8, mdim=256)
        load_w2mb(b, "L0a_BT", wb, "l0a_wb", kdim=8, mdim=256)
        load_w2mb(b, "L0a_CT", wc, "l0a_wc", kdim=2, mdim=1024)
        load_w2mb(b, "L0a_DT", wd, "l0a_wd", kdim=8, mdim=1024)
        wab = b.sb("l0a_wab", [128, 2], F32)
        b.dma("sp", wab[:, :], b.dram["L0a_Wab"], writes=["l0a_wab"])

        scr = make_ln_scr(b, W)
        ones = scr["ones"]
        rx = [b.sb("rx", [128, 8 * W], F32) for _ in range(2)]
        xnc = [b.sb("xnc", [128, 8 * W], BF16) for _ in range(2)]
        asb = b.sb("asb", [128, 2 * T], F32)
        bsb = b.sb("bsb", [128, 2 * T], F32)
        hbf = b.sb("hbf", [128, 2 * T], BF16)

        for c in range(NCH):
            rxc = rx[c % 2]
            xc = xnc[c % 2]
            rxc_r, xc_r = b.rn(rxc), b.rn(xc)
            b.dma("sp", rxc[:, :], resid_chunk_ap(resid_in, c, W),
                  reads=[(rin_res, c)], writes=[rxc_r], tag=f"l0a_rx{c}")
            xn_ap = ap3(xc, (8 * W, 128), (W, 8), (1, W))
            emit_ln(b, rxc, rxc_r, xn_ap, xc_r, W, scr, ones)
            b.dma("sp", resid_chunk_ap(xns, c, W), xc[:, :],
                  reads=[xc_r], writes=[("xn_scr", c)], tag=f"l0a_xn{c}")
            for m in range(2):
                pa, pa_r = proj_group(b, wa, "l0a_wa", xc, xc_r, m, W, wm=256)
                p.emit("act", lambda e, pa=pa, m=m, c=c: e.activation(
                    asb[:, m * T + c * W:m * T + (c + 1) * W], pa[:, 0:W],
                    AF.Sigmoid, bias=wab[:, m:m + 1]),
                    reads=[pa_r, "l0a_wab"], writes=[("l0a_a", (m, c))],
                    tag="l0a_sig")
            for m in range(2):
                pb_, pb_r = proj_group(b, wb, "l0a_wb", xc, xc_r, m, W, wm=256)
                p.emit("act", lambda e, pb_=pb_, m=m, c=c: e.copy(
                    bsb[:, m * T + c * W:m * T + (c + 1) * W], pb_[:, 0:W]),
                    reads=[pb_r], writes=[("l0a_b", (m, c))], tag="l0a_b")

        for j in range(2):
            p.emit("dve", lambda e, j=j: e.tensor_tensor_scan(
                hbf[:, j * T:(j + 1) * T], asb[:, j * T:(j + 1) * T],
                bsb[:, j * T:(j + 1) * T], 0.0, OP.mult, OP.add),
                reads=["l0a_a", "l0a_b"], writes=[("l0a_h", j)], tag="l0a_scan")

        xb2 = [b.sb("xb2", [128, 8 * W], BF16) for _ in range(2)]
        for c in range(NCH):
            rxc = rx[c % 2]
            xc = xb2[c % 2]
            rxc_r, xc_r = b.rn(rxc), b.rn(xc)
            b.dma("sp", rxc[:, :], resid_chunk_ap(resid_in, c, W),
                  reads=[(rin_res, c)], writes=[rxc_r], tag=f"l0a_2rx{c}")
            b.dma("sp", xc[:, :], resid_chunk_ap(xns, c, W),
                  reads=[("xn_scr", c)], writes=[xc_r], tag=f"l0a_2xn{c}")
            for m in range(8):
                pbk, pbk_r = b.bank()
                pairs = [(bass.AP(wc, k * 1024 + m * 128, [[2 * 1024, 128], [1, 128]]),
                          bass.AP(hbf, k * T + c * W, [[2 * T, 128], [1, W]]),
                          ["l0a_wc", ("l0a_h", k)]) for k in range(2)]
                pairs += [(bass.AP(wd, k * 1024 + m * 128, [[8 * 1024, 128], [1, 128]]),
                           bass.AP(xc, k * W, [[8 * W, 128], [1, W]]),
                           ["l0a_wd", xc_r]) for k in range(8)]
                b.mm_group(pbk[:, 0:W], pairs, pbk_r, tag="l0a_y")
                p.emit("dve", lambda e, pbk=pbk, m=m, rxc=rxc: e.tensor_tensor(
                    rxc[:, m * W:(m + 1) * W], rxc[:, m * W:(m + 1) * W],
                    pbk[:, 0:W], OP.add),
                    reads=[pbk_r, rxc_r], writes=[(rxc_r, ("y", m))],
                    tag="l0a_add")
            b.dma("sp", resid_chunk_ap(resid_out, c, W), rxc[:, :],
                  reads=[rxc_r], writes=[(rout_res, c)], tag=f"l0a_out{c}")



def emit_l3ffn(b, resid_in, rin_res, resid_out, rout_res, first):
    p = b.p
    W = 512
    NCH = T // W
    H4 = 4096
    caches = {
        "w1": b.dram_tmp("l3c_w1", [D, H4], BF16),
        "w2": b.dram_tmp("l3c_w2", [D, H4], BF16),
        "h1": b.dram_tmp("l3c_h1", [H4, H4], BF16),
        "h2": b.dram_tmp("l3c_h2", [H4, H4], BF16),
        "w3": b.dram_tmp("l3c_w3", [H4, D], BF16),
    }
    srcs = {"w1": "L3f_w1T", "w2": "L3f_w2T", "h1": "L3f_h1T",
            "h2": "L3f_h2T", "w3": "L3f_w3T"}
    kd = {"w1": 8, "w2": 8, "h1": 32, "h2": 32, "w3": 32}
    md = {"w1": H4, "w2": H4, "h1": H4, "h2": H4, "w3": D}

    def wblock_ap(tensor_ap, which, m0, mn):
        mdim = md[which]
        return bass.AP(tensor_ap.tensor, m0 * 128,
                       [[mdim, 128], [128 * mdim, kd[which]], [1, mn * 128]])

    def cblock_ap(tensor_ap, which, m0, mn):
        bn = mn * 128
        kdim = kd[which]
        blk = m0 // mn
        return bass.AP(tensor_ap.tensor, blk * kdim * 128 * bn,
                       [[kdim * bn, 128], [1, kdim * bn]])

    with b.scope():
        sc = b.sb("l3f_sc", [128, 32 * 4 + 8 * 2], F32)
        b.dma("sp", sc[:, 0:128], b.dram["L3f_sc"], writes=[("l3f_sc", 0)])
        b.dma("sp", sc[:, 128:136], b.dram["L3f_s3"], writes=[("l3f_sc", 1)])
        p.emit("dve", lambda e: e.tensor_scalar(sc[:, 136:144], sc[:, 128:136],
                                                0.5, None, OP.mult),
               reads=[("l3f_sc", 1)], writes=[("l3f_sc", 2)], tag="l3f_s3h")

        def S1(m):
            return sc[:, m:m + 1]

        def S2(m):
            return sc[:, 32 + m:32 + m + 1]

        def SH1(m):
            return sc[:, 64 + m:64 + m + 1]

        def SH2(m):
            return sc[:, 96 + m:96 + m + 1]

        def S3H(m):
            return sc[:, 136 + m:136 + m + 1]

        scr = make_ln_scr(b, 256)
        ones = scr["ones"]
        rx = b.sb("rx", [128, 8 * W], F32)
        rx_r = b.rn(rx)
        xn = b.sb("xn", [128, 8 * W], BF16)
        xn_r = b.rn(xn)
        hA = b.sb("hA", [128, 32 * W], BF16)
        hB = b.sb("hB", [128, 32 * W], BF16)
        hA_r, hB_r = b.rn(hA), b.rn(hB)
        t1 = b.sb("t1", [128, W], BF16)
        t1_r = b.rn(t1)
        w1b = b.sb("w1b", [128, 8 * 512], BF16)
        w2b = b.sb("w2b", [128, 8 * 512], BF16)
        hst = [b.sb("hst", [128, 32 * 128], BF16) for _ in range(2)]

        def stream_w(which, m0, mn, buf, buf_res, c):
            if first and c == 0:
                src = wblock_ap(b.dram[srcs[which]], which, m0, mn)
                b.dma("pool", buf[:, 0:kd[which] * mn * 128], src,
                      writes=[buf_res], tag=f"l3fw_{which}{m0}")
                dstc = cblock_ap(caches[which], which, m0, mn)
                b.dma("sp", dstc, buf[:, 0:kd[which] * mn * 128],
                      reads=[buf_res], writes=[(f"l3c_{which}", m0)],
                      tag=f"l3fc_{which}{m0}")
            else:
                src = cblock_ap(caches[which], which, m0, mn)
                b.dma("sp", buf[:, 0:kd[which] * mn * 128], src,
                      reads=[(f"l3c_{which}", m0)],
                      writes=[buf_res], tag=f"l3fr_{which}{m0}")

        for c in range(NCH):
            b.dma("sp", rx[:, :], resid_chunk_ap(resid_in, c, W),
                  reads=[(rin_res, c)], writes=[rx_r], tag=f"l3f_rx{c}")
            for s in range(2):
                xn_ap = ap3(xn, (8 * W, 128), (W, 8), (1, 256), offset=s * 256)
                emit_ln(b, rx, rx_r, xn_ap, (xn_r, s), 256, scr, ones,
                        rx_fw=8 * W, rx_stride=W, rx_off=s * 256)
            for mb in range(8):
                stream_w("w1", mb * 4, 4, w1b, (b.rn(w1b), "w"), c)
                stream_w("w2", mb * 4, 4, w2b, (b.rn(w2b), "w"), c)
                for mi in range(4):
                    m = mb * 4 + mi
                    pg, pg_r = b.bank()
                    pairs = [(bass.AP(w1b, k * 512 + mi * 128,
                                      [[8 * 512, 128], [1, 128]]),
                              bass.AP(xn, k * W, [[8 * W, 128], [1, W]]),
                              [(b.rn(w1b), "w"), xn_r])
                             for k in range(8)]
                    b.mm_group(pg[:, 0:W], pairs, pg_r, tag="l3f_g")
                    p.emit("act", lambda e, pg=pg, m=m: e.activation(
                        t1[:, :], pg[:, 0:W], AF.Silu, scale=S1(m)),
                        reads=[pg_r, ("l3f_sc", 0)], writes=[t1_r], tag="l3f_silu")
                    pu, pu_r = b.bank()
                    pairs = [(bass.AP(w2b, k * 512 + mi * 128,
                                      [[8 * 512, 128], [1, 128]]),
                              bass.AP(xn, k * W, [[8 * W, 128], [1, W]]),
                              [(b.rn(w2b), "w"), xn_r])
                             for k in range(8)]
                    b.mm_group(pu[:, 0:W], pairs, pu_r, tag="l3f_u")
                    p.emit("dve", lambda e, pu=pu, m=m: e.scalar_tensor_tensor(
                        hA[:, m * W:(m + 1) * W], pu[:, 0:W], S2(m), t1[:, :],
                        OP.mult, OP.mult),
                        reads=[pu_r, t1_r, ("l3f_sc", 0)], writes=[(hA_r, m)],
                        tag="l3f_hA")
            for src_buf, src_r, dst_buf, dst_r, which, SH in (
                    (hA, hA_r, hB, hB_r, "h1", SH1),
                    (hB, hB_r, hA, hA_r, "h2", SH2)):
                for m in range(32):
                    hsb = hst[m % 2]
                    stream_w(which, m, 1, hsb, b.rn(hsb), c)
                    ph, ph_r = b.bank()
                    pairs = [(bass.AP(hsb, k * 128, [[32 * 128, 128], [1, 128]]),
                              bass.AP(src_buf, k * W, [[32 * W, 128], [1, W]]),
                              [b.rn(hsb), src_r])
                             for k in range(32)]
                    b.mm_group(ph[:, 0:W], pairs, ph_r, tag=f"l3f_{which}")
                    p.emit("act", lambda e, ph=ph, m=m, SH=SH, dst_buf=dst_buf:
                           e.activation(dst_buf[:, m * W:(m + 1) * W], ph[:, 0:W],
                                        AF.Gelu, scale=SH(m)),
                           reads=[ph_r, ("l3f_sc", 0)], writes=[(dst_r, m)],
                           tag=f"l3f_gelu")
            for m in range(8):
                hsb = hst[m % 2]
                stream_w("w3", m, 1, hsb, b.rn(hsb), c)
                pd, pd_r = b.bank()
                pairs = [(bass.AP(hsb, k * 128, [[32 * 128, 128], [1, 128]]),
                          bass.AP(hA, k * W, [[32 * W, 128], [1, W]]),
                          [b.rn(hsb), hA_r]) for k in range(32)]
                b.mm_group(pd[:, 0:W], pairs, pd_r, tag="l3f_w3")
                p.emit("dve", lambda e, pd=pd, m=m: e.scalar_tensor_tensor(
                    rx[:, m * W:(m + 1) * W], pd[:, 0:W], S3H(m),
                    rx[:, m * W:(m + 1) * W], OP.mult, OP.add),
                    reads=[pd_r, rx_r, ("l3f_sc", 2)], writes=[(rx_r, ("y", m))],
                    tag="l3f_add")
            b.dma("sp", resid_chunk_ap(resid_out, c, W), rx[:, :],
                  reads=[rx_r], writes=[(rout_res, c)], tag=f"l3f_out{c}")



def emit_l3attn(b, resid_in, rin_res, resid_out, rout_res):
    p = b.p
    W = 512
    NCH = T // W
    qts = b.dram_tmp("qT_scr", [D, T], BF16)
    kts = b.dram_tmp("kT_scr", [D, T], BF16)
    vns = b.dram_tmp("vn_scr", [16 * T, 64], BF16)
    with b.scope():
        wo = b.sb("l3a_wo", [128, 8 * 1024], BF16)
        load_w2mb(b, "L3a_WoT", wo, "l3a_wo")
        bias = b.sb("l3a_bias", [128, 8 * 4], F32)
        b.dma("sp", bias[:, :], b.dram["L3a_bias"], writes=["l3a_bias"])
        masks = b.sb("l3a_masks", [128, 4 * W], BF16)
        b.dma("sp", masks[:, :], b.dram["c_l3mask"], writes=["c_l3mask_s"])
        onesrow = b.sb("onesrow", [1, 128], BF16)
        p.emit("pool", lambda e: e.memset(onesrow[:, :], 1.0),
               writes=["c_onesrow"])
        ones = b.sb("ones", [128, 1], BF16)
        p.emit("pool", lambda e: e.memset(ones[:, :], 1.0), writes=["c_ones_s"])
        rx = b.sb("rx", [128, 8 * W], F32)
        rx_r = b.rn(rx)
        conc = b.sb("conc", [128, 8 * T], BF16)
        conc_r = b.rn(conc)
        sweep1 = b.scope()
        sweep1.__enter__()
        wq = b.sb("l3a_wq", [128, 8 * 1024], BF16)
        wk = b.sb("l3a_wk", [128, 8 * 1024], BF16)
        wv = b.sb("l3a_wv", [128, 8 * 1024], BF16)
        for nm, buf, res in (("L3a_WqT", wq, "l3a_wq"), ("L3a_WkT", wk, "l3a_wk"),
                             ("L3a_WvT", wv, "l3a_wv")):
            load_w2mb(b, nm, buf, res)
        eyeb = b.sb("l3a_eyeb", [128, 128], BF16)
        b.dma("sp", eyeb[:, :], b.dram["c_eye"], writes=["c_eye_s"])
        scr = make_ln_scr(b, 256, ones=ones, onesrow=onesrow)
        xn = b.sb("xn", [128, 8 * W], BF16)
        xn_r = b.rn(xn)
        ev = b.sb("ev", [128, W], BF16)
        ev_r = b.rn(ev)
        vntc = b.sb("vntc", [128, 64], BF16)
        vntc_r = b.rn(vntc)

        for c in range(NCH):
            b.dma("sp", rx[:, :], resid_chunk_ap(resid_in, c, W),
                  reads=[(rin_res, c)], writes=[rx_r], tag=f"l3a_rx{c}")
            for s in range(2):
                xn_ap = ap3(xn, (8 * W, 128), (W, 8), (1, 256), offset=s * 256)
                emit_ln(b, rx, rx_r, xn_ap, (xn_r, s), 256, scr, ones,
                        rx_fw=8 * W, rx_stride=W, rx_off=s * 256)
            for which, wbuf, wres, scrt, bofs in (
                    ("q", wq, "l3a_wq", qts, 0), ("k", wk, "l3a_wk", kts, 1),
                    ("v", wv, "l3a_wv", None, 2)):
                for m in range(8):
                    pk, pk_r = proj_group(b, wbuf, wres, xn, xn_r, m, W)
                    p.emit("act", lambda e, pk=pk, m=m, bofs=bofs: e.activation(
                        ev[:, :], pk[:, 0:W], AF.Identity,
                        bias=bias[:, bofs * 8 + m:bofs * 8 + m + 1]),
                        reads=[pk_r, "l3a_bias"], writes=[ev_r], tag="l3a_ev")
                    if which != "v":
                        dst = bass.AP(scrt.tensor, m * 128 * T + c * W,
                                      [[T, 128], [1, W]])
                        b.dma("sp", dst, ev[:, :], reads=[ev_r],
                              writes=[(f"{which}T_scr", (m, c))], tag="l3a_st")
                    else:
                        for hh_i in range(2):
                            for gg in range(W // 128):
                                h_idx = 2 * m + hh_i
                                t0 = c * W + gg * 128
                                p.emit("pe", lambda e, hh_i=hh_i, gg=gg:
                                       bf16_transpose(
                                           e, b.nc, b.pbt[:, 0:64],
                                           ev[64 * hh_i:64 * (hh_i + 1),
                                              gg * 128:(gg + 1) * 128],
                                           eyeb[64 * hh_i:64 * hh_i + 64,
                                                64 * hh_i:64 * hh_i + 64]),
                                       reads=[ev_r, "c_eye_s"],
                                       writes=[b.pbt_res], tag="l3a_vT")
                                p.emit("act", lambda e: e.copy(
                                    vntc[:, :], b.pbt[:, 0:64]),
                                    reads=[b.pbt_res], writes=[vntc_r],
                                    tag="l3a_vTe")
                                dst = bass.AP(vns.tensor,
                                              (h_idx * T + t0) * 64,
                                              [[64, 128], [1, 64]])
                                b.dma("sp", dst, vntc[:, :], reads=[vntc_r],
                                      writes=[("vn_scr", (h_idx, c, gg))],
                                      tag="l3a_vno")

        sweep1.__exit__(None, None, None)
        kaug = b.sb("kaug", [65, T], BF16)
        qaug = b.sb("qaug", [65, T], BF16)
        kaug_r, qaug_r = b.rn(kaug), b.rn(qaug)
        vab = b.sb("vab", [128, 32 * 65], BF16)
        vab_r = b.rn(vab)
        sqs = b.sb("sqs", [64, T], BF16)
        sqs_r = b.rn(sqs)
        arow = b.sb("arow", [1, T], BF16)
        arow_r = b.rn(arow)
        kmx = b.sb("kmx", [1, 8], F32)
        kmx_r = b.rn(kmx)
        expt = [b.sb("expt", [128, W], BF16) for _ in range(2)]
        dreci = b.sb("dreci", [1, W], BF16)
        dreci_r = b.rn(dreci)
        ctxe = b.sb("ctxe", [64, W], BF16)
        ctxe_r = b.rn(ctxe)
        conc = b.sb("conc", [128, 8 * T], BF16)
        conc_r = b.rn(conc)
        p.emit("pool", lambda e: e.memset(kaug[64:65, :], 1.0),
               writes=[(kaug_r, "ones")], tag="l3a_k1")
        p.emit("pool", lambda e: e.memset(
            ap3(vab, (32 * 65, 128), (65, 32), (1, 1), offset=64), 1.0),
            writes=[(vab_r, "ones")], tag="l3a_v1")
        pav, pav_r = b.pb[5], b.pb_res[5]
        for h in range(16):
            b.dma("sp", kaug[0:64, :],
                  bass.AP(kts.tensor, h * 64 * T, [[T, 64], [1, T]]),
                  reads=["kT_scr"], writes=[(kaug_r, "k")], tag=f"l3a_kl{h}")
            b.dma("sp", qaug[0:64, :],
                  bass.AP(qts.tensor, h * 64 * T, [[T, 64], [1, T]]),
                  reads=["qT_scr"], writes=[(qaug_r, "q")], tag=f"l3a_ql{h}")
            b.dma("sp", ap3(vab, (32 * 65, 128), (65, 32), (1, 64)),
                  bass.AP(vns.tensor, h * T * 64, [[64, 128], [128 * 64, 32], [1, 64]]),
                  reads=["vn_scr"], writes=[(vab_r, "v")], tag=f"l3a_vl{h}")
            for aug, nm in ((kaug, "k"), (qaug, "q")):
                p.emit("act", lambda e, aug=aug, nm=nm: e.activation(
                    sqs[:, :], aug[0:64, :], AF.Square),
                    reads=[(b.rn(aug), nm)], writes=[sqs_r], tag="l3a_sq")
                for cc in range(8):
                    psq, psq_r = b.bank()
                    prs = [(ones[0:64, 0:1], sqs[0:64, cc * W:(cc + 1) * W],
                            [sqs_r, "c_ones_s"])]
                    b.mm_group(psq[0:1, 0:W], prs, psq_r, tag="l3a_qs")
                    p.emit("act", lambda e, psq=psq, cc=cc, nm=nm: e.activation(
                        arow[0:1, cc * W:(cc + 1) * W], psq[0:1, 0:W], AF.Sqrt),
                        reads=[psq_r], writes=[(arow_r, (nm, cc))], tag="l3a_sqrt")
                if nm == "k":
                    p.emit("dve", lambda e: e.reduce_max(
                        kmx[0:1, 0:1], arow[0:1, :], axis=AX.X),
                        reads=[arow_r], writes=[kmx_r], tag="l3a_kmx")
            p.emit("dve", lambda e: e.tensor_scalar(
                qaug[64:65, :], arow[0:1, :], kmx[0:1, 0:1], -1.0,
                OP.mult, OP.mult),
                reads=[arow_r, kmx_r], writes=[(qaug_r, "m")], tag="l3a_negM")
            for qc in range(NCH):
                nkb = 4 * qc + 4
                for kb in range(nkb):
                    ps, ps_r = b.bank()
                    ex = expt[kb % 2]
                    ex_r = b.rn(ex)
                    p.emit("pe", lambda e, ps=ps, kb=kb, qc=qc: e.matmul(
                        ps[:, 0:W], kaug[:, kb * 128:(kb + 1) * 128],
                        qaug[:, qc * W:(qc + 1) * W], start=True, stop=True),
                        reads=[(kaug_r, "k"), (kaug_r, "ones"),
                               (qaug_r, "q"), (qaug_r, "m")],
                        writes=[ps_r], tag="l3a_s")
                    p.emit("act", lambda e, ps=ps, ex=ex: e.activation(
                        ex[:, :], ps[:, 0:W], AF.Exp, scale=0.125),
                        reads=[ps_r], writes=[ex_r], tag="l3a_exp")
                    dv = kb - 4 * qc
                    if dv >= 0:
                        p.emit("dve", lambda e, ex=ex, dv=dv: e.tensor_tensor(
                            ex[:, :], ex[:, :],
                            masks[:, dv * W:(dv + 1) * W], OP.mult),
                            reads=[ex_r, "c_l3mask_s"], writes=[ex_r],
                            tag="l3a_msk")
                    p.emit("pe", lambda e, ex=ex, kb=kb, st=(kb == 0),
                           sp=(kb == nkb - 1): e.matmul(
                        pav[0:65, 0:W], vab[:, kb * 65:(kb + 1) * 65],
                        ex[:, :], start=st, stop=sp),
                        reads=[ex_r, (vab_r, "v"), (vab_r, "ones")],
                        writes=[pav_r], tag="l3a_av")
                p.emit("dve", lambda e: lp(b.nc, lambda: e.reciprocal(
                    dreci[0:1, :], pav[64:65, 0:W])),
                       reads=[pav_r], writes=[dreci_r], tag="l3a_rec")
                pbc, pbc_r = b.bank()
                p.emit("pe", lambda e, pbc=pbc: e.matmul(
                    pbc[0:64, 0:W], scr["onesrow"][0:1, 0:64], dreci[0:1, :],
                    start=True, stop=True),
                    reads=[dreci_r, "c_onesrow"], writes=[pbc_r], tag="l3a_rb")
                p.emit("act", lambda e: e.copy(ctxe[:, :], pav[0:64, 0:W]),
                       reads=[pav_r], writes=[ctxe_r], tag="l3a_ctx")
                co = (h // 2) * T + qc * W
                cr0 = 64 * (h % 2)
                p.emit("dve", lambda e, pbc=pbc, co=co, cr0=cr0: e.tensor_tensor(
                    conc[cr0:cr0 + 64, co:co + W], ctxe[:, :], pbc[0:64, 0:W],
                    OP.mult),
                    reads=[ctxe_r, pbc_r], writes=[(conc_r, (h, qc))],
                    tag="l3a_nrm")

        for c in range(NCH):
            b.dma("sp", rx[:, :], resid_chunk_ap(resid_in, c, W),
                  reads=[(rin_res, c)], writes=[rx_r], tag=f"l3a_2rx{c}")
            for m in range(8):
                po, po_r = proj_group(b, wo, "l3a_wo", conc, conc_r, m, W,
                                      rhs_stride=T, rhs_off=c * W)
                p.emit("dve", lambda e, po=po, m=m: e.scalar_tensor_tensor(
                    rx[:, m * W:(m + 1) * W], po[:, 0:W],
                    bias[:, 3 * 8 + m:3 * 8 + m + 1],
                    rx[:, m * W:(m + 1) * W], OP.add, OP.add),
                    reads=[po_r, rx_r, "l3a_bias"], writes=[(rx_r, ("y", m))],
                    tag="l3a_add")
            b.dma("sp", resid_chunk_ap(resid_out, c, W), rx[:, :],
                  reads=[rx_r], writes=[(rout_res, c)], tag=f"l3a_out{c}")



def emit_fp4_quant(b, x_h, x_off, x_fw, x_res, out_ap, out_res, F, q):
    p = b.p
    NB = F // 16
    x3 = ap3(x_h, (x_fw, 128), (16, NB), (1, 16), offset=x_off)
    t_h, i32_h, sgn_h, sc_h, rc_h, qq_h = (q["t"], q["i32"], q["sgn"],
                                           q["sc"], q["rc"], q["qq"])
    t_r, i_r, g_r, sc_r, rc_r, qq_r = (b.rn(t_h), b.rn(i32_h), b.rn(sgn_h),
                                       b.rn(sc_h), b.rn(rc_h), b.rn(qq_h))
    WQ, WNB = q["wide"], q["nbw"]
    xa = bass.AP(x_h, x_off, [[x_fw, 128], [1, F]])
    ta = t_h[:, 0:F]
    ia = i32_h[:, 0:F]
    ga = sgn_h[:, 0:F]
    qa = qq_h[:, 0:F]
    sca = sc_h[:, 0:NB]
    rca = rc_h[:, 0:NB]
    sc3 = ap3(sc_h, (WNB, 128), (1, NB), (0, 16))
    rc3 = ap3(rc_h, (WNB, 128), (1, NB), (0, 16))
    q3 = ap3(qq_h, (WQ, 128), (16, NB), (1, 16))
    p.emit("dve", lambda e: e.reduce_max(sca, x3, axis=AX.X,
                                         apply_absolute_value=True),
           reads=[x_res], writes=[sc_r], tag="q_amax")
    p.emit("dve", lambda e: e.tensor_scalar(sca, sca, 1.0 / 6, 1e-12,
                                            OP.mult, OP.max),
           reads=[sc_r], writes=[sc_r], tag="q_sc")
    p.emit("dve", lambda e: e.reciprocal(rca, sca),
           reads=[sc_r], writes=[rc_r], tag="q_rc")
    p.emit("dve", lambda e: e.tensor_tensor(x3, x3, rc3, OP.mult),
           reads=[x_res, rc_r], writes=[x_res], tag="q_xn")
    p.emit("act", lambda e: e.activation(ta, xa, AF.Abs),
           reads=[x_res], writes=[t_r], tag="q_abs")
    p.emit("act", lambda e: e.activation(ga, xa, AF.Sign),
           reads=[x_res], writes=[g_r], tag="q_sgn")
    p.emit("dve", lambda e: e.tensor_scalar(ia, ta, 2.0, None, OP.mult),
           reads=[t_r], writes=[i_r], tag="q_rnd")
    p.emit("dve", lambda e: e.tensor_scalar(qa, ia, 0.5, 2.0, OP.mult, OP.min),
           reads=[i_r], writes=[qq_r], tag="q_low")
    for bd in (2.5, 3.5, 5.0, 5.0):
        p.emit("dve", lambda e, bd=bd: e.scalar_tensor_tensor(
            qa, ta, bd, qa, OP.is_gt, OP.add),
            reads=[t_r, qq_r], writes=[qq_r], tag="q_hi")
    p.emit("dve", lambda e: e.tensor_tensor(qa, qa, ga, OP.mult),
           reads=[qq_r, g_r], writes=[qq_r], tag="q_sg")
    p.emit("dve", lambda e: e.tensor_tensor(out_ap, q3, sc3, OP.mult),
           reads=[qq_r, sc_r], writes=[out_res], tag="q_out")


def emit_l0ffn(b, resid_in, rin_res, resid_out, rout_res, first):
    p = b.p
    W = 256
    NCH = T // W
    I0 = 2816
    NI = 22
    cg = b.dram_tmp("l0c_g", [D, I0], BF16)
    cu = b.dram_tmp("l0c_u", [D, I0], BF16)
    cd = b.dram_tmp("l0c_d", [I0, D], BF16)
    with b.scope():
        qs = {"t": b.sb("q_t", [128, 2048], F32),
              "i32": b.sb("q_i", [128, 2048], I32),
              "sgn": b.sb("q_g", [128, 2048], BF16),
              "sc": b.sb("q_s", [128, 128], F32),
              "rc": b.sb("q_r", [128, 128], F32),
              "qq": b.sb("q_q", [128, 2048], F32),
              "wide": 2048, "nbw": 128}
        eyeb = b.sb("l0_eyeb", [128, 128], BF16)
        b.dma("sp", eyeb[:, :], b.dram["c_eye"], writes=["c_eye_s"])

        if first:
            with b.scope():
                wtile = b.sb("l0w_tile", [128, I0], F32)
                wt_r = b.rn(wtile)
                wqt = b.sb("l0w_q", [128, I0], BF16)
                wq_r = b.rn(wqt)
                stg = b.sb("l0w_stg", [128, 128], BF16)
                stg_r = b.rn(stg)
                for srcnm, cch, ccres, ntile, fw in (
                        ("L0f_wg", cg, "l0c_g", 22, 1024),
                        ("L0f_wu", cu, "l0c_u", 22, 1024),
                        ("L0f_wd", cd, "l0c_d", 8, I0)):
                    for n in range(ntile):
                        src = bass.AP(b.dram[srcnm].tensor, n * 128 * fw,
                                      [[fw, 128], [1, fw]])
                        b.dma("sp", wtile[:, 0:fw], src, writes=[wt_r],
                              tag=f"l0w_{srcnm}{n}")
                        subs = [(0, 1024)] if fw == 1024 else [(0, 1408),
                                                               (1408, 1408)]
                        for off, ln_ in subs:
                            emit_fp4_quant(
                                b, wtile, off, I0, wt_r,
                                bass.AP(wqt, off, [[I0, 128], [1, ln_]]),
                                wq_r, ln_, qs)
                        for ks in range(fw // 128):
                            p.emit("pe", lambda e, ks=ks: bf16_transpose(
                                e, b.nc, b.pbt[:, 0:128],
                                wqt[:, ks * 128:(ks + 1) * 128], eyeb[:, :]),
                                reads=[wq_r, "c_eye_s"], writes=[b.pbt_res],
                                tag="l0w_T")
                            p.emit("act", lambda e: e.copy(stg[:, :],
                                                           b.pbt[:, 0:128]),
                                   reads=[b.pbt_res], writes=[stg_r],
                                   tag="l0w_Te")
                            if fw == 1024:
                                dst = bass.AP(cch.tensor,
                                              ks * I0 + n * 128,
                                              [[8 * I0, 128], [1, 128]])
                            else:
                                dst = bass.AP(cch.tensor,
                                              n * 2816 + ks * 128,
                                              [[8 * 2816, 128], [1, 128]])
                            b.dma("sp", dst, stg[:, :], reads=[stg_r],
                                  writes=[(ccres, (n, ks))], tag="l0w_To")

        wg = b.sb("l0_wg", [128, 8 * I0], BF16)
        wu = b.sb("l0_wu", [128, 8 * I0], BF16)
        wg_r, wu_r = b.rn(wg), b.rn(wu)
        src = bass.AP(cg.tensor, 0, [[8 * I0, 128], [1, 8 * I0]])
        b.dma("sp", wg[:, :], src, reads=["l0c_g"], writes=[wg_r], tag="l0_lg")
        src = bass.AP(cu.tensor, 0, [[8 * I0, 128], [1, 8 * I0]])
        b.dma("sp", wu[:, :], src, reads=["l0c_u"], writes=[wu_r], tag="l0_lu")
        wd = [b.sb("l0_wd", [128, 22 * 128], BF16) for _ in range(2)]

        scr = make_ln_scr(b, W, xc_buf=qs["t"])
        scr["ln_tmp32"] = qs["qq"]
        ones = scr["ones"]
        rx = [b.sb("rx", [128, 8 * W], F32)]
        rx.append(rx[0])
        xnf = b.sb("xnf", [128, 8 * W], F32)
        xnf_r = b.rn(xnf)
        xbt = b.sb("xbt", [128, 8 * W], F32)
        xbt_r = b.rn(xbt)
        xqb = qs["sgn"]
        xqb_r = b.rn(xqb)
        t1f = b.sb("t1f", [128, W], F32)
        t1_r = b.rn(t1f)
        hf = b.sb("hf", [128, NI * W], F32)
        hf_r = b.rn(hf)
        hqb = b.sb("hqb", [128, NI * W], BF16)
        hqb_r = b.rn(hqb)

        for c in range(NCH):
            rxc = rx[c % 2]
            rxc_r = b.rn(rxc)
            b.dma("sp", rxc[:, :], resid_chunk_ap(resid_in, c, W),
                  reads=[(rin_res, c)], writes=[rxc_r], tag=f"l0_rx{c}")
            xn_ap = ap3(xnf, (8 * W, 128), (W, 8), (1, W))
            emit_ln(b, rxc, rxc_r, xn_ap, xnf_r, W, scr, ones)
            p.emit("dve", lambda e: e.transpose(xbt[:, :], xnf[:, :]),
                   reads=[xnf_r], writes=[xbt_r], tag="l0_bt")
            emit_fp4_quant(b, xbt, 0, 8 * W, xbt_r,
                           bass.AP(xnf, 0, [[8 * W, 128], [1, 8 * W]]),
                           xnf_r, 8 * W, qs)
            p.emit("dve", lambda e: e.transpose(xbt[:, :], xnf[:, :]),
                   reads=[xnf_r], writes=[xbt_r], tag="l0_btb")
            p.emit("act", lambda e: e.copy(xqb[:, :], xbt[:, :]),
                   reads=[xbt_r], writes=[xqb_r], tag="l0_xq")
            for m in range(NI):
                pg, pg_r = proj_group(b, wg, wg_r, xqb, xqb_r, m, W, wm=I0)
                p.emit("act", lambda e, pg=pg: e.activation(
                    t1f[:, :], pg[:, 0:W], AF.Silu),
                    reads=[pg_r], writes=[t1_r], tag="l0_silu")
                pu, pu_r = proj_group(b, wu, wu_r, xqb, xqb_r, m, W, wm=I0)
                p.emit("dve", lambda e, pu=pu, m=m: e.tensor_tensor(
                    hf[:, m * W:(m + 1) * W], pu[:, 0:W], t1f[:, :], OP.mult),
                    reads=[pu_r, t1_r], writes=[(hf_r, m)], tag="l0_h")
            for sbi in range(4):
                o = sbi * 1408
                p.emit("dve", lambda e, o=o: e.transpose(
                    bass.AP(xbt, 0, [[8 * W, 128], [1, 1408]]),
                    bass.AP(hf, o, [[NI * W, 128], [1, 1408]])),
                    reads=[hf_r], writes=[xbt_r], tag="l0_hbt")
                emit_fp4_quant(b, xbt, 0, 8 * W, xbt_r,
                               bass.AP(xnf, 0, [[8 * W, 128], [1, 1408]]),
                               xnf_r, 1408, qs)
                p.emit("dve", lambda e, o=o: e.transpose(
                    bass.AP(hf, o, [[NI * W, 128], [1, 1408]]),
                    bass.AP(xnf, 0, [[8 * W, 128], [1, 1408]])),
                    reads=[xnf_r], writes=[(hf_r, ("q", sbi))], tag="l0_hbtb")
            p.emit("act", lambda e: e.copy(hqb[:, :], hf[:, :]),
                   reads=[hf_r], writes=[hqb_r], tag="l0_hq")
            for m in range(8):
                wdc = wd[m % 2]
                wdc_r = b.rn(wdc)
                src = bass.AP(cd.tensor, m * 2816,
                              [[8 * 2816, 128], [1, 2816]])
                b.dma("sp", wdc[:, :], src, reads=["l0c_d"], writes=[wdc_r],
                      tag=f"l0_ld{c}_{m}")
                pd, pd_r = b.bank()
                pairs = [(bass.AP(wdc, k * 128, [[22 * 128, 128], [1, 128]]),
                          bass.AP(hqb, k * W, [[NI * W, 128], [1, W]]),
                          [wdc_r, hqb_r]) for k in range(22)]
                b.mm_group(pd[:, 0:W], pairs, pd_r, tag="l0_d")
                p.emit("dve", lambda e, pd=pd, m=m, rxc=rxc:
                       e.scalar_tensor_tensor(
                           rxc[:, m * W:(m + 1) * W], pd[:, 0:W], 0.5,
                           rxc[:, m * W:(m + 1) * W], OP.mult, OP.add),
                       reads=[pd_r, rxc_r], writes=[(rxc_r, ("y", m))],
                       tag="l0_add")
            b.dma("sp", resid_chunk_ap(resid_out, c, W), rxc[:, :],
                  reads=[rxc_r], writes=[(rout_res, c)], tag=f"l0_out{c}")



def emit_l1attn(b, resid_in, rin_res, resid_out, rout_res):
    p = b.p
    W = 512
    NCH = T // W
    NW = T // 128
    qts = b.dram_tmp("qT_scr", [D, T], BF16)
    kts = b.dram_tmp("kT_scr", [D, T], BF16)
    vns = b.dram_tmp("vn_scr", [16 * T, 64], BF16)
    with b.scope():
        wo = b.sb("l1a_wo", [128, 8 * 1024], BF16)
        load_w2mb(b, "L1a_WoT", wo, "l1a_wo")
        onesrow = b.sb("onesrow", [1, 128], BF16)
        p.emit("pool", lambda e: e.memset(onesrow[:, :], 1.0),
               writes=["c_onesrow"])
        ones = b.sb("ones", [128, 1], BF16)
        p.emit("pool", lambda e: e.memset(ones[:, :], 1.0), writes=["c_ones_s"])
        rx = b.sb("rx", [128, 8 * W], F32)
        rx_r = b.rn(rx)
        conc = b.sb("conc", [128, 8 * T], BF16)
        conc_r = b.rn(conc)
        sweep1 = b.scope()
        sweep1.__enter__()
        wq = b.sb("l1a_wq", [128, 8 * 1024], BF16)
        wk = b.sb("l1a_wk", [128, 8 * 1024], BF16)
        wv = b.sb("l1a_wv", [128, 8 * 1024], BF16)
        for nm, buf, res in (("L1a_WqT", wq, "l1a_wq"), ("L1a_WkT", wk, "l1a_wk"),
                             ("L1a_WvT", wv, "l1a_wv")):
            load_w2mb(b, nm, buf, res)
        eyeb = b.sb("l1a_eyeb", [128, 128], BF16)
        b.dma("sp", eyeb[:, :], b.dram["c_eye"], writes=["c_eye_s"])
        scr = make_ln_scr(b, 256, ones=ones, onesrow=onesrow)
        xn = b.sb("xn", [128, 8 * W], BF16)
        xn_r = b.rn(xn)
        ev = b.sb("ev", [128, W], BF16)
        ev_r = b.rn(ev)
        vntc = b.sb("vntc", [128, 64], BF16)
        vntc_r = b.rn(vntc)
        for c in range(NCH):
            b.dma("sp", rx[:, :], resid_chunk_ap(resid_in, c, W),
                  reads=[(rin_res, c)], writes=[rx_r], tag=f"l1a_rx{c}")
            for s in range(2):
                xn_ap = ap3(xn, (8 * W, 128), (W, 8), (1, 256), offset=s * 256)
                emit_ln(b, rx, rx_r, xn_ap, (xn_r, s), 256, scr, ones,
                        rx_fw=8 * W, rx_stride=W, rx_off=s * 256)
            for which, wbuf, wres in (("q", wq, "l1a_wq"), ("k", wk, "l1a_wk"),
                                      ("v", wv, "l1a_wv")):
                for m in range(8):
                    pk, pk_r = proj_group(b, wbuf, wres, xn, xn_r, m, W)
                    p.emit("act", lambda e, pk=pk: e.copy(ev[:, :], pk[:, 0:W]),
                           reads=[pk_r], writes=[ev_r], tag="l1a_ev")
                    if which != "v":
                        scrt = qts if which == "q" else kts
                        dst = bass.AP(scrt.tensor, m * 128 * T + c * W,
                                      [[T, 128], [1, W]])
                        b.dma("sp", dst, ev[:, :], reads=[ev_r],
                              writes=[(f"{which}T_scr", (m, c))], tag="l1a_st")
                    else:
                        for hh_i in range(2):
                            for gg in range(W // 128):
                                h_idx = 2 * m + hh_i
                                t0 = c * W + gg * 128
                                p.emit("pe", lambda e, hh_i=hh_i, gg=gg:
                                       bf16_transpose(
                                           e, b.nc, b.pbt[:, 0:64],
                                           ev[64 * hh_i:64 * (hh_i + 1),
                                              gg * 128:(gg + 1) * 128],
                                           eyeb[64 * hh_i:64 * hh_i + 64,
                                                64 * hh_i:64 * hh_i + 64]),
                                       reads=[ev_r, "c_eye_s"],
                                       writes=[b.pbt_res], tag="l1a_vT")
                                p.emit("act", lambda e: e.copy(
                                    vntc[:, :], b.pbt[:, 0:64]),
                                    reads=[b.pbt_res], writes=[vntc_r],
                                    tag="l1a_vTe")
                                dst = bass.AP(vns.tensor,
                                              (h_idx * T + t0) * 64,
                                              [[64, 128], [1, 64]])
                                b.dma("sp", dst, vntc[:, :], reads=[vntc_r],
                                      writes=[("vn_scr", (h_idx, c, gg))],
                                      tag="l1a_vno")
        sweep1.__exit__(None, None, None)
        kaug = b.sb("kaug", [65, T], BF16)
        qaug = b.sb("qaug", [65, T], BF16)
        kaug_r, qaug_r = b.rn(kaug), b.rn(qaug)
        vab = b.sb("vab", [128, 32 * 65], BF16)
        vab_r = b.rn(vab)
        sqs = b.sb("sqs", [64, T], BF16)
        sqs_r = b.rn(sqs)
        arow = b.sb("arow", [1, T], BF16)
        arow_r = b.rn(arow)
        kmx = b.sb("kmx", [1, 8], F32)
        kmx_r = b.rn(kmx)
        expl = [b.sb("expl", [128, 256], BF16) for _ in range(2)]
        dreci = b.sb("dreci", [1, 128], BF16)
        dreci_r = b.rn(dreci)
        ctxe = b.sb("ctxe", [64, 128], BF16)
        ctxe_r = b.rn(ctxe)
        p.emit("pool", lambda e: e.memset(kaug[64:65, :], 1.0),
               writes=[(kaug_r, "ones")], tag="l1a_k1")
        p.emit("pool", lambda e: e.memset(
            ap3(vab, (32 * 65, 128), (65, 32), (1, 1), offset=64), 1.0),
            writes=[(vab_r, "ones")], tag="l1a_v1")
        pav, pav_r = b.pb[5], b.pb_res[5]
        for h in range(16):
            b.dma("sp", kaug[0:64, :],
                  bass.AP(kts.tensor, h * 64 * T, [[T, 64], [1, T]]),
                  reads=["kT_scr"], writes=[(kaug_r, "k")], tag=f"l1a_kl{h}")
            b.dma("sp", qaug[0:64, :],
                  bass.AP(qts.tensor, h * 64 * T, [[T, 64], [1, T]]),
                  reads=["qT_scr"], writes=[(qaug_r, "q")], tag=f"l1a_ql{h}")
            b.dma("sp", ap3(vab, (32 * 65, 128), (65, 32), (1, 64)),
                  bass.AP(vns.tensor, h * T * 64,
                          [[64, 128], [128 * 64, 32], [1, 64]]),
                  reads=["vn_scr"], writes=[(vab_r, "v")], tag=f"l1a_vl{h}")
            for aug, nm in ((kaug, "k"), (qaug, "q")):
                p.emit("act", lambda e, aug=aug, nm=nm: e.activation(
                    sqs[:, :], aug[0:64, :], AF.Square),
                    reads=[(b.rn(aug), nm)], writes=[sqs_r], tag="l1a_sq")
                for cc in range(8):
                    psq, psq_r = b.bank()
                    prs = [(ones[0:64, 0:1], sqs[0:64, cc * W:(cc + 1) * W],
                            [sqs_r, "c_ones_s"])]
                    b.mm_group(psq[0:1, 0:W], prs, psq_r, tag="l1a_qs")
                    p.emit("act", lambda e, psq=psq, cc=cc, nm=nm: e.activation(
                        arow[0:1, cc * W:(cc + 1) * W], psq[0:1, 0:W], AF.Sqrt),
                        reads=[psq_r], writes=[(arow_r, (nm, cc))], tag="l1a_sqrt")
                if nm == "k":
                    p.emit("dve", lambda e: e.reduce_max(
                        kmx[0:1, 0:1], arow[0:1, :], axis=AX.X),
                        reads=[arow_r], writes=[kmx_r], tag="l1a_kmx")
            p.emit("dve", lambda e: e.tensor_scalar(
                qaug[64:65, :], arow[0:1, :], kmx[0:1, 0:1], -1.0,
                OP.mult, OP.mult),
                reads=[arow_r, kmx_r], writes=[(qaug_r, "m")], tag="l1a_negM")
            for w in range(NW):
                ps, ps_r = b.bank()
                ex = expl[w % 2]
                ex_r = b.rn(ex)
                p.emit("pe", lambda e, ps=ps, w=w: e.matmul(
                    ps[:, 0:128], kaug[:, w * 128:(w + 1) * 128],
                    qaug[:, w * 128:(w + 1) * 128], start=True, stop=True),
                    reads=[(kaug_r, "k"), (kaug_r, "ones"),
                           (qaug_r, "q"), (qaug_r, "m")],
                    writes=[ps_r], tag="l1a_sl")
                p.emit("pe", lambda e, ps=ps, w=w: e.matmul(
                    ps[0:64, 128:256], kaug[:, 0:64],
                    qaug[:, w * 128:(w + 1) * 128], start=True, stop=True),
                    reads=[(kaug_r, "k"), (kaug_r, "ones"),
                           (qaug_r, "q"), (qaug_r, "m")],
                    writes=[ps_r], tag="l1a_sg")
                p.emit("act", lambda e, ps=ps, ex=ex: e.activation(
                    ex[:, 0:128], ps[:, 0:128], AF.Exp, scale=0.125),
                    reads=[ps_r], writes=[(ex_r, "l")], tag="l1a_expl")
                p.emit("act", lambda e, ps=ps, ex=ex: e.activation(
                    ex[0:64, 128:256], ps[0:64, 128:256], AF.Exp, scale=0.125),
                    reads=[ps_r], writes=[(ex_r, "g")], tag="l1a_expg")
                p.emit("pe", lambda e, ex=ex, w=w: e.matmul(
                    pav[0:65, 0:128], vab[:, w * 65:(w + 1) * 65],
                    ex[:, 0:128], start=True, stop=False),
                    reads=[(ex_r, "l"), (vab_r, "v"), (vab_r, "ones")],
                    writes=[pav_r], tag="l1a_avl")
                p.emit("pe", lambda e, ex=ex: e.matmul(
                    pav[0:65, 0:128], vab[0:64, 0:65],
                    ex[0:64, 128:256], start=False, stop=True),
                    reads=[(ex_r, "g"), (vab_r, "v"), (vab_r, "ones")],
                    writes=[pav_r], tag="l1a_avg")
                p.emit("dve", lambda e: lp(b.nc, lambda: e.reciprocal(
                    dreci[0:1, :], pav[64:65, 0:128])),
                       reads=[pav_r], writes=[dreci_r], tag="l1a_rec")
                pbc, pbc_r = b.bank()
                p.emit("pe", lambda e, pbc=pbc: e.matmul(
                    pbc[0:64, 0:128], onesrow[0:1, 0:64], dreci[0:1, :],
                    start=True, stop=True),
                    reads=[dreci_r, "c_onesrow"], writes=[pbc_r], tag="l1a_rb")
                p.emit("act", lambda e: e.copy(ctxe[:, :], pav[0:64, 0:128]),
                       reads=[pav_r], writes=[ctxe_r], tag="l1a_ctx")
                co = (h // 2) * T + w * 128
                cr0 = 64 * (h % 2)
                p.emit("dve", lambda e, pbc=pbc, co=co, cr0=cr0: e.tensor_tensor(
                    conc[cr0:cr0 + 64, co:co + 128], ctxe[:, :],
                    pbc[0:64, 0:128], OP.mult),
                    reads=[ctxe_r, pbc_r], writes=[(conc_r, (h, w))],
                    tag="l1a_nrm")
        for c in range(NCH):
            b.dma("sp", rx[:, :], resid_chunk_ap(resid_in, c, W),
                  reads=[(rin_res, c)], writes=[rx_r], tag=f"l1a_2rx{c}")
            for m in range(8):
                po, po_r = proj_group(b, wo, "l1a_wo", conc, conc_r, m, W,
                                      rhs_stride=T, rhs_off=c * W)
                p.emit("dve", lambda e, po=po, m=m: e.tensor_tensor(
                    rx[:, m * W:(m + 1) * W], rx[:, m * W:(m + 1) * W],
                    po[:, 0:W], OP.add),
                    reads=[po_r, rx_r], writes=[(rx_r, ("y", m))],
                    tag="l1a_add")
            b.dma("sp", resid_chunk_ap(resid_out, c, W), rx[:, :],
                  reads=[rx_r], writes=[(rout_res, c)], tag=f"l1a_out{c}")



def emit_l1ffn(b, resid_in, rin_res, resid_out, rout_res, first):
    p = b.p
    TC = 1024
    NCH = T // TC
    NE = 9
    I2 = 2048
    cache_g = b.dram_tmp("l1c_wg", [NE * D, I2], BF16)
    cache_u = b.dram_tmp("l1c_wu", [NE * D, I2], BF16)
    cache_d = b.dram_tmp("l1c_wd", [NE * I2, D], BF16)

    with b.scope():
        wrT = b.sb("l1_wrT", [128, 8 * 8], BF16)
        load_w2mb(b, "L1f_WrT", wrT, "l1_wrT", kdim=8, mdim=8)
        rb = b.sb("l1_rb", [8, 1], F32)
        b.dma("sp", rb[:, :], b.dram["L1f_rb"], writes=["l1_rb"])
        einds = b.sb("l1_einds", [8, 8 * 128], BF16)
        b.dma("sp", einds[:, :], b.dram["c_einds"], writes=["c_einds_s"])

        scr = make_ln_scr(b, 256)
        ones = scr["ones"]
        rx = b.sb("rx", [128, 8 * TC], F32)
        rx_r = b.rn(rx)
        xn = b.sb("xn", [128, 8 * TC], BF16)
        xn_r = b.rn(xn)
        wg = b.sb("wg", [128, 8 * I2], BF16)
        wu = b.sb("wu", [128, 8 * I2], BF16)
        wd = b.sb("wd", [128, 16 * D], BF16)
        wg_r, wu_r, wd_r = b.rn(wg), b.rn(wu), b.rn(wd)
        hh = b.sb("hh", [128, 16 * 512], BF16)
        hh_r = b.rn(hh)
        xe = b.sb("xe", [128, 8 * 512], BF16)
        xe_r = b.rn(xe)
        t1 = b.sb("t1", [128, 512], BF16)
        t1_r = b.rn(t1)
        probs = b.sb("probs", [8, TC], F32)
        pr_r = b.rn(probs)
        eyef = b.sb("l1_eyef", [128, 128], F32)
        b.dma("sp", eyef[:, :], b.dram["c_eyef"], writes=["c_eyef_s"])
        ptS = b.sb("ptS", [128, 64], F32)
        ptS_r = b.rn(ptS)
        m12 = b.sb("m12", [128, 16], F32)
        m12_r = b.rn(m12)
        msk = b.sb("msk", [128, 64], F32)
        msk_r = b.rn(msk)
        cwt = b.sb("cwt", [128, 64], F32)
        cwt_r = b.rn(cwt)
        cwbf = b.sb("cwbf", [8, TC], BF16)
        cw_r = b.rn(cwbf)

        def stream_we(which, e, c, buf, buf_res):
            cach, src, kdim, mdim = {
                "g": (cache_g, "L1f_WgT", 8, I2),
                "u": (cache_u, "L1f_WuT", 8, I2),
                "d": (cache_d, "L1f_WdT", 16, D)}[which]
            nb = kdim * 128
            cap_ = bass.AP(cach.tensor, e * nb * mdim,
                           [[kdim * mdim, 128], [1, kdim * mdim]])
            if first and c == 0:
                sap = bass.AP(b.dram[src].tensor, e * nb * mdim,
                              [[mdim, 128], [128 * mdim, kdim], [1, mdim]])
                b.dma("pool", buf[:, :], sap, writes=[buf_res],
                      tag=f"l1w{which}{e}")
                b.dma("sp", cap_, buf[:, :], reads=[buf_res],
                      writes=[(f"l1c_{which}", e)], tag=f"l1cw{which}{e}")
            else:
                b.dma("sp", buf[:, :], cap_, reads=[(f"l1c_{which}", e)],
                      writes=[buf_res], tag=f"l1r{which}{e}")

        for c in range(NCH):
            b.dma("sp", rx[:, :], resid_chunk_ap(resid_in, c, TC),
                  reads=[(rin_res, c)], writes=[rx_r], tag=f"l1_rx{c}")
            for s in range(4):
                xn_ap = ap3(xn, (8 * TC, 128), (TC, 8), (1, 256), offset=s * 256)
                emit_ln(b, rx, rx_r, xn_ap, (xn_r, s), 256, scr, ones,
                        rx_fw=8 * TC, rx_stride=TC, rx_off=s * 256)
            for hf in range(2):
                plog, plog_r = b.bank()
                pairs = [(bass.AP(wrT, k * 8, [[8 * 8, 128], [1, 8]]),
                          bass.AP(xn, k * TC + hf * 512, [[8 * TC, 128], [1, 512]]),
                          ["l1_wrT", xn_r]) for k in range(8)]
                b.mm_group(plog[0:8, 0:512], pairs, plog_r, tag="l1_logit")
                p.emit("act", lambda e, plog=plog, hf=hf: e.activation(
                    probs[0:8, hf * 512:(hf + 1) * 512], plog[0:8, 0:512],
                    AF.Sigmoid, bias=rb[0:8, 0:1]),
                    reads=[plog_r, "l1_rb"], writes=[(pr_r, hf)], tag="l1_probs")
            for ggi in range(8):
                ptp, ptp_r = b.bank()
                p.emit("pe", lambda e, ptp=ptp, ggi=ggi: e.transpose(
                    ptp[:, 0:8], probs[0:8, ggi * 128:(ggi + 1) * 128],
                    eyef[0:8, 0:8]),
                    reads=[pr_r, "c_eyef_s"], writes=[ptp_r], tag="l1_ptT")
                p.emit("act", lambda e, ptp=ptp, ggi=ggi: e.copy(
                    ptS[:, ggi * 8:(ggi + 1) * 8], ptp[:, 0:8]),
                    reads=[ptp_r], writes=[(ptS_r, ggi)], tag="l1_ptE")
            pt3 = ap3(ptS, (64, 128), (8, 8), (1, 8))
            msk3 = ap3(msk, (64, 128), (8, 8), (1, 8))
            cwt3 = ap3(cwt, (64, 128), (8, 8), (1, 8))
            m1e = ap3(m12, (16, 128), (1, 8), (0, 8))
            m2e = ap3(m12, (16, 128), (1, 8), (0, 8), offset=8)
            p.emit("dve", lambda e: e.reduce_max(
                m12[:, 0:8], pt3, axis=AX.X),
                reads=[ptS_r], writes=[(m12_r, 1)], tag="l1_m1")
            p.emit("dve", lambda e: e.tensor_tensor(msk3, pt3, m1e, OP.is_ge),
                   reads=[ptS_r, (m12_r, 1)], writes=[msk_r], tag="l1_msk")
            p.emit("dve", lambda e: e.scalar_tensor_tensor(
                msk3, msk3, -1e30, pt3, OP.mult, OP.add),
                reads=[msk_r, ptS_r], writes=[msk_r], tag="l1_t2")
            p.emit("dve", lambda e: e.reduce_max(
                m12[:, 8:16], msk3, axis=AX.X),
                reads=[msk_r], writes=[(m12_r, 2)], tag="l1_m2")
            p.emit("dve", lambda e: e.tensor_tensor(msk3, pt3, m2e, OP.is_ge),
                   reads=[ptS_r, (m12_r, 2)], writes=[msk_r], tag="l1_kp")
            p.emit("dve", lambda e: e.tensor_tensor(cwt3, pt3, msk3, OP.mult),
                   reads=[ptS_r, msk_r], writes=[cwt_r], tag="l1_cwt")
            for ggi in range(8):
                pcb, pcb_r = b.bank()
                p.emit("pe", lambda e, pcb=pcb, ggi=ggi: e.transpose(
                    pcb[0:8, 0:128], cwt[:, ggi * 8:(ggi + 1) * 8], eyef[:, :]),
                    reads=[cwt_r, "c_eyef_s"], writes=[pcb_r], tag="l1_cwT")
                p.emit("act", lambda e, pcb=pcb, ggi=ggi: e.copy(
                    cwbf[0:8, ggi * 128:(ggi + 1) * 128], pcb[0:8, 0:128]),
                    reads=[pcb_r], writes=[(cw_r, ggi)], tag="l1_cwE")
            for e_i in range(NE):
                stream_we("g", e_i, c, wg, wg_r)
                stream_we("u", e_i, c, wu, wu_r)
                stream_we("d", e_i, c, wd, wd_r)
                for hf in range(2):
                    if e_i == 0:
                        up_rhs, up_rhs_r, up_stride = xn, xn_r, TC
                        up_off = hf * 512
                    else:
                        pcw, pcw_r = b.bank()
                        p.emit("pe", lambda e, pcw=pcw, e_i=e_i, hf=hf: e.matmul(
                            pcw[:, 0:512],
                            einds[0:8, (e_i - 1) * 128:e_i * 128],
                            cwbf[0:8, hf * 512:(hf + 1) * 512],
                            start=True, stop=True),
                            reads=[cw_r, "c_einds_s"], writes=[pcw_r], tag="l1_cwb")
                        src3 = ap3(xn, (8 * TC, 128), (TC, 8), (1, 512),
                                   offset=hf * 512)
                        xe3 = ap3(xe, (8 * 512, 128), (512, 8), (1, 512))
                        cw3 = ap3(pcw, (512, 128), (0, 8), (1, 512))
                        p.emit("dve", lambda e, xe3=xe3, src3=src3, cw3=cw3:
                               e.tensor_tensor(xe3, src3, cw3, OP.mult),
                               reads=[xn_r, pcw_r], writes=[xe_r], tag="l1_xe")
                        up_rhs, up_rhs_r, up_stride = xe, xe_r, 512
                        up_off = 0
                    for it in range(16):
                        pg, pg_r = b.bank()
                        pairs = [(bass.AP(wg, k * I2 + it * 128,
                                          [[8 * I2, 128], [1, 128]]),
                                  bass.AP(xn, k * TC + hf * 512,
                                          [[8 * TC, 128], [1, 512]]),
                                  [wg_r, xn_r]) for k in range(8)]
                        b.mm_group(pg[:, 0:512], pairs, pg_r, tag="l1_g")
                        p.emit("act", lambda e, pg=pg: e.activation(
                            t1[:, :], pg[:, 0:512], AF.Silu),
                            reads=[pg_r], writes=[t1_r], tag="l1_silu")
                        pu, pu_r = b.bank()
                        pairs = [(bass.AP(wu, k * I2 + it * 128,
                                          [[8 * I2, 128], [1, 128]]),
                                  bass.AP(up_rhs, k * up_stride + up_off,
                                          [[8 * up_stride, 128], [1, 512]]),
                                  [wu_r, up_rhs_r]) for k in range(8)]
                        b.mm_group(pu[:, 0:512], pairs, pu_r, tag="l1_u")
                        p.emit("dve", lambda e, pu=pu, it=it: e.tensor_tensor(
                            hh[:, it * 512:(it + 1) * 512], pu[:, 0:512],
                            t1[:, :], OP.mult),
                            reads=[pu_r, t1_r], writes=[(hh_r, it)], tag="l1_h")
                    for m in range(8):
                        pd, pd_r = b.bank()
                        pairs = [(bass.AP(wd, k * D + m * 128,
                                          [[16 * D, 128], [1, 128]]),
                                  bass.AP(hh, k * 512, [[16 * 512, 128], [1, 512]]),
                                  [wd_r, hh_r]) for k in range(16)]
                        b.mm_group(pd[:, 0:512], pairs, pd_r, tag="l1_d")
                        p.emit("dve", lambda e, pd=pd, m=m, hf=hf:
                               e.scalar_tensor_tensor(
                                   rx[:, m * TC + hf * 512:m * TC + (hf + 1) * 512],
                                   pd[:, 0:512], 0.5,
                                   rx[:, m * TC + hf * 512:m * TC + (hf + 1) * 512],
                                   OP.mult, OP.add),
                               reads=[pd_r, rx_r],
                               writes=[(rx_r, ("y", e_i, hf, m))], tag="l1_add")
            b.dma("sp", resid_chunk_ap(resid_out, c, TC), rx[:, :],
                  reads=[rx_r], writes=[(rout_res, c)], tag=f"l1_out{c}")



def emit_l2ffn(b, resid_in, rin_res, resid_out, rout_res, dbg=None):
    nc, p = b.nc, b.p
    W = 512
    NCH = T // W
    with b.scope():
        wr = b.sb("l2f_wr", [128, 8 * 1024], BF16)
        wk = b.sb("l2f_wk", [128, 8 * 1024], BF16)
        wv = b.sb("l2f_wv", [128, 8 * 1024], BF16)
        mur = b.sb("l2f_mur", [128, 8], F32)
        muk = b.sb("l2f_muk", [128, 8], F32)
        for nm, buf in (("L2f_WrT", wr), ("L2f_WkT", wk), ("L2f_WvT", wv)):
            src = bass.AP(b.dram[nm].tensor, 0,
                          [[1024, 128], [128 * 1024, 8], [1, 1024]])
            b.dma("pool", buf[:, :], src, writes=[b.rn(buf)], tag="l2f_w")
        b.dma("sp", mur[:, :], b.dram["L2f_mur"], writes=[b.rn(mur)])
        b.dma("sp", muk[:, :], b.dram["L2f_muk"], writes=[b.rn(muk)])
        wr_r, wk_r, wv_r = b.rn(wr), b.rn(wk), b.rn(wv)
        mur_r, muk_r = b.rn(mur), b.rn(muk)

        ones = b.sb("ones", [128, 1], BF16)
        p.emit("pool", lambda e: e.memset(ones[:, :], 1.0), writes=["c_ones_s"])
        onesrow = b.sb("onesrow", [1, 128], BF16)
        p.emit("pool", lambda e: e.memset(onesrow[:, :], 1.0), writes=["c_onesrow"])

        rx = [b.sb("rx", [128, 8 * W], F32) for _ in range(2)]
        xnb = [b.sb("xnb", [128, 8 * (W + 1)], BF16) for _ in range(2)]
        LNW = 256
        scr = {
            "rxbf": b.sb("rxbf", [128, 8 * LNW], BF16),
            "rsq": b.sb("rsq", [128, 8 * LNW], BF16),
            "ln_xc": b.sb("ln_xc", [128, 8 * LNW], F32),
            "rows": b.sb("rows", [1, 4 * LNW], F32),
            "rows_bf": b.sb("rows_bf", [1, 2 * LNW], BF16),
            "ln_tmp": b.sb("ln_tmp", [128, 8 * LNW], BF16),
            "pS1": b.pb[4], "pS2": b.pb[5], "pAB": b.pb[6],
            "pS1_r": b.pb_res[4], "pS2_r": b.pb_res[5], "pAB_r": b.pb_res[6],
            "onesrow": onesrow,
        }
        delta = b.sb("delta", [128, 8 * W], BF16)
        xr_m = b.sb("xr_m", [128, 8 * W], BF16)
        xk_m = b.sb("xk_m", [128, 8 * W], BF16)
        rr = b.sb("rr", [128, 8 * W], BF16)
        kr = b.sb("kr", [128, 8 * W], BF16)
        ka = kr
        mu_bc_r = b.sb("mu_bc_r", [128, 8 * W], BF16)
        mu_bc_k = b.sb("mu_bc_k", [128, 8 * W], BF16)
        p.emit("dve", lambda e: e.tensor_copy(
            ap3(mu_bc_r, (8 * W, 128), (W, 8), (1, W)),
            ap3(mur, (8, 128), (1, 8), (0, W))),
            reads=[mur_r], writes=[b.rn(mu_bc_r)])
        p.emit("dve", lambda e: e.tensor_copy(
            ap3(mu_bc_k, (8 * W, 128), (W, 8), (1, W)),
            ap3(muk, (8, 128), (1, 8), (0, W))),
            reads=[muk_r], writes=[b.rn(mu_bc_k)])

        for c in range(NCH):
            rxc = rx[c % 2]
            ryc = rxc
            xc = xnb[c % 2]
            xp = xnb[(c + 1) % 2]
            rxc_r, ryc_r, xc_r = b.rn(rxc), b.rn(rxc), b.rn(xc)
            xp_r = b.rn(xp)
            src = bass.AP(resid_in.tensor, c * W,
                          [[T, 128], [128 * T, 8], [1, W]])
            b.dma("sp", rxc[:, :], src, reads=[(rin_res, c)],
                  writes=[rxc_r], tag=f"l2f_rx{c}")
            for sct in range(W // 256):
                xn_ap = ap3(xc, (8 * (W + 1), 128), (W + 1, 8), (1, 256),
                            offset=1 + sct * 256)
                emit_ln(b, rxc, rxc_r, xn_ap, (xc_r, ("s", sct)), 256, scr, ones,
                        rx_fw=8 * W, rx_stride=W, rx_off=sct * 256)
            if c == 0:
                p.emit("dve", lambda e, xc=xc: e.memset(
                    ap3(xc, (8 * (W + 1), 128), (W + 1, 8), (1, 1)), 0.0),
                    writes=[(xc_r, "b")], tag="l2f_zb")
            else:
                p.emit("dve", lambda e, xc=xc, xp=xp: e.tensor_copy(
                    ap3(xc, (8 * (W + 1), 128), (W + 1, 8), (1, 1)),
                    ap3(xp, (8 * (W + 1), 128), (W + 1, 8), (1, 1), offset=W)),
                    reads=[xp_r], writes=[(xc_r, "b")], tag="l2f_cb")
            cur = ap3(xc, (8 * (W + 1), 128), (W + 1, 8), (1, W), offset=1)
            sh = ap3(xc, (8 * (W + 1), 128), (W + 1, 8), (1, W), offset=0)
            d3 = ap3(delta, (8 * W, 128), (W, 8), (1, W))
            p.emit("dve", lambda e, d3=d3, cur=cur, sh=sh:
                   e.tensor_tensor(d3, cur, sh, OP.subtract),
                   reads=[xc_r, (xc_r, "b")], writes=[b.rn(delta)], tag="l2f_d")
            t3r = ap3(xr_m, (8 * W, 128), (W, 8), (1, W))
            t3k = ap3(xk_m, (8 * W, 128), (W, 8), (1, W))
            m3r = ap3(mu_bc_r, (8 * W, 128), (W, 8), (1, W))
            m3k = ap3(mu_bc_k, (8 * W, 128), (W, 8), (1, W))
            p.emit("dve", lambda e, t3r=t3r, d3=d3, m3r=m3r:
                   e.tensor_tensor(t3r, d3, m3r, OP.mult),
                   reads=[b.rn(delta), b.rn(mu_bc_r)], writes=[b.rn(xr_m)])
            p.emit("dve", lambda e, t3r=t3r, sh=sh: e.tensor_tensor(t3r, t3r, sh, OP.add),
                   reads=[b.rn(xr_m), xc_r, (xc_r, "b")], writes=[b.rn(xr_m)])
            p.emit("dve", lambda e, t3k=t3k, d3=d3, m3k=m3k:
                   e.tensor_tensor(t3k, d3, m3k, OP.mult),
                   reads=[b.rn(delta), b.rn(mu_bc_k)], writes=[b.rn(xk_m)])
            p.emit("dve", lambda e, t3k=t3k, sh=sh: e.tensor_tensor(t3k, t3k, sh, OP.add),
                   reads=[b.rn(xk_m), xc_r, (xc_r, "b")], writes=[b.rn(xk_m)])
            for m in range(8):
                pbk, pbk_r = b.bank()
                pairs = [(bass.AP(wr, k * 1024 + m * 128, [[8 * 1024, 128], [1, 128]]),
                          bass.AP(xr_m, k * W, [[8 * W, 128], [1, W]]),
                          [wr_r, b.rn(xr_m)]) for k in range(8)]
                b.mm_group(pbk[:, 0:W], pairs, pbk_r, tag="l2f_r")
                p.emit("act", lambda e, pbk=pbk, m=m: e.activation(
                    rr[:, m * W:(m + 1) * W], pbk[:, 0:W], AF.Sigmoid),
                    reads=[pbk_r], writes=[(b.rn(rr), m)], tag="l2f_sig")
            for m in range(8):
                pbk, pbk_r = b.bank()
                pairs = [(bass.AP(wk, k * 1024 + m * 128, [[8 * 1024, 128], [1, 128]]),
                          bass.AP(xk_m, k * W, [[8 * W, 128], [1, W]]),
                          [wk_r, b.rn(xk_m)]) for k in range(8)]
                b.mm_group(pbk[:, 0:W], pairs, pbk_r, tag="l2f_k")
                p.emit("act", lambda e, pbk=pbk, m=m: e.activation(
                    kr[:, m * W:(m + 1) * W], pbk[:, 0:W], AF.Relu),
                    reads=[pbk_r], writes=[(b.rn(kr), m)], tag="l2f_relu")
            p.emit("dve", lambda e: e.tensor_tensor(ka[:, :], kr[:, :], kr[:, :],
                                                    OP.mult),
                   reads=[b.rn(kr), (b.rn(kr), 0), (b.rn(kr), 1), (b.rn(kr), 2),
                          (b.rn(kr), 3), (b.rn(kr), 4), (b.rn(kr), 5),
                          (b.rn(kr), 6), (b.rn(kr), 7)],
                   writes=[b.rn(ka)], tag="l2f_sq")
            for m in range(8):
                pbk, pbk_r = b.bank()
                pairs = [(bass.AP(wv, k * 1024 + m * 128, [[8 * 1024, 128], [1, 128]]),
                          bass.AP(ka, k * W, [[8 * W, 128], [1, W]]),
                          [wv_r, b.rn(ka)]) for k in range(8)]
                b.mm_group(pbk[:, 0:W], pairs, pbk_r, tag="l2f_v")
                tbuf = delta
                p.emit("dve", lambda e, pbk=pbk, m=m, tbuf=tbuf: e.scalar_tensor_tensor(
                    tbuf[:, m * W:(m + 1) * W], pbk[:, 0:W], 0.5,
                    rr[:, m * W:(m + 1) * W], OP.mult, OP.mult),
                    reads=[pbk_r, (b.rn(rr), m), b.rn(delta)],
                    writes=[(b.rn(delta), ("y", m))], tag="l2f_g")
                p.emit("dve", lambda e, m=m, ryc=ryc, rxc=rxc, tbuf=tbuf: e.tensor_tensor(
                    ryc[:, m * W:(m + 1) * W], rxc[:, m * W:(m + 1) * W],
                    tbuf[:, m * W:(m + 1) * W], OP.add),
                    reads=[rxc_r, (b.rn(delta), ("y", m))],
                    writes=[(ryc_r, ("y", m))], tag="l2f_add")
            if dbg is not None:
                dbg1, dbg2, dbg3 = dbg
                ddst = bass.AP(dbg1.tensor, c * W, [[T, 128], [128 * T, 8], [1, W]])
                xn_src = ap3(xc, (8 * (W + 1), 128), (W + 1, 8), (1, W), offset=1)
                b.dma("sp", ddst, xn_src, reads=[xc_r], writes=[("dbg", c)],
                      tag=f"dbg{c}")
                d2 = bass.AP(dbg2.tensor, c * W, [[T, 128], [128 * T, 8], [1, W]])
                b.dma("sp", d2, xr_m[:, :], reads=[b.rn(xr_m)],
                      writes=[("dbg2", c)], tag=f"dbg2_{c}")
                dd3 = bass.AP(dbg3.tensor, c * W, [[T, 128], [128 * T, 8], [1, W]])
                b.dma("sp", dd3, rr[:, :], reads=[b.rn(rr)],
                      writes=[("dbg3", c)], tag=f"dbg3_{c}")
            dst = bass.AP(resid_out.tensor, c * W,
                          [[T, 128], [128 * T, 8], [1, W]])
            b.dma("sp", dst, ryc[:, :],
                  reads=[ryc_r],
                  writes=[(rout_res, c)], tag=f"l2f_out{c}")



PASSES = ["l0ffn", "l0attn", "l0ffn",
          "l1ffn", "l1attn", "l1ffn",
          "l2ffn", "l2attn", "l2ffn",
          "l3ffn", "l3attn", "l3ffn"]
MODE = "fused"


def build_program(passes=PASSES):
    nc = bass.Bass("TRN2", target_bir_lowering=False)
    b = B(nc)
    xT = b.dram_in("xT", [D, T], F32)
    out = b.dram_out("out", [D, T], F32)
    if "l0ffn" in passes:
        b.dram_in("L0f_wg", [2816, 1024]); b.dram_in("L0f_wu", [2816, 1024])
        b.dram_in("L0f_wd", [1024, 2816])
        b.dram_in("c_eye", [128, 128], BF16)
    if "l1attn" in passes:
        for nm in ("L1a_WqT", "L1a_WkT", "L1a_WvT", "L1a_WoT"):
            b.dram_in(nm, [D, D])
        b.dram_in("c_eye", [128, 128], BF16)
    if "l3attn" in passes:
        for nm in ("L3a_WqT", "L3a_WkT", "L3a_WvT", "L3a_WoT"):
            b.dram_in(nm, [D, D])
        b.dram_in("L3a_bias", [128, 32])
        b.dram_in("c_eye", [128, 128], BF16)
        b.dram_in("c_l3mask", [128, 4 * 512], BF16)
    if "l1ffn" in passes:
        b.dram_in("L1f_WgT", [9 * 1024, 2048]); b.dram_in("L1f_WuT", [9 * 1024, 2048])
        b.dram_in("L1f_WdT", [9 * 2048, 1024])
        b.dram_in("L1f_WrT", [1024, 8]); b.dram_in("L1f_rb", [8, 1])
        b.dram_in("c_einds", [8, 8 * 128], BF16)
        b.dram_in("c_eyef", [128, 128])
        b.dram_in("c_eye", [128, 128], BF16)
        b.dram_in("c_tok128", [128, 32 * 4], mybir.dt.uint32)
        b.dram_in("c_padidx", [8, 2048 * 4], mybir.dt.uint32)
    if "l3ffn" in passes:
        b.dram_in("L3f_w1T", [D, 4096]); b.dram_in("L3f_w2T", [D, 4096])
        b.dram_in("L3f_h1T", [4096, 4096]); b.dram_in("L3f_h2T", [4096, 4096])
        b.dram_in("L3f_w3T", [4096, D])
        b.dram_in("L3f_sc", [128, 128]); b.dram_in("L3f_s3", [128, 8])
    if "l0attn" in passes:
        b.dram_in("L0a_WaT", [D, 256]); b.dram_in("L0a_Wab", [128, 2])
        b.dram_in("L0a_BT", [D, 256]); b.dram_in("L0a_CT", [256, D])
        b.dram_in("L0a_DT", [D, D])
    if "l2attn" in passes:
        for nm in ("L2a_WgT", "L2a_WrT", "L2a_WkT", "L2a_WvT", "L2a_WoT"):
            b.dram_in(nm, [D, D])
        b.dram_in("L2a_vecs", [128, 88])
        b.dram_in("c_ind", [128, 128], BF16)
        b.dram_in("c_indT", [16, 1024], BF16)
    if "l2ffn" in passes:
        b.dram_in("L2f_WrT", [D, D]); b.dram_in("L2f_WkT", [D, D])
        b.dram_in("L2f_WvT", [D, D])
        b.dram_in("L2f_mur", [128, 8]); b.dram_in("L2f_muk", [128, 8])

    resid = b.dram_tmp("resid", [D, T], F32)
    chain = []
    for i, name in enumerate(passes):
        rin = xT if i == 0 else resid
        rin_res = "xT" if i == 0 else "resid"
        rout = out if i == len(passes) - 1 else resid
        rout_res = "out" if i == len(passes) - 1 else "resid"
        chain.append((name, rin, rin_res, rout, rout_res))
    seen = {}
    import os
    dbg = None
    if os.environ.get("K_DBG"):
        dbg = (b.dram_out("dbg", [D, T], BF16), b.dram_out("dbg2", [D, T], BF16),
               b.dram_out("dbg3", [D, T], BF16))
    for name, rin, rin_res, rout, rout_res in chain:
        if name == "l2ffn":
            emit_l2ffn(b, rin, rin_res, rout, rout_res, dbg=dbg)
        elif name == "l2attn":
            emit_l2attn(b, rin, rin_res, rout, rout_res)
        elif name == "l0attn":
            emit_l0attn(b, rin, rin_res, rout, rout_res)
        elif name == "l3ffn":
            emit_l3ffn(b, rin, rin_res, rout, rout_res,
                       first=(seen.setdefault("l3ffn", 0) == 0))
        elif name == "l1ffn":
            emit_l1ffn(b, rin, rin_res, rout, rout_res,
                       first=(seen.setdefault("l1ffn", 0) == 0))
        elif name == "l3attn":
            emit_l3attn(b, rin, rin_res, rout, rout_res)
        elif name == "l1attn":
            emit_l1attn(b, rin, rin_res, rout, rout_res)
        elif name == "l0ffn":
            emit_l0ffn(b, rin, rin_res, rout, rout_res,
                       first=(seen.setdefault("l0ffn", 0) == 0))
        if name in seen:
            seen[name] += 1
    b.p.finish()
    return nc, b


def kernel(**inputs):
    g = prep_shared(inputs)
    x = np.asarray(inputs["L0_attn_x"])
    cores = list(range(NCORES))
    if MODE == "fused":
        nc, b = build_program(PASSES)
        in_maps = []
        for core in cores:
            m = {k: v for k, v in g.items() if k in b.input_names}
            m["xT"] = np.ascontiguousarray(x[core].T)
            in_maps.append(m)
        res = run_bass_kernel_spmd(nc, in_maps, core_ids=cores)
        b.stack.close()
        outs = [np.ascontiguousarray(r["out"].T) for r in res.results]
        return np.stack(outs, axis=0)
    progs = {}
    resid = [np.ascontiguousarray(x[core].T) for core in cores]
    for name in PASSES:
        if name not in progs:
            progs[name] = build_program([name])
        nc, b = progs[name]
        in_maps = []
        for core in cores:
            m = {k: v for k, v in g.items() if k in b.input_names}
            m["xT"] = resid[core]
            in_maps.append(m)
        res = run_bass_kernel_spmd(nc, in_maps, core_ids=cores)
        resid = [np.ascontiguousarray(r["out"]) for r in res.results]
    for nc, b in progs.values():
        b.stack.close()
    return np.stack([r.T for r in resid], axis=0)
